# version 01
# baseline (speedup 1.0000x reference)
import numpy as np
import ml_dtypes
import concourse.bass as bass
import concourse.mybir as mybir
from concourse.bass_utils import run_bass_kernel_spmd

F32 = mybir.dt.float32
BF16 = mybir.dt.bfloat16
I32 = mybir.dt.int32
AF = mybir.ActivationFunctionType
OP = mybir.AluOpType
AX = mybir.AxisListType

D = 1024
T = 2048
NC = 8
TG = 512
NTG = T // TG
HEADS = 16
NEGBIG = -1073741824.0

_N_SUB = 6
_RUN = None



def _t(a):
    return np.ascontiguousarray(np.asarray(a, np.float32).T)


def _cols(v, n):
    return np.ascontiguousarray(np.asarray(v, np.float32).reshape(n, 128).T)


def _consts():
    c = {}
    c["ident_f"] = np.eye(128, dtype=np.float32)
    itop = np.zeros((128, 128), dtype=ml_dtypes.bfloat16)
    ibot = np.zeros((128, 128), dtype=ml_dtypes.bfloat16)
    for q in range(64):
        itop[q, q] = 1
        itop[64 + q, q] = 1
        ibot[q, 64 + q] = 1
        ibot[64 + q, 64 + q] = 1
    c["itop"] = itop
    c["ibot"] = ibot
    m1 = np.where(np.arange(128)[None, :] > np.arange(128)[:, None], NEGBIG, 0.0)
    m1 = m1.astype(ml_dtypes.bfloat16)
    c["m1a"] = np.concatenate([m1[0:64], m1[0:64]], axis=0)
    c["m1b"] = np.concatenate([m1[64:128], m1[64:128]], axis=0)
    m2 = np.where(np.arange(128)[None, :] < np.arange(128)[:, None], NEGBIG, 0.0)
    m2 = m2.astype(ml_dtypes.bfloat16)
    c["m2a"] = np.concatenate([m2[0:64], m2[0:64]], axis=0)
    c["m2b"] = np.concatenate([m2[64:128], m2[64:128]], axis=0)
    r064 = np.zeros((128, 128), np.float32)
    r064[0, :] = 1.0
    r064[64, :] = 1.0
    c["r064f"] = r064
    c["r064b"] = r064.astype(ml_dtypes.bfloat16)
    c["invd"] = np.full((128, 1), 1.0 / D, np.float32)
    c["eps5"] = np.full((128, 1), 1e-5, np.float32)
    c["eps6"] = np.full((128, 1), 1e-6, np.float32)
    c["onescol"] = np.ones((128, 1), np.float32)
    c["onesrow"] = np.ones((1, 512), np.float32)
    c["iota8"] = np.broadcast_to(np.arange(8, dtype=np.float32), (128, 8)).copy()
    c["c999"] = np.full((128, 8), 999.0, np.float32)
    c["cneg"] = np.full((128, 8), -1e30, np.float32)
    sel8 = np.zeros((8, 8 * 128), np.float32)
    for e in range(8):
        sel8[e, e * 128:(e + 1) * 128] = 1.0
    c["sel8"] = sel8
    c["sel8b"] = sel8.astype(ml_dtypes.bfloat16)
    return c


CONST_SPECS = [
    ("ident_f", [128, 128], F32), ("itop", [128, 128], BF16), ("ibot", [128, 128], BF16),
    ("m1a", [128, 128], BF16), ("m1b", [128, 128], BF16),
    ("m2a", [128, 128], BF16), ("m2b", [128, 128], BF16),
    ("r064f", [128, 128], F32), ("r064b", [128, 128], BF16),
    ("invd", [128, 1], F32), ("eps5", [128, 1], F32), ("eps6", [128, 1], F32),
    ("onescol", [128, 1], F32), ("onesrow", [1, 512], F32),
    ("iota8", [128, 8], F32), ("c999", [128, 8], F32), ("cneg", [128, 8], F32),
]

GEN_MATS = (("ca", 1024, 3072), ("cp", 1024, 1024), ("cf", 1024, 4096), ("cm", 4096, 1024))


def _host_inputs(inp):
    cst = _consts()
    shared = {}
    for nm, I_, O in GEN_MATS:
        key = "L0_attn_" + nm
        sub = np.asarray(inp[key + "_sub_w"], np.float32)
        shared[nm + "_s0"] = _t(sub[0])
        shared[nm + "_s1"] = _t(sub[1])
        shared[nm + "_W"] = _t(inp[key + "_W_w"])
        shared[nm + "_B"] = _t(inp[key + "_B_w"])
        wl = np.asarray(inp[key + "_Wl_w"], np.float32)
        shared[nm + "_wl"] = np.broadcast_to(wl[0][None, :], (128, 2)).copy()
        nco = O // 128
        shared[nm + "_sb"] = np.concatenate(
            [_cols(inp[key + "_sub_b"][0], nco), _cols(inp[key + "_sub_b"][1], nco)], axis=1)
        shared[nm + "_Wb"] = _cols(inp[key + "_W_b"], nco)
        shared[nm + "_Bb"] = _cols(inp[key + "_B_b"], nco)
        wlb = np.asarray(inp[key + "_Wl_b"], np.float32)
        shared[nm + "_wlb"] = np.broadcast_to(wlb[0][None, :], (128, 2)).copy()
    shared["ln1g"] = _cols(inp["L0_attn_ln1_g"], NC)
    shared["ln1b"] = _cols(inp["L0_attn_ln1_b"], NC)
    shared["ln2g"] = _cols(inp["L0_attn_ln2_g"], NC)
    shared["ln2b"] = _cols(inp["L0_attn_ln2_b"], NC)
    shared["l0g"] = _cols(inp["L0_ffn_ln_gamma"], NC)
    shared["l0b"] = _cols(inp["L0_ffn_ln_beta"], NC)
    shared["fc1q"] = np.ascontiguousarray(np.asarray(inp["L0_ffn_fc1_q"], np.int32))
    shared["fc1s"] = np.ascontiguousarray(np.asarray(inp["L0_ffn_fc1_scales"], np.float32))
    shared["fc1b"] = _cols(inp["L0_ffn_fc1_bias"], 32)
    shared["fc2q"] = np.ascontiguousarray(np.asarray(inp["L0_ffn_fc2_q"], np.int32))
    s2 = np.asarray(inp["L0_ffn_fc2_scales"], np.float32)
    shared["fc2s"] = np.ascontiguousarray(
        s2.reshape(4, 8, 1024).transpose(1, 0, 2).reshape(8, 4096))
    shared["fc2brow"] = np.asarray(inp["L0_ffn_fc2_bias"], np.float32).reshape(1, 1024).copy()
    shared["l1w"] = _cols(inp["L1_attn_norm_w"], NC)
    shared["wf"] = _t(inp["L1_attn_w_forget"])
    shared["wi"] = _t(inp["L1_attn_w_input"])
    shared["wo"] = _t(inp["L1_attn_w_output"])
    shared["gw"] = np.ascontiguousarray(
        _t(inp["L1_ffn_gate_weight"]).reshape(8, 128, 8).transpose(1, 0, 2).reshape(128, 64))
    shared["wg"] = np.ascontiguousarray(np.asarray(inp["L1_ffn_w_gate"], np.float32).transpose(0, 2, 1))
    shared["wu"] = np.ascontiguousarray(np.asarray(inp["L1_ffn_w_up"], np.float32).transpose(0, 2, 1))
    shared["wd"] = np.ascontiguousarray(np.asarray(inp["L1_ffn_w_down"], np.float32).transpose(0, 2, 1))
    shared["wsg"] = _t(inp["L1_ffn_ws_gate"])
    shared["wsu"] = _t(inp["L1_ffn_ws_up"])
    shared["wsd"] = _t(inp["L1_ffn_ws_down"])
    shared.update(cst)

    x = np.asarray(inp["L0_attn_hs"], np.float32)
    in_maps = []
    for c in range(8):
        m = dict(shared)
        m["x"] = _t(x[c])
        in_maps.append(m)
    return in_maps



ENGS = ("sync", "tensor", "vector", "scalar", "gpsimd")
T0 = ("tE", 0)


class P:
    def __init__(self, nc, es):
        self.nc = nc
        self.es = es
        self.q = {e: [] for e in ENGS}
        self.cnt = {}
        self.sems = {}

    def sem(self, name):
        if name not in self.sems:
            self.sems[name] = self.es.enter_context(self.nc.semaphore(name))
            self.cnt[name] = 0
        return name

    def emit(self, eng, fn, waits=(), inc=None, n=1):
        ws = []
        for s, v in waits:
            if v <= 0:
                continue
            assert v <= self.cnt.get(s, 0), f"wait on future {s}:{v}>{self.cnt.get(s, 0)}"
            ws.append((s, v))
        self.q[eng].append((ws, fn, inc, n))
        if inc is not None:
            self.sem(inc)
            self.cnt[inc] += n
            return (inc, self.cnt[inc])
        return None

    def replay(self, ename, engine):
        maxw = {}
        for ws, fn, inc, n in self.q[ename]:
            for s, v in ws:
                if maxw.get(s, 0) < v:
                    engine.wait_ge(self.sems[s], v)
                    maxw[s] = v
            if fn is None:
                continue
            inst = fn(engine)
            if inc is not None:
                inst.then_inc(self.sems[inc], n)


class Dma:

    def __init__(self, p, eng, name, n_slots=1):
        self.p = p
        self.eng = eng
        self.sems = [p.sem(f"{name}{i}") for i in range(n_slots)]

    def load(self, dst, src, waits=(), slot=0):
        return self.p.emit(self.eng, lambda e, d=dst, s=src: e.dma_start(out=d, in_=s),
                           waits=waits, inc=self.sems[slot], n=16)



def build(n_sub=None):
    if n_sub is None:
        n_sub = _N_SUB
    nc = bass.Bass("TRN2", target_bir_lowering=False)
    from contextlib import ExitStack
    es = ExitStack()
    p = P(nc, es)

    def din(name, shape, dt=F32):
        return nc.dram_tensor(name, shape, dt, kind="ExternalInput").ap()

    x_h = din("x", [D, T])
    out_h = nc.dram_tensor("out", [D, T], F32, kind="ExternalOutput").ap()
    cst_h = {n_: din(n_, sh, dt) for n_, sh, dt in CONST_SPECS}
    h_sel8 = din("sel8", [8, 8 * 128])
    genh = {}
    for nm, I_, O in GEN_MATS:
        nco = O // 128
        genh[nm] = dict(
            s0=din(nm + "_s0", [I_, O]), s1=din(nm + "_s1", [I_, O]),
            W=din(nm + "_W", [I_, O]), B=din(nm + "_B", [I_, O]),
            wl=din(nm + "_wl", [128, 2]), sb=din(nm + "_sb", [128, 2 * nco]),
            Wb=din(nm + "_Wb", [128, nco]), Bb=din(nm + "_Bb", [128, nco]),
            wlb=din(nm + "_wlb", [128, 2]), O=O, I=I_, nco=nco)
    smh = {n_: din(n_, sh) for n_, sh in [
        ("ln1g", [128, NC]), ("ln1b", [128, NC]), ("ln2g", [128, NC]), ("ln2b", [128, NC]),
        ("l0g", [128, NC]), ("l0b", [128, NC]), ("fc1b", [128, 32]),
        ("l1w", [128, NC]), ("gw", [128, 64])]}
    h_fc1s = din("fc1s", [8, 4096])
    h_fc2s = din("fc2s", [8, 4096])
    h_fc2brow = din("fc2brow", [1, 1024])
    h_fc1q = din("fc1q", [D, 4096], I32)
    h_fc2q = din("fc2q", [4096, D], I32)
    h_wf = din("wf", [D, D]); h_wi = din("wi", [D, D]); h_wo = din("wo", [D, D])
    h_wg = din("wg", [8, D, 512]); h_wu = din("wu", [8, D, 512]); h_wd = din("wd", [8, 512, D])
    h_wsg = din("wsg", [D, D]); h_wsu = din("wsu", [D, D]); h_wsd = din("wsd", [D, D])

    s_fc1 = nc.dram_tensor("s_fc1", [D, 4096], BF16, kind="Internal").ap()
    s_fc2 = nc.dram_tensor("s_fc2", [4096, D], BF16, kind="Internal").ap()
    s_cf = nc.dram_tensor("s_cf", [D, 4096], BF16, kind="Internal").ap()
    s_cm = nc.dram_tensor("s_cm", [4096, D], BF16, kind="Internal").ap()
    s_qkv = nc.dram_tensor("s_qkv", [D, 3072], BF16, kind="Internal").ap()
    s_cp = nc.dram_tensor("s_cp", [D, D], BF16, kind="Internal").ap()
    s_lnx = nc.dram_tensor("s_lnx", [D, T], F32, kind="Internal").ap()
    s_v = nc.dram_tensor("s_v", [T, HEADS * 65], BF16, kind="Internal").ap()

    def sb(name, shape, dt):
        return es.enter_context(nc.sbuf_tensor(name, shape, dt))

    X = sb("X", [128, NC * T], F32)
    XN = sb("XN", [128, NC * T], BF16)
    C = {n_: sb("c_" + n_, sh, dt) for n_, sh, dt in CONST_SPECS}
    SM = {n_: sb("p_" + n_, sh, F32) for n_, sh in [
        ("ln1g", [128, NC]), ("ln1b", [128, NC]), ("ln2g", [128, NC]), ("ln2b", [128, NC]),
        ("l0g", [128, NC]), ("l0b", [128, NC]), ("fc1b", [128, 32]),
        ("l1w", [128, NC]), ("gw", [128, 64])]}
    GB = {}
    for nm, I_, O in GEN_MATS:
        nco = O // 128
        GB[nm] = dict(sb=sb(nm + "sb", [128, 2 * nco], F32), Wb=sb(nm + "Wb", [128, nco], F32),
                      Bb=sb(nm + "Bb", [128, nco], F32), wl=sb(nm + "wl", [128, 2], F32),
                      wlb=sb(nm + "wlb", [128, 2], F32), bias=sb(nm + "bias", [128, nco], F32))
    ROWS = sb("ROWS", [1, 2048], F32)
    STC = sb("STC", [128, 256], F32)
    SCR = sb("SCR", [128, 2048], F32)
    EPI = sb("EPI", [128, 512], F32)
    NEGBC = sb("NEGBC", [128, 512], BF16)
    PS = [es.enter_context(nc.psum_tensor(f"PS{i}", [128, 512], F32)) for i in range(8)]

    scope_stack = []

    def sopen(*specs):
        cms = []
        ts = []
        for nm_, sh_, dt_ in specs:
            cm = nc.sbuf_tensor(nm_, sh_, dt_)
            ts.append(cm.__enter__())
            cms.append(cm)
        scope_stack.append(cms)
        return ts

    def sclose():
        cms = scope_stack.pop()
        for cm in reversed(cms):
            cm.__exit__(None, None, None)

    def barrier():
        return [(sname, p.cnt[sname]) for sname in list(p.cnt)]

    dW = Dma(p, "sync", "dW")
    dX = Dma(p, "sync", "dX")
    dS = Dma(p, "scalar", "dS")
    dG = Dma(p, "gpsimd", "dG")
    for s in ("tE", "vE", "sE"):
        p.sem(s)

    def TT_(fn, waits=(), inc="tE"):
        return p.emit("tensor", fn, waits=waits, inc=inc)

    def V_(fn, waits=(), inc="vE"):
        return p.emit("vector", fn, waits=waits, inc=inc)

    def S_(fn, waits=(), inc="sE"):
        return p.emit("scalar", fn, waits=waits, inc=inc)

    bank_free = [T0] * 8

    def mm_group(bi, mms, waits=(), out=None):
        w = list(waits) + [bank_free[bi]]
        n = len(mms)
        tok = None
        for i, mm in enumerate(mms):
            lh, rh = mm[0], mm[1]
            oa = mm[2] if len(mm) > 2 else PS[bi][:]
            tok = p.emit("tensor",
                         lambda e, oa=oa, lh=lh, rh=rh, st=(i == 0), sp=(i == n - 1):
                         e.matmul(oa, lh, rh, start=st, stop=sp),
                         waits=(w if i == 0 else ()), inc=("tE" if i == n - 1 else None))
        return tok

    def consume(bi, tok):
        bank_free[bi] = tok

    x_tok = dX.load(bass.AP(X, 0, [[NC * T, 128], [T, NC], [1, T]]),
                    bass.AP(x_h.tensor, 0, [[T, 128], [128 * T, NC], [1, T]]))
    ctok = None
    for n_ in C:
        ctok = dW.load(C[n_][:], cst_h[n_])
    for n_ in SM:
        ctok = dW.load(SM[n_][:], smh[n_])
    for nm, I_, O in GEN_MATS:
        for k2 in ("sb", "Wb", "Bb", "wl", "wlb"):
            ctok = dW.load(GB[nm][k2][:], genh[nm][k2])
    CW = ctok

    gbias = {}
    for nm, I_, O in GEN_MATS:
        b = GB[nm]
        nco = O // 128
        t1 = S_(lambda e, b=b, nco=nco: e.activation(
            out=b["bias"][:, 0:nco], in_=b["sb"][:, nco:2 * nco], func=AF.Copy,
            scale=b["wlb"][:, 1:2]), waits=[CW])
        t2 = V_(lambda e, b=b, nco=nco: e.scalar_tensor_tensor(
            b["bias"][:, 0:nco], b["sb"][:, 0:nco], b["wlb"][:, 0:1],
            b["bias"][:, 0:nco], OP.mult, OP.add), waits=[t1, CW])
        t3 = S_(lambda e, b=b, nco=nco: e.activation(
            out=b["bias"][:, 0:nco], in_=b["bias"][:, 0:nco], func=AF.Tanh), waits=[t2])
        t4 = V_(lambda e, b=b, nco=nco: e.scalar_tensor_tensor(
            b["bias"][:, 0:nco], b["bias"][:, 0:nco], 1.0, b["Wb"][:, 0:nco],
            OP.mult, OP.mult), waits=[t3])
        t5 = V_(lambda e, b=b, nco=nco: e.tensor_add(
            b["bias"][:, 0:nco], b["bias"][:, 0:nco], b["Bb"][:, 0:nco]), waits=[t4])
        gbias[nm] = t5
    gbias["ca"] = V_(lambda e: e.tensor_scalar_mul(
        GB["ca"]["bias"][:, 0:8], GB["ca"]["bias"][:, 0:8], 0.125), waits=[gbias["ca"]])

    rows_free = [T0] * 4

    def col_to_bcast(col_of, n_cols, bi_t, bi_bc, rslot, waits=(), out_bf16=None):
        w = list(waits) + [bank_free[bi_t]]
        tok = None
        for j in range(n_cols):
            tok = p.emit("tensor",
                         lambda e, j=j: e.transpose(
                             PS[bi_t][0:1, 128 * j:128 * (j + 1)], col_of(j), C["ident_f"][:]),
                         waits=(w if j == 0 else ()), inc=("tE" if j == n_cols - 1 else None))
        tokc = V_(lambda e: e.tensor_copy(
            ROWS[0:1, 512 * rslot:512 * rslot + 128 * n_cols],
            PS[bi_t][0:1, 0:128 * n_cols]), waits=[tok, rows_free[rslot]])
        consume(bi_t, tokc)
        tokb = mm_group(bi_bc, [(C["r064f"][0:1, 0:128],
                                 ROWS[0:1, 512 * rslot:512 * rslot + 128 * n_cols],
                                 PS[bi_bc][0:128, 0:128 * n_cols])], waits=[tokc])
        rows_free[rslot] = tokb
        if out_bf16 is not None:
            toko = V_(lambda e: e.tensor_copy(out_bf16, PS[bi_bc][0:128, 0:128 * n_cols]),
                      waits=[tokb])
            consume(bi_bc, toko)
            return tokb, toko
        return tokb, None

    SUMC, SQC, VARC, R1C, TMPC, R2C, RHC, MR1C, RRC = (0, 16, 32, 48, 64, 80, 96, 112, 128)

    def stats(buf, buf_tok, eps2=None, rms_eps=None):
        last_st = None
        for g in range(NTG):
            tok_s = mm_group(6, [(C["invd"][:, 0:1],
                                  buf[:, T * c + TG * g:T * c + TG * g + TG],
                                  PS[6][0:1, 0:TG]) for c in range(NC)], waits=[buf_tok])
            tok_q = None
            for wave in range(2):
                sq_tok = None
                for cc in range(4):
                    c = 4 * wave + cc
                    sq_tok = S_(lambda e, c=c, cc=cc, g=g: e.activation(
                        out=SCR[:, 512 * cc:512 * cc + 512],
                        in_=buf[:, T * c + TG * g:T * c + TG * g + TG], func=AF.Square),
                        waits=[buf_tok, tok_q or state.get("scr_free", T0)]
                              + ([last_st] if last_st else []))
                w0 = [sq_tok] + ([bank_free[7]] if wave == 0 else [])
                for cc in range(4):
                    tok_q = p.emit("tensor",
                                   lambda e, cc=cc, st=(wave == 0 and cc == 0),
                                   sp=(wave == 1 and cc == 3): e.matmul(
                                       PS[7][0:1, 0:TG], C["invd"][:, 0:1],
                                       SCR[:, 512 * cc:512 * cc + 512],
                                       start=st, stop=sp),
                                   waits=(w0 if cc == 0 else ()),
                                   inc=("tE" if (cc == 3) else None))
            state["scr_free"] = tok_q
            cm1 = V_(lambda e: e.tensor_copy(ROWS[0:1, 0:TG], PS[6][0:1, 0:TG]),
                     waits=[tok_s, rows_free[0]])
            consume(6, cm1)
            cm2 = V_(lambda e: e.tensor_copy(ROWS[0:1, 512:512 + TG], PS[7][0:1, 0:TG]),
                     waits=[tok_q, rows_free[1]])
            consume(7, cm2)
            w1 = [bank_free[5], cm2]
            tokt = None
            for j in range(4):
                tokt = p.emit("tensor", lambda e, j=j: e.transpose(
                    PS[5][0:128, j:j + 1], ROWS[0:1, 128 * j:128 * (j + 1)],
                    C["ident_f"][0:1, 0:1]),
                    waits=(w1 if j == 0 else ()), inc=None)
            for j in range(4):
                tokt = p.emit("tensor", lambda e, j=j: e.transpose(
                    PS[5][0:128, 4 + j:5 + j], ROWS[0:1, 512 + 128 * j:512 + 128 * (j + 1)],
                    C["ident_f"][0:1, 0:1]), waits=(), inc=("tE" if j == 3 else None))
            st1 = V_(lambda e, g=g: e.tensor_copy(STC[:, SUMC + 4 * g:SUMC + 4 * g + 4],
                                                  PS[5][0:128, 0:4]), waits=[tokt])
            last_st = V_(lambda e, g=g: e.tensor_copy(STC[:, SQC + 4 * g:SQC + 4 * g + 4],
                                                      PS[5][0:128, 4:8]), waits=[tokt])
            consume(5, last_st)
            rows_free[0] = tokt
            rows_free[1] = tokt
        a = V_(lambda e: e.scalar_tensor_tensor(
            STC[:, TMPC:TMPC + 16], STC[:, SUMC:SUMC + 16], -1.0, STC[:, SUMC:SUMC + 16],
            OP.mult, OP.mult), waits=[last_st])
        a = V_(lambda e: e.tensor_add(STC[:, VARC:VARC + 16], STC[:, TMPC:TMPC + 16],
                                      STC[:, SQC:SQC + 16]), waits=[a])
        a = S_(lambda e: e.activation(out=STC[:, TMPC:TMPC + 16], in_=STC[:, VARC:VARC + 16],
                                      func=AF.Sqrt, bias=C["eps5"][:, 0:1]), waits=[a])
        a = V_(lambda e: e.reciprocal(STC[:, R1C:R1C + 16], STC[:, TMPC:TMPC + 16]), waits=[a])
        a = V_(lambda e: e.tensor_mul(STC[:, MR1C:MR1C + 16], STC[:, SUMC:SUMC + 16],
                                      STC[:, R1C:R1C + 16]), waits=[a])
        if eps2 is not None or rms_eps is not None:
            a = V_(lambda e: e.tensor_mul(STC[:, TMPC:TMPC + 16], STC[:, R1C:R1C + 16],
                                          STC[:, R1C:R1C + 16]), waits=[a])
            a = V_(lambda e: e.tensor_mul(STC[:, TMPC:TMPC + 16], STC[:, TMPC:TMPC + 16],
                                          STC[:, VARC:VARC + 16]), waits=[a])
            ee = eps2 if eps2 is not None else rms_eps
            tgt = R2C if eps2 is not None else RRC
            eps_ap = C["eps5"] if ee == 1e-5 else C["eps6"]
            a = S_(lambda e, eps_ap=eps_ap: e.activation(out=STC[:, TMPC:TMPC + 16],
                                                 in_=STC[:, TMPC:TMPC + 16],
                                                 func=AF.Sqrt, bias=eps_ap[:, 0:1]), waits=[a])
            a = V_(lambda e, tgt=tgt: e.reciprocal(STC[:, tgt:tgt + 16],
                                                   STC[:, TMPC:TMPC + 16]), waits=[a])
            if eps2 is not None:
                a = V_(lambda e: e.tensor_mul(STC[:, RHC:RHC + 16], STC[:, R1C:R1C + 16],
                                              STC[:, R2C:R2C + 16]), waits=[a])
        return a

    def stc_col(slot, g):
        return lambda j: STC[:, slot + 4 * g + j:slot + 4 * g + j + 1]

    state = {"x": x_tok}

    def materialize_lnx(AUX, stat_tok, bar=()):
        last = None
        for g in range(NTG):
            tr, _ = col_to_bcast(stc_col(R1C, g), 4, 5, 4, 0, waits=[stat_tok])
            tm, _ = col_to_bcast(stc_col(MR1C, g), 4, 5, 3, 1, waits=[stat_tok])
            for c in range(NC):
                tt = V_(lambda e, c=c, g=g: e.scalar_tensor_tensor(
                    AUX[:, T * c + TG * g:T * c + TG * g + TG],
                    X[:, T * c + TG * g:T * c + TG * g + TG], 1.0,
                    PS[4][0:128, 0:TG], OP.mult, OP.mult),
                    waits=[tr, state["x"], stat_tok] + list(bar))
                tt = V_(lambda e, c=c, g=g: e.tensor_sub(
                    AUX[:, T * c + TG * g:T * c + TG * g + TG],
                    AUX[:, T * c + TG * g:T * c + TG * g + TG],
                    PS[3][0:128, 0:TG]), waits=[tm, tt])
                last = tt
            consume(4, last)
            consume(3, last)
        return last

    def materialize_xn(SRC, src_tok, colslot, gamma, beta):
        last = None
        for g in range(NTG):
            tb, _ = col_to_bcast(stc_col(colslot, g), 4, 5, 4, 0, waits=[src_tok])
            for c in range(NC):
                tt = V_(lambda e, c=c, g=g: e.scalar_tensor_tensor(
                    XN[:, T * c + TG * g:T * c + TG * g + TG],
                    SRC[:, T * c + TG * g:T * c + TG * g + TG], 1.0,
                    PS[4][0:128, 0:TG], OP.mult, OP.mult), waits=[tb, src_tok])
                if gamma is not None:
                    if beta is not None:
                        tt = V_(lambda e, c=c, g=g: e.tensor_scalar(
                            XN[:, T * c + TG * g:T * c + TG * g + TG],
                            XN[:, T * c + TG * g:T * c + TG * g + TG],
                            gamma[:, c:c + 1], beta[:, c:c + 1], OP.mult, OP.add),
                            waits=[tt, CW])
                    else:
                        tt = V_(lambda e, c=c, g=g: e.tensor_scalar(
                            XN[:, T * c + TG * g:T * c + TG * g + TG],
                            XN[:, T * c + TG * g:T * c + TG * g + TG],
                            gamma[:, c:c + 1], None, OP.mult),
                            waits=[tt, CW])
                last = tt
            consume(4, last)
        return last

    dq = Dma(p, "gpsimd", "dq")
    dqs = Dma(p, "scalar", "dqs")
    dqsc = Dma(p, "sync", "dqsc", n_slots=2)

    def emit_dequant(DQ, DQO, SEL8, SCL, selw, qh, scales_h, scratch, n_k, n_n, blkmap, bar):
        dq_free = state.get("dq", T0)
        dqo_free = state.get("dqo", T0)
        scl_free = state.get("scl", T0)
        last_store = None
        for kc in range(n_k):
            tq = dq.load(DQ[:, 0:n_n],
                         bass.AP(qh.tensor, kc * 128 * n_n, [[n_n, 128], [1, n_n]]),
                         waits=[dq_free] + bar)
            selp, colb = blkmap(kc)
            td = None
            for j in range(n_n // 512):
                jj = j % 2
                tsc = dqsc.load(SCL[0:8, jj * 512:jj * 512 + 512],
                                bass.AP(scales_h.tensor, colb * n_n + j * 512,
                                        [[4096, 8], [1, 512]]),
                                waits=[state.get("sclb%d" % jj, T0)] + bar, slot=jj)
                tokb = mm_group(2, [(SEL8[0:8, selp * 128:selp * 128 + 128],
                                     SCL[0:8, jj * 512:jj * 512 + 512],
                                     PS[2][0:128, 0:512])], waits=[CW, tsc] + selw)
                state["sclb%d" % jj] = tokb
                td = V_(lambda e, j=j: e.scalar_tensor_tensor(
                    DQO[:, j * 512:(j + 1) * 512], DQ[:, j * 512:(j + 1) * 512],
                    -128.0, PS[2][0:128, 0:512], OP.add, OP.mult),
                    waits=[tokb, tq, dqo_free])
                consume(2, td)
            dq_free = td
            ts = dqs.load(bass.AP(scratch.tensor, kc * 128 * n_n, [[n_n, 128], [1, n_n]]),
                          DQO[:, 0:n_n], waits=[td])
            dqo_free = ts
            last_store = ts
        state["dq"] = dq_free
        state["dqo"] = dqo_free
        return last_store

    def emit_gen(nm, scratch, scale_head=None, bar=None):
        bar = barrier()
        g = genh[nm]
        b = GB[nm]
        O = g["O"]
        I_ = g["I"]
        NRG = I_ // 1024
        GS = sopen(("gGS0_" + nm, [128, 1024], F32), ("gGS1_" + nm, [128, 1024], F32),
                   ("gGS2_" + nm, [128, 1024], F32), ("gGS3_" + nm, [128, 1024], F32),
                   ("gOUT_" + nm, [128, 1024], BF16))
        GOUT = GS[4]
        gfree = [list(bar)] * 4
        last = None
        genw = Dma(p, "sync", "g_" + nm, n_slots=4)
        gens_ = Dma(p, "scalar", "gs_" + nm)
        NSL = O // 128
        for jj in range(NSL * NRG):
            j, rg = jj // NRG, jj % NRG
            tks = []
            for i, key in enumerate(("s0", "s1", "W", "B")):
                tks.append(genw.load(
                    bass.AP(GS[i], 0, [[1024, 128], [128, NC], [1, 128]]),
                    bass.AP(g[key].tensor, rg * 1024 * O + j * 128,
                            [[O, 128], [128 * O, NC], [1, 128]]),
                    waits=list(gfree[i]), slot=i))
            t1 = S_(lambda e: e.activation(out=SCR[:, 0:1024], in_=GS[1][:, 0:1024],
                                           func=AF.Copy, scale=b["wl"][:, 1:2]),
                    waits=[tks[1], CW, state.get("scr_free", T0)])
            t2 = V_(lambda e: e.scalar_tensor_tensor(
                SCR[:, 0:1024], GS[0][:, 0:1024], b["wl"][:, 0:1], SCR[:, 0:1024],
                OP.mult, OP.add), waits=[t1, tks[0]])
            t3 = S_(lambda e: e.activation(out=SCR[:, 0:1024], in_=SCR[:, 0:1024],
                                           func=AF.Tanh), waits=[t2])
            t4 = V_(lambda e: e.scalar_tensor_tensor(
                SCR[:, 0:1024], SCR[:, 0:1024], 1.0, GS[2][:, 0:1024],
                OP.mult, OP.mult), waits=[t3, tks[2]])
            gout_free = state.get("gout_" + nm, T0)
            if scale_head is not None and j < scale_head[0]:
                t5 = V_(lambda e: e.tensor_add(SCR[:, 0:1024], SCR[:, 0:1024],
                                               GS[3][:, 0:1024]), waits=[t4, tks[3]])
                t5 = V_(lambda e, f=scale_head[1]: e.tensor_scalar(
                    GOUT[:, 0:1024], SCR[:, 0:1024], f, None, OP.mult),
                    waits=[t5, gout_free])
            else:
                t5 = V_(lambda e: e.tensor_add(
                    GOUT[:, 0:1024], SCR[:, 0:1024], GS[3][:, 0:1024]),
                    waits=[t4, tks[3], gout_free])
            state["scr_free"] = t5
            gfree = [[t5]] * 4
            ts = gens_.load(bass.AP(scratch.tensor, rg * 1024 * O + j * 128,
                                    [[O, 128], [128 * O, NC], [1, 128]]),
                            bass.AP(GOUT, 0, [[1024, 128], [128, NC], [1, 128]]),
                            waits=[t5])
            state["gout_" + nm] = ts
            last = ts
        sclose()
        return last


    def ffn_stream(pref, WS, WT, xn_tok, w1_scr, w1_tok, w2_scr, w2_tok,
                   b1_cols, b1_tok, act_fn, b2_row, b2_tok, out_scale, Y,
                   y_free0=T0, bar=(), b2_cols=None):
        w1d = Dma(p, "sync", pref + "w1", n_slots=2)
        w2d = Dma(p, "scalar", pref + "w2", n_slots=2)
        ws_free = [list(bar), list(bar)]
        wt_free = [list(bar), list(bar)]
        y_free = y_free0
        lastx = state["x"]
        for g in range(NTG):
            y_toks = []
            for io in range(8):
                slot = io % 2
                ldt = w1d.load(
                    bass.AP(WS[slot], 0, [[4096, 128], [512, NC], [1, 512]]),
                    bass.AP(w1_scr.tensor, io * 512, [[4096, 128], [128 * 4096, NC], [1, 512]]),
                    waits=list(ws_free[slot]) + [w1_tok], slot=slot)
                tok = None
                for isub in range(4):
                    bi = isub % 2
                    mms = [(WS[slot][:, kc * 512 + isub * 128:kc * 512 + isub * 128 + 128],
                            XN[:, T * kc + TG * g:T * kc + TG * g + TG]) for kc in range(NC)]
                    tok = mm_group(bi, mms, waits=[ldt, xn_tok])
                    ic = io * 4 + isub
                    ytk = S_(lambda e, bi=bi, ic=ic: e.activation(
                        out=Y[:, ic * TG:(ic + 1) * TG], in_=PS[bi][0:128, 0:TG],
                        func=act_fn, bias=b1_cols[:, ic:ic + 1]),
                        waits=[tok, b1_tok, y_free])
                    consume(bi, ytk)
                    y_toks.append(ytk)
                ws_free[slot] = [tok]
            y_tok = y_toks[-1]
            first = [True] * 8
            slot_tok = {}
            for ks in range(8):
                slot = ks % 2
                ldt = w2d.load(
                    bass.AP(WT[slot], 0, [[4096, 128], [1024, 4], [1, 1024]]),
                    bass.AP(w2_scr.tensor, ks * 512 * D, [[D, 128], [128 * D, 4], [1, 1024]]),
                    waits=list(wt_free[slot]) + [w2_tok], slot=slot)
                for nc_ in range(8):
                    w0 = [ldt, y_tok]
                    if first[nc_]:
                        w0 = w0 + [bank_free[nc_]]
                    for ksub in range(4):
                        ic = ks * 4 + ksub
                        is_last_of_ks = (nc_ == 7 and ksub == 3)
                        lastmm = (b2_cols is not None and ks == 7 and ksub == 3)
                        tokm = p.emit(
                            "tensor",
                            lambda e, nc_=nc_, slot=slot, ksub=ksub, ic=ic,
                            st=(first[nc_] and ksub == 0), sp=lastmm:
                            e.matmul(PS[nc_][0:128, 0:TG],
                                     WT[slot][:, ksub * 1024 + nc_ * 128:ksub * 1024 + nc_ * 128 + 128],
                                     Y[:, ic * TG:(ic + 1) * TG], start=st, stop=sp),
                            waits=(w0 if ksub == 0 else ()),
                            inc=("tE" if (is_last_of_ks or lastmm) else None))
                        if is_last_of_ks:
                            slot_tok[ks] = tokm
                        if lastmm:
                            slot_tok[(7, nc_)] = tokm
                    first[nc_] = False
                wt_free[slot] = [slot_tok[ks]]
            lasttok = None
            for nc_ in range(8):
                if b2_cols is None:
                    tok = p.emit("tensor",
                                 lambda e, nc_=nc_: e.matmul(
                                     PS[nc_][0:128, 0:TG],
                                     b2_row[0:1, nc_ * 128:nc_ * 128 + 128],
                                     C["onesrow"][0:1, 0:TG], start=False, stop=True),
                                 waits=([b2_tok] if nc_ == 0 else ()), inc="tE")
                else:
                    tok = slot_tok[(7, nc_)]
                xt = V_(lambda e, nc_=nc_, g=g: e.scalar_tensor_tensor(
                    X[:, T * nc_ + TG * g:T * nc_ + TG * g + TG],
                    PS[nc_][0:128, 0:TG], out_scale,
                    X[:, T * nc_ + TG * g:T * nc_ + TG * g + TG], OP.mult, OP.add),
                    waits=[tok, lastx])
                if b2_cols is not None:
                    xt = V_(lambda e, nc_=nc_, g=g: e.tensor_scalar(
                        X[:, T * nc_ + TG * g:T * nc_ + TG * g + TG],
                        X[:, T * nc_ + TG * g:T * nc_ + TG * g + TG],
                        b2_cols[1][:, nc_:nc_ + 1], None, OP.add),
                        waits=[xt, b2_cols[0]])
                consume(nc_, xt)
                lastx = xt
                lasttok = tok
            y_free = lasttok
        state["x"] = lastx
        return lastx

    def emit_fc_dequants(bar):
        DQ, DQO, SEL8, SCL = sopen(
            ("dqDQ", [128, 4096], F32), ("dqO", [128, 4096], BF16),
            ("dqSEL", [8, 8 * 128], F32), ("dqSCL", [8, 1024], F32))
        dsel = Dma(p, "sync", "dsel_fc")
        selt = dsel.load(SEL8[:], h_sel8)
        t1 = emit_dequant(DQ, DQO, SEL8, SCL, [selt], h_fc1q, h_fc1s, s_fc1, 8, 4096,
                          lambda kc: (kc, 0), bar)
        t2 = emit_dequant(DQ, DQO, SEL8, SCL, [selt], h_fc2q, h_fc2s, s_fc2, 32, 1024,
                          lambda kc: (kc % 8, kc // 8), bar)
        sclose()
        return t1, t2

    def sub_l0ffn(pref, fc1_tok, fc2_tok):
        bar = barrier()
        st_tok = stats(X, state["x"], eps2=1e-5)
        AUX, = sopen((pref + "AUX", [128, NC * T], F32))
        lnx_tok = materialize_lnx(AUX, st_tok, bar=bar)
        xn_tok = materialize_xn(AUX, lnx_tok, R2C, SM["l0g"], SM["l0b"])
        lastx = state["x"]
        for c in range(NC):
            lastx = V_(lambda e, c=c: e.scalar_tensor_tensor(
                X[:, T * c:T * c + T], AUX[:, T * c:T * c + T], 0.5,
                X[:, T * c:T * c + T], OP.mult, OP.add), waits=[lnx_tok, lastx])
        state["x"] = lastx
        sclose()
        bar2 = barrier()
        Y, WSa, WSb, WTa, WTb, B2R = sopen(
            (pref + "Y", [128, 32 * TG], BF16),
            (pref + "WSa", [128, 4096], BF16), (pref + "WSb", [128, 4096], BF16),
            (pref + "WTa", [128, 4096], BF16), (pref + "WTb", [128, 4096], BF16),
            (pref + "B2R", [1, 1024], F32))
        db2 = Dma(p, "sync", pref + "db2")
        b2t = db2.load(B2R[:], h_fc2brow, waits=bar2)
        ffn_stream(pref, [WSa, WSb], [WTa, WTb], xn_tok, s_fc1, fc1_tok, s_fc2,
                   fc2_tok, SM["fc1b"], CW, AF.Silu, B2R, b2t, 0.25, Y,
                   y_free0=state["x"], bar=bar2)
        sclose()

    if n_sub >= 1:
        t1, t2 = emit_fc_dequants([])
        sub_l0ffn("f1", t1, t2)


    def materialize_ln_full(SRC, src_tok, gamma, beta, stat_tok):
        last = None
        for g in range(NTG):
            tr, _ = col_to_bcast(stc_col(R1C, g), 4, 5, 4, 0, waits=[stat_tok])
            tm, _ = col_to_bcast(stc_col(MR1C, g), 4, 5, 3, 1, waits=[stat_tok])
            for c in range(NC):
                sl = SCR[:, 512 * (c % 4):512 * (c % 4) + 512]
                t1 = V_(lambda e, c=c, g=g, sl=sl: e.scalar_tensor_tensor(
                    sl, SRC[:, T * c + TG * g:T * c + TG * g + TG], 1.0,
                    PS[4][0:128, 0:TG], OP.mult, OP.mult),
                    waits=[tr, src_tok, state.get("scr_free", T0)]
                          + ([last] if last else []))
                t1 = V_(lambda e, c=c, sl=sl: e.tensor_sub(sl, sl, PS[3][0:128, 0:TG]),
                        waits=[tm, t1])
                t1 = V_(lambda e, c=c, g=g, sl=sl: e.tensor_scalar(
                    XN[:, T * c + TG * g:T * c + TG * g + TG], sl,
                    gamma[:, c:c + 1], beta[:, c:c + 1], OP.mult, OP.add),
                    waits=[t1, CW])
                last = t1
            consume(4, last)
            consume(3, last)
        state["scr_free"] = last
        return last

    def sub_l0attn():
        CH = HEADS // 2
        dLx = Dma(p, "scalar", "dLx")
        bar = barrier()
        st_tok = stats(X, state["x"], eps2=1e-5)
        AUX, = sopen(("a2AUX", [128, NC * T], F32))
        lnx_tok = materialize_lnx(AUX, st_tok, bar=bar)
        h1_tok = materialize_xn(AUX, lnx_tok, R2C, SM["ln1g"], SM["ln1b"])
        slnx = dLx.load(bass.AP(s_lnx.tensor, 0, [[T, 128], [128 * T, NC], [1, T]]),
                        bass.AP(AUX, 0, [[NC * T, 128], [T, NC], [1, T]]),
                        waits=[lnx_tok])
        sclose()
        bar2 = barrier()
        QF, KF, VST0, VST1, VTMh, PT, QW0, QW1 = sopen(
            ("aQF", [128, NC * T], BF16), ("aKF", [128, NC * T], BF16),
            ("aVST0", [128, 8 * 65], BF16), ("aVST1", [128, 8 * 65], BF16),
            ("aVTMh", [128, 16 * 65], BF16), ("aPT", [128, 2 * 512], BF16),
            ("aQW0", [128, 8 * 256], BF16), ("aQW1", [128, 8 * 256], BF16))
        QW = [QW0, QW1]
        VSTs = [VST0, VST1]
        ones_tok = [V_(lambda e, i=i: e.memset(
            bass.AP(VSTs[i], 64, [[8 * 65, 128], [65, 8], [1, 1]]), 1.0),
            waits=bar2) for i in range(2)]
        vbrow_tok = None
        for grp in range(2):
            wvb = [bank_free[5], gbias["ca"]] + bar2
            tokt = None
            for j in range(4):
                jj = 4 * grp + j
                tokt = p.emit("tensor", lambda e, j=j, jj=jj: e.transpose(
                    PS[5][0:1, 128 * j:128 * (j + 1)], GB["ca"]["bias"][:, 16 + jj:17 + jj],
                    C["ident_f"][:]), waits=(wvb if j == 0 else ()),
                    inc=("tE" if j == 3 else None))
            vbrow_tok = V_(lambda e, grp=grp: e.tensor_copy(
                ROWS[0:1, 1024 + 512 * grp:1536 + 512 * grp], PS[5][0:1, 0:512]),
                waits=[tokt, rows_free[2 + grp]])
            consume(5, vbrow_tok)
            rows_free[2 + grp] = vbrow_tok
        qkvw = Dma(p, "sync", "qkvw", n_slots=2)
        qkv_done = None
        vstore = Dma(p, "scalar", "vst", n_slots=2)
        vst_free = [[ones_tok[0]], [ones_tok[1]]]
        for nsl in range(12):
            slot = nsl % 2
            ldt = qkvw.load(
                bass.AP(QW[slot], 0, [[8 * 256, 128], [256, NC], [1, 256]]),
                bass.AP(s_qkv.tensor, nsl * 256, [[3072, 128], [128 * 3072, NC], [1, 256]]),
                waits=list(state.get("qw%d" % slot, [T0])) + bar2, slot=slot)
            if nsl < 8:
                tok = None
                for nsub in range(2):
                    for g in range(NTG):
                        bi = (nsub * NTG + g) % 2
                        n_global = nsl * 256 + nsub * 128
                        mms = [(QW[slot][:, kc * 256 + nsub * 128:kc * 256 + nsub * 128 + 128],
                                XN[:, T * kc + TG * g:T * kc + TG * g + TG])
                               for kc in range(NC)]
                        tok = mm_group(bi, mms, waits=[ldt, h1_tok])
                        dst = QF if n_global < 1024 else KF
                        colg = n_global if n_global < 1024 else n_global - 1024
                        ch = colg // 128
                        etk = V_(lambda e, bi=bi, dst=dst, ch=ch, g=g,
                                 col=n_global // 128: e.tensor_scalar(
                            dst[:, ch * T + TG * g:ch * T + TG * g + TG],
                            PS[bi][0:128, 0:TG],
                            GB["ca"]["bias"][:, col:col + 1], None, OP.add),
                            waits=[tok, gbias["ca"]])
                        consume(bi, etk)
                qkv_done = tok
            else:
                vsl = nsl - 8
                for tt in range(16):
                    bi = tt % 2
                    w0 = [ldt, h1_tok, bank_free[bi]]
                    tokv = None
                    for kc in range(NC):
                        tokv = p.emit("tensor",
                                      lambda e, kc=kc, tt=tt, slot=slot, bi=bi,
                                      st=(kc == 0): e.matmul(
                                          PS[bi][0:128, 0:256],
                                          XN[:, T * kc + 128 * tt:T * kc + 128 * tt + 128],
                                          QW[slot][:, kc * 256:kc * 256 + 256],
                                          start=st, stop=False),
                                      waits=(w0 if kc == 0 else ()), inc=None)
                    tokv = p.emit("tensor",
                                  lambda e, vsl=vsl, bi=bi: e.matmul(
                                      PS[bi][0:128, 0:256], C["r064f"][0:1, 0:128],
                                      ROWS[0:1, 1024 + 256 * vsl:1024 + 256 * vsl + 256],
                                      start=False, stop=True),
                                  waits=[vbrow_tok], inc="tE")
                    vslot = tt % 2
                    evk = V_(lambda e, bi=bi, vslot=vslot: e.tensor_copy(
                        bass.AP(VSTs[vslot], 0, [[8 * 65, 128], [65, 4], [1, 64]]),
                        bass.AP(PS[bi], 0, [[512, 128], [64, 4], [1, 64]])),
                        waits=[tokv] + list(vst_free[vslot]))
                    consume(bi, evk)
                    tokst = vstore.load(
                        bass.AP(s_v.tensor, tt * 128 * 1040 + (4 * vsl) * 65,
                                [[1040, 128], [1, 4 * 65]]),
                        bass.AP(VSTs[vslot], 0, [[8 * 65, 128], [1, 4 * 65]]),
                        waits=[evk], slot=vslot)
                    vst_free[vslot] = [tokst]
                qkv_done = tokv
            state["qw%d" % slot] = [tok if nsl < 8 else tokv]
        vload = Dma(p, "sync", "vload", n_slots=2)
        NEGQ_off = RRC + 16
        vput = [T0, T0]
        negbc_free = T0
        o_first = [qkv_done]
        exp_toks = {}
        for h in range(HEADS):
            ch, b_ = h // 2, 64 * (h % 2)
            vslot = h % 2
            vld = vload.load(
                bass.AP(VTMh, 0, [[16 * 65, 128], [65, 16], [1, 65]]),
                bass.AP(s_v.tensor, h * 65, [[1040, 128], [128 * 1040, 16], [1, 65]]),
                waits=[vput[0]] + ([state["vst_all"]] if h == 0 else []), slot=0)                 if False else vload.load(
                bass.AP(VTMh, 0, [[16 * 65, 128], [65, 16], [1, 65]]),
                bass.AP(s_v.tensor, h * 65, [[1040, 128], [128 * 1040, 16], [1, 65]]),
                waits=[vput[0], vst_free[0][0], vst_free[1][0]], slot=0)
            pv_last = None
            for qg in range(4):
                for qi in range(4):
                    qt = 4 * qg + qi
                    nm_tok = None
                    for kg in range(qt // 4 + 1):
                        width = 512 if kg < qt // 4 else (qt % 4) * 128 + 128
                        bi = (qt + kg) % 2
                        mms = [(QF[b_:b_ + 64, ch * T + 128 * qt:ch * T + 128 * qt + 128],
                                KF[b_:b_ + 64, ch * T + 512 * kg:ch * T + 512 * kg + width],
                                PS[bi][0:128, 0:width])]
                        if kg == qt // 4:
                            moff = (qt % 4) * 128
                            mms.append((C["itop"][b_:b_ + 64, :],
                                        C["m1a"][b_:b_ + 64, :],
                                        PS[bi][0:128, moff:moff + 128]))
                            mms.append((C["ibot"][b_:b_ + 64, :],
                                        C["m1b"][b_:b_ + 64, :],
                                        PS[bi][0:128, moff:moff + 128]))
                        tok = mm_group(bi, mms, waits=[qkv_done])
                        if kg == 0:
                            nm_tok = V_(lambda e, bi=bi, qi=qi, width=width: e.reduce_max(
                                STC[:, NEGQ_off + qi:NEGQ_off + qi + 1],
                                PS[bi][0:128, 0:width], axis=AX.X, negate=True),
                                waits=[tok, state.get("negq", T0)])
                        else:
                            t2 = V_(lambda e, bi=bi, width=width: e.reduce_max(
                                STC[:, NEGQ_off + 4:NEGQ_off + 5],
                                PS[bi][0:128, 0:width], axis=AX.X, negate=True),
                                waits=[tok, nm_tok])
                            nm_tok = V_(lambda e, qi=qi: e.tensor_tensor(
                                STC[:, NEGQ_off + qi:NEGQ_off + qi + 1],
                                STC[:, NEGQ_off + qi:NEGQ_off + qi + 1],
                                STC[:, NEGQ_off + 4:NEGQ_off + 5], OP.min),
                                waits=[t2])
                        consume(bi, nm_tok)
                    state["negq"] = nm_tok
                tb, tout = col_to_bcast(
                    lambda j: STC[:, NEGQ_off + j:NEGQ_off + j + 1], 4, 5, 6, 2,
                    waits=[state["negq"], negbc_free], out_bf16=NEGBC[:, 0:512])
                negbc_tok = tout
                nkc = 4 * qg + 4
                for kc in range(nkc):
                    off = 128 * (kc - 4 * qg) if kc >= 4 * qg else 0
                    bi = 2 + kc % 2
                    mms = [(KF[b_:b_ + 64, ch * T + 128 * kc:ch * T + 128 * kc + 128],
                            QF[b_:b_ + 64, ch * T + 512 * qg + off:ch * T + 512 * qg + 512],
                            PS[bi][0:128, off:512]),
                           (C["r064b"][b_:b_ + 64, 0:128],
                            NEGBC[b_:b_ + 64, off:512],
                            PS[bi][0:128, off:512])]
                    if kc >= 4 * qg:
                        mms.append((C["itop"][b_:b_ + 64, :], C["m2a"][b_:b_ + 64, :],
                                    PS[bi][0:128, off:off + 128]))
                        mms.append((C["ibot"][b_:b_ + 64, :], C["m2b"][b_:b_ + 64, :],
                                    PS[bi][0:128, off:off + 128]))
                    tok = mm_group(bi, mms, waits=[negbc_tok, qkv_done])
                    pslot = kc % 2
                    etk = S_(lambda e, bi=bi, off=off, pslot=pslot: e.activation(
                        out=PT[0:128, 512 * pslot + off:512 * pslot + 512],
                        in_=PS[bi][0:128, off:512], func=AF.Exp),
                        waits=[tok, exp_toks.get(pslot, T0)])
                    consume(bi, etk)
                    pv_w = [etk, vld]
                    if kc == 0:
                        pv_w.append(bank_free[7])
                    pv_tok = p.emit("tensor",
                                    lambda e, kc=kc, off=off, pslot=pslot,
                                    st=(kc == 0), sp=(kc == nkc - 1): e.matmul(
                                        PS[7][0:65, off:512],
                                        VTMh[:, kc * 65:kc * 65 + 65],
                                        PT[0:128, 512 * pslot + off:512 * pslot + 512],
                                        start=st, stop=sp),
                                    waits=pv_w, inc="tE")
                    exp_toks[pslot] = pv_tok
                rc = V_(lambda e: e.reciprocal(EPI[64:65, 0:512], PS[7][64:65, 0:512]),
                        waits=[pv_tok, state.get("epi_free", T0)])
                tb2 = mm_group(6, [(C["r064f"][64:65, 0:128], EPI[64:65, 0:512],
                                    PS[6][0:128, 0:512])], waits=[rc])
                cpy = V_(lambda e: e.tensor_copy(EPI[:, 0:512], PS[6][0:128, 0:512]),
                         waits=[tb2])
                consume(6, cpy)
                ot = V_(lambda e, b_=b_, ch=ch, qg=qg: e.tensor_mul(
                    XN[b_:b_ + 64, ch * T + 512 * qg:ch * T + 512 * qg + 512],
                    PS[7][0:64, 0:512], EPI[0:64, 0:512]),
                    waits=[cpy] + o_first)
                o_first = []
                consume(7, ot)
                state["epi_free"] = ot
                negbc_free = tok
            vput[0] = pv_tok
        o_tok = ("vE", p.cnt["vE"])
        sclose()
        bar3 = barrier()
        AUX, CP = sopen(("a3AUX", [128, NC * T], F32), ("aCP", [128, NC * 1024], BF16))
        rl = dLx.load(bass.AP(AUX, 0, [[NC * T, 128], [T, NC], [1, T]]),
                      bass.AP(s_lnx.tensor, 0, [[T, 128], [128 * T, NC], [1, T]]),
                      waits=[slnx] + bar3)
        cpw = Dma(p, "sync", "cpw")
        cpl = cpw.load(bass.AP(CP, 0, [[NC * 1024, 128], [1024, NC], [1, 1024]]),
                       bass.AP(s_cp.tensor, 0, [[1024, 128], [128 * 1024, NC], [1, 1024]]),
                       waits=[state.get("gout_cp", T0)] + bar3)
        lastx = state["x"]
        hm_tok = None
        for nc_ in range(NC):
            for g in range(NTG):
                bi = (nc_ * NTG + g) % 2
                mms = [(CP[:, kc * 1024 + nc_ * 128:kc * 1024 + nc_ * 128 + 128],
                        XN[:, T * kc + TG * g:T * kc + TG * g + TG])
                       for kc in range(NC)]
                tok = mm_group(bi, mms, waits=[cpl, o_tok])
                hm = V_(lambda e, bi=bi, nc_=nc_, g=g: e.scalar_tensor_tensor(
                    AUX[:, T * nc_ + TG * g:T * nc_ + TG * g + TG],
                    PS[bi][0:128, 0:TG], 1.0,
                    AUX[:, T * nc_ + TG * g:T * nc_ + TG * g + TG], OP.mult, OP.add),
                    waits=[tok, rl])
                consume(bi, hm)
                hm_tok = hm
            hm_tok = V_(lambda e, nc_=nc_: e.tensor_scalar(
                AUX[:, T * nc_:T * nc_ + T], AUX[:, T * nc_:T * nc_ + T],
                GB["cp"]["bias"][:, nc_:nc_ + 1], None, OP.add),
                waits=[hm_tok, gbias["cp"]])
            lastx = V_(lambda e, nc_=nc_: e.tensor_add(
                X[:, T * nc_:T * nc_ + T], X[:, T * nc_:T * nc_ + T],
                AUX[:, T * nc_:T * nc_ + T]), waits=[hm_tok, lastx])
        state["x"] = lastx
        st2 = stats(AUX, hm_tok)
        xn2_tok = materialize_ln_full(AUX, hm_tok, SM["ln2g"], SM["ln2b"], st2)
        sclose()
        bar4 = barrier()
        Y, WSa, WSb, WTa, WTb = sopen(
            ("a4Y", [128, 32 * TG], BF16),
            ("a4WSa", [128, 4096], BF16), ("a4WSb", [128, 4096], BF16),
            ("a4WTa", [128, 4096], BF16), ("a4WTb", [128, 4096], BF16))
        ffn_stream("a4", [WSa, WSb], [WTa, WTb], xn2_tok, s_cf,
                   state.get("gout_cf", T0), s_cm, state.get("gout_cm", T0),
                   GB["cf"]["bias"], gbias["cf"], AF.Gelu_apprx_tanh,
                   None, T0, 1.0, Y, y_free0=xn2_tok, bar=bar4,
                   b2_cols=(gbias["cm"], GB["cm"]["bias"]))
        sclose()

    if n_sub >= 2:
        gbar = barrier()
        emit_gen("ca", s_qkv, scale_head=(8, 0.125), bar=gbar)
        emit_gen("cp", s_cp, bar=gbar)
        emit_gen("cf", s_cf, bar=gbar)
        emit_gen("cm", s_cm, bar=gbar)
        sub_l0attn()

    h_sel8b = din("sel8b", [8, 8 * 128], BF16)
    GM1, GA1, GM2, GA2, GW1, GW2, GTM = 144, 160, 176, 192, 208, 224, 240

    def sub_l1ffn(pref):
        bar = barrier()
        st_tok = stats(X, state["x"])
        CFM, SEL8B = sopen((pref + "CFM", [8, 2048], BF16),
                           (pref + "SEL8B", [8, 8 * 128], BF16))
        dsel = Dma(p, "sync", pref + "dsel")
        selt = dsel.load(SEL8B[:], h_sel8b, waits=bar)
        AUX, SCO, IT8, FT8, FT8b = sopen(
            (pref + "AUX", [128, NC * T], F32), (pref + "SCO", [128, 128], F32),
            (pref + "IT8", [128, 8], I32), (pref + "FT8", [128, 8], F32),
            (pref + "FT8b", [128, 8], F32))
        lnx_tok = materialize_lnx(AUX, st_tok, bar=bar)
        xn_tok = None
        for c in range(NC):
            xn_tok = V_(lambda e, c=c: e.tensor_copy(
                XN[:, T * c:T * c + T], AUX[:, T * c:T * c + T]), waits=[lnx_tok])
        glast = None
        for tt in range(16):
            mms = [(AUX[:, T * kc + 128 * tt:T * kc + 128 * tt + 128],
                    SM["gw"][:, kc * 8:kc * 8 + 8],
                    PS[4][0:128, 8 * tt:8 * tt + 8]) for kc in range(NC)]
            glast = mm_group(4, mms, waits=[lnx_tok, CW])
        sg_tok = S_(lambda e: e.activation(out=SCO[:], in_=PS[4][0:128, 0:128],
                                           func=AF.Sigmoid), waits=[glast])
        consume(4, sg_tok)
        a = sg_tok
        for tt in range(16):
            sl = SCO[:, 8 * tt:8 * tt + 8]
            a = V_(lambda e, sl=sl, tt=tt: e.reduce_max(
                STC[:, GM1 + tt:GM1 + tt + 1], sl, axis=AX.X), waits=[a])
            a = V_(lambda e, sl=sl, tt=tt: e.tensor_scalar(
                IT8[:], sl, STC[:, GM1 + tt:GM1 + tt + 1], None, OP.is_equal),
                waits=[a])
            a = V_(lambda e: e.select(FT8[:], IT8[:], C["iota8"][:], C["c999"][:],
                                      add_drain=True), waits=[a])
            a = V_(lambda e, tt=tt: e.tensor_reduce(
                STC[:, GA1 + tt:GA1 + tt + 1], FT8[:], AX.X, OP.min), waits=[a])
            a = V_(lambda e, tt=tt: e.tensor_scalar(
                IT8[:], C["iota8"][:], STC[:, GA1 + tt:GA1 + tt + 1], None,
                OP.is_equal), waits=[a])
            a = V_(lambda e, sl=sl: e.select(FT8[:], IT8[:], C["cneg"][:], sl,
                                             add_drain=True), waits=[a])
            a = V_(lambda e, tt=tt: e.reduce_max(
                STC[:, GM2 + tt:GM2 + tt + 1], FT8[:], axis=AX.X), waits=[a])
            a = V_(lambda e, tt=tt: e.tensor_scalar(
                IT8[:], FT8[:], STC[:, GM2 + tt:GM2 + tt + 1], None, OP.is_equal),
                waits=[a])
            a = V_(lambda e: e.select(FT8[:], IT8[:], C["iota8"][:], C["c999"][:],
                                      add_drain=True), waits=[a])
            a = V_(lambda e, tt=tt: e.tensor_reduce(
                STC[:, GA2 + tt:GA2 + tt + 1], FT8[:], AX.X, OP.min), waits=[a])
        a = V_(lambda e: e.tensor_add(STC[:, GTM:GTM + 16], STC[:, GM1:GM1 + 16],
                                      STC[:, GM2:GM2 + 16]), waits=[a])
        a = V_(lambda e: e.tensor_scalar(STC[:, GTM:GTM + 16], STC[:, GTM:GTM + 16],
                                         1e-20, None, OP.add), waits=[a])
        a = V_(lambda e: e.reciprocal(STC[:, GTM:GTM + 16], STC[:, GTM:GTM + 16]),
               waits=[a])
        a = V_(lambda e: e.scalar_tensor_tensor(
            STC[:, GW1:GW1 + 16], STC[:, GM1:GM1 + 16], 2.5, STC[:, GTM:GTM + 16],
            OP.mult, OP.mult), waits=[a])
        a = V_(lambda e: e.scalar_tensor_tensor(
            STC[:, GW2:GW2 + 16], STC[:, GM2:GM2 + 16], 2.5, STC[:, GTM:GTM + 16],
            OP.mult, OP.mult), waits=[a])
        for tt in range(16):
            a = V_(lambda e, tt=tt: e.tensor_scalar(
                FT8[:], C["iota8"][:], STC[:, GA1 + tt:GA1 + tt + 1], None,
                OP.is_equal), waits=[a])
            a = V_(lambda e, tt=tt: e.tensor_scalar(
                FT8[:], FT8[:], STC[:, GW1 + tt:GW1 + tt + 1], None, OP.mult),
                waits=[a])
            a = V_(lambda e, tt=tt: e.tensor_scalar(
                FT8b[:], C["iota8"][:], STC[:, GA2 + tt:GA2 + tt + 1], None,
                OP.is_equal), waits=[a])
            a = V_(lambda e, tt=tt: e.tensor_scalar(
                FT8b[:], FT8b[:], STC[:, GW2 + tt:GW2 + tt + 1], None, OP.mult),
                waits=[a])
            a = V_(lambda e: e.tensor_add(FT8[:], FT8[:], FT8b[:]), waits=[a])
            tokt = TT_(lambda e: e.transpose(PS[5][0:8, 0:128], FT8[:],
                                             C["ident_f"][:]),
                       waits=[a, bank_free[5]])
            a = V_(lambda e, tt=tt: e.tensor_copy(CFM[0:8, 128 * tt:128 * tt + 128],
                                                  PS[5][0:8, 0:128]), waits=[tokt])
            consume(5, a)
        cfm_tok = a
        sclose()
        bar2 = barrier()
        bufs = sopen(
            (pref + "HMS", [128, 8 * 512], BF16),
            (pref + "HM0", [128, 4 * 512], BF16), (pref + "HM1", [128, 4 * 512], BF16),
            (pref + "HM2", [128, 4 * 512], BF16), (pref + "HM3", [128, 4 * 512], BF16),
            (pref + "HM4", [128, 4 * 512], BF16), (pref + "HM5", [128, 4 * 512], BF16),
            (pref + "HM6", [128, 4 * 512], BF16), (pref + "HM7", [128, 4 * 512], BF16),
            (pref + "SGB", [128, 8 * 512], BF16),
            (pref + "GUW0", [128, 8 * 256], BF16), (pref + "GUW1", [128, 8 * 256], BF16),
            (pref + "WD0", [128, 1024], BF16), (pref + "WD1", [128, 1024], BF16))
        HMS = bufs[0]
        HM = bufs[1:9]
        SGB, GUW0, GUW1, WD0, WD1 = bufs[9:]
        GUW = [GUW0, GUW1]
        WD = [WD0, WD1]
        guw = Dma(p, "gpsimd", pref + "guw", n_slots=2)
        wdd = Dma(p, "gpsimd", pref + "wdd", n_slots=2)
        lastx = state["x"]
        guw_free = [list(bar2), list(bar2)]
        wd_free = [list(bar2), list(bar2)]
        hm_free = {}
        sg_free = {}
        slot_ctr = [0]

        def gu_block(g, wgh, wuh, base_off, NI, HBUF, key, cbc):
            nonlocal lastx
            cbc_tok = None
            if cbc is not None:
                cbc_tok = mm_group(4, [(SEL8B[0:8, cbc * 128:cbc * 128 + 128],
                                        CFM[0:8, TG * g:TG * g + TG],
                                        PS[4][0:128, 0:TG])],
                                   waits=[cfm_tok, selt])
            sg_last = None
            for which, wh in ((0, wgh), (1, wuh)):
                for sl_ in range(NI // 256):
                    slot = slot_ctr[0] % 2
                    slot_ctr[0] += 1
                    ldt = guw.load(
                        bass.AP(GUW[slot], 0, [[8 * 256, 128], [256, NC], [1, 256]]),
                        bass.AP(wh.tensor, base_off + sl_ * 256,
                                [[NI, 128], [128 * NI, NC], [1, 256]]),
                        waits=list(guw_free[slot]), slot=slot)
                    for nsub in range(2):
                        ic = sl_ * 2 + nsub
                        bi = ic % 2
                        mms = [(GUW[slot][:, kc * 256 + nsub * 128:kc * 256 + nsub * 128 + 128],
                                XN[:, T * kc + TG * g:T * kc + TG * g + TG])
                               for kc in range(NC)]
                        tok = mm_group(bi, mms, waits=[ldt, xn_tok])
                        if which == 0:
                            sgt = S_(lambda e, bi=bi, ic=ic: e.activation(
                                out=SGB[:, ic * 512:ic * 512 + 512],
                                in_=PS[bi][0:128, 0:TG], func=AF.Silu),
                                waits=[tok, sg_free.get(ic, T0)])
                            consume(bi, sgt)
                            sg_free[ic] = sgt
                            sg_last = sgt
                        else:
                            hmt = V_(lambda e, bi=bi, ic=ic, HBUF=HBUF:
                                     e.scalar_tensor_tensor(
                                         HBUF[:, ic * 512:ic * 512 + 512],
                                         PS[bi][0:128, 0:TG], 1.0,
                                         SGB[:, ic * 512:ic * 512 + 512],
                                         OP.mult, OP.mult),
                                     waits=[tok, sg_free[ic],
                                            hm_free.get((key, ic), T0)])
                            if cbc is not None:
                                hmt = V_(lambda e, ic=ic, HBUF=HBUF: e.tensor_mul(
                                    HBUF[:, ic * 512:ic * 512 + 512],
                                    HBUF[:, ic * 512:ic * 512 + 512],
                                    PS[4][0:128, 0:TG]), waits=[hmt, cbc_tok])
                            consume(bi, hmt)
                            hm_free[(key, ic)] = hmt
                    guw_free[slot] = [tok]
            if cbc is not None:
                consume(4, hm_free[(key, NI // 128 - 1)])
            return

        for g in range(NTG):
            gu_block(g, h_wsg, h_wsu, 0, 1024, HMS, "s", None)
            for e_ in range(8):
                gu_block(g, h_wg, h_wu, e_ * D * 512, 512, HM[e_], "e%d" % e_, e_)
            dn = [("s", HMS, ic, h_wsd.tensor, ic * 128 * 1024) for ic in range(8)]
            for e_ in range(8):
                dn += [("e%d" % e_, HM[e_], ic, h_wd.tensor,
                        e_ * 512 * 1024 + ic * 128 * 1024) for ic in range(4)]
            z_toks = [None] * 8
            for di, (key, HBUF, ic, wt, woff) in enumerate(dn):
                slot = di % 2
                ldt = wdd.load(bass.AP(WD[slot], 0, [[1024, 128], [1, 1024]]),
                               bass.AP(wt, woff, [[1024, 128], [1, 1024]]),
                               waits=list(wd_free[slot]), slot=slot)
                lastmm = None
                for nc_ in range(8):
                    if nc_ == 0:
                        w0 = [ldt, hm_free[(key, ic)]]
                    else:
                        w0 = []
                    if di == 0:
                        w0 = w0 + [bank_free[nc_]]
                    lastmm = p.emit(
                        "tensor",
                        lambda e, nc_=nc_, slot=slot, HBUF=HBUF, ic=ic,
                        st=(di == 0), sp=(di == len(dn) - 1): e.matmul(
                            PS[nc_][0:128, 0:TG],
                            WD[slot][:, nc_ * 128:nc_ * 128 + 128],
                            HBUF[:, ic * 512:ic * 512 + 512], start=st, stop=sp),
                        waits=w0,
                        inc=("tE" if (nc_ == 7 or di == len(dn) - 1) else None))
                    if di == len(dn) - 1:
                        z_toks[nc_] = lastmm
                wd_free[slot] = [lastmm]
            for nc_ in range(8):
                xt = V_(lambda e, nc_=nc_, g=g: e.scalar_tensor_tensor(
                    X[:, T * nc_ + TG * g:T * nc_ + TG * g + TG],
                    PS[nc_][0:128, 0:TG], 0.5,
                    X[:, T * nc_ + TG * g:T * nc_ + TG * g + TG], OP.mult, OP.add),
                    waits=[z_toks[nc_], lastx])
                consume(nc_, xt)
                lastx = xt
            for key, HBUF, ic, wt, woff in dn:
                hm_free[(key, ic)] = z_toks[7]
            for ic in range(8):
                sg_free[ic] = z_toks[7]
        state["x"] = lastx
        sclose()
        sclose()

    def sub_l1attn():
        bar = barrier()
        st_tok = stats(X, state["x"], rms_eps=1e-6)
        AUX, = sopen(("l5AUX", [128, NC * T], F32))
        lnx_tok = materialize_lnx(AUX, st_tok, bar=bar)
        xn_tok = materialize_xn(AUX, lnx_tok, RRC, SM["l1w"], None)
        lastx = state["x"]
        for c in range(NC):
            lastx = V_(lambda e, c=c: e.tensor_add(
                X[:, T * c:T * c + T], X[:, T * c:T * c + T],
                AUX[:, T * c:T * c + T]), waits=[lnx_tok, lastx])
        state["x"] = lastx
        sclose()
        bar2 = barrier()
        FB, OB, GBF, SB_, WSL0, WSL1, WSMB, WSMF = sopen(
            ("l5FB", [128, T], F32), ("l5OB", [128, T], F32),
            ("l5GB", [128, T], F32), ("l5SB", [128, T], F32),
            ("l5W0", [128, 8 * 128], BF16), ("l5W1", [128, 8 * 128], BF16),
            ("l5WSM", [128, 8], BF16), ("l5WSF", [128, 8], F32))
        WSL = [WSL0, WSL1]
        wl = Dma(p, "gpsimd", "l5wl", n_slots=2)
        wl_free = [list(bar2), list(bar2)]
        fb_free = list(bar2)
        ob_free = list(bar2)
        gb_free = list(bar2)
        sb_free = list(bar2)
        wsm_free = T0
        sctr = [0]

        def wload(wh, c):
            slot = sctr[0] % 2
            sctr[0] += 1
            t = wl.load(bass.AP(WSL[slot], 0, [[8 * 128, 128], [128, NC], [1, 128]]),
                        bass.AP(wh.tensor, c * 128, [[D, 128], [128 * D, NC], [1, 128]]),
                        waits=list(wl_free[slot]), slot=slot)
            return slot, t

        for c in range(NC):
            slot, ldt = wload(h_wf, c)
            fts = []
            for g in range(NTG):
                bi = g % 2
                mms = [(WSL[slot][:, kc * 128:kc * 128 + 128],
                        XN[:, T * kc + TG * g:T * kc + TG * g + TG])
                       for kc in range(NC)]
                tok = mm_group(bi, mms, waits=[ldt, xn_tok])
                ft = S_(lambda e, bi=bi, g=g: e.activation(
                    out=FB[:, TG * g:TG * g + TG], in_=PS[bi][0:128, 0:TG],
                    func=AF.Sigmoid), waits=[tok] + (fb_free if g == 0 else []))
                consume(bi, ft)
                fts.append(ft)
            wl_free[slot] = [tok]
            fmax = V_(lambda e: e.tensor_scalar(FB[:], FB[:], 0.9, None, OP.max),
                      waits=[fts[-1]])
            slot, ldt = wload(h_wo, c)
            ots = []
            for g in range(NTG):
                bi = g % 2
                mms = [(WSL[slot][:, kc * 128:kc * 128 + 128],
                        XN[:, T * kc + TG * g:T * kc + TG * g + TG])
                       for kc in range(NC)]
                tok = mm_group(bi, mms, waits=[ldt, xn_tok])
                ot = S_(lambda e, bi=bi, g=g: e.activation(
                    out=OB[:, TG * g:TG * g + TG], in_=PS[bi][0:128, 0:TG],
                    func=AF.Sigmoid), waits=[tok] + (ob_free if g == 0 else []))
                consume(bi, ot)
                ots.append(ot)
            wl_free[slot] = [tok]
            slot, ldt = wload(h_wi, c)
            wst = None
            for kc in range(NC):
                wst = V_(lambda e, kc=kc, slot=slot: e.reduce_sum(
                    WSMF[:, kc:kc + 1], WSL[slot][:, kc * 128:kc * 128 + 128],
                    axis=AX.X), waits=[ldt, wsm_free])
            wst = V_(lambda e: e.tensor_copy(WSMB[:], WSMF[:]), waits=[wst])
            wl_free[slot] = [wst]
            ginit = V_(lambda e: e.tensor_scalar(GBF[:], FB[:], -1.0, 1.0,
                                                 OP.mult, OP.add),
                       waits=[fmax] + gb_free)
            glast = None
            for g in range(NTG):
                tokv = mm_group(6, [(WSMB[:, kc:kc + 1],
                                     XN[:, T * kc + TG * g:T * kc + TG * g + TG],
                                     PS[6][0:1, 0:TG]) for kc in range(NC)],
                                waits=[wst, xn_tok])
                cpv = V_(lambda e: e.tensor_copy(ROWS[0:1, 0:TG], PS[6][0:1, 0:TG]),
                         waits=[tokv, rows_free[0]])
                consume(6, cpv)
                tokb = mm_group(5, [(C["r064f"][0:1, 0:128], ROWS[0:1, 0:TG],
                                     PS[5][0:128, 0:TG])], waits=[cpv])
                rows_free[0] = tokb
                glast = V_(lambda e, g=g: e.tensor_mul(
                    GBF[:, TG * g:TG * g + TG], GBF[:, TG * g:TG * g + TG],
                    PS[5][0:128, 0:TG]), waits=[tokb, ginit])
                consume(5, glast)
            wsm_free = glast
            sc = V_(lambda e: e.tensor_tensor_scan(SB_[:], FB[:], GBF[:], 0.0,
                                                   OP.mult, OP.add),
                    waits=[glast, fmax] + sb_free)
            fb_free = [sc]
            gb_free = [sc]
            yv = V_(lambda e: e.tensor_mul(SB_[:], SB_[:], OB[:]), waits=[sc, ots[-1]])
            ob_free = [yv]
            lastx = V_(lambda e, c=c: e.tensor_add(
                X[:, T * c:T * c + T], X[:, T * c:T * c + T], SB_[:]),
                waits=[yv, lastx])
            sb_free = [lastx]
        state["x"] = lastx
        sclose()

    if n_sub >= 3:
        sub_l0ffn("f3", t1, t2)
    if n_sub >= 4:
        sub_l1ffn("m4")
    if n_sub >= 5:
        sub_l1attn()
    if n_sub >= 6:
        sub_l1ffn("m6")


    st = dX.load(bass.AP(out_h.tensor, 0, [[T, 128], [128 * T, NC], [1, T]]),
                 bass.AP(X, 0, [[NC * T, 128], [T, NC], [1, T]]), waits=[state["x"]],
                 slot=0)
    p.emit("sync", None, waits=[st])

    with nc.Block() as block:
        @block.sync
        def _(e):
            p.replay("sync", e)

        @block.tensor
        def _(e):
            p.replay("tensor", e)

        @block.vector
        def _(e):
            p.replay("vector", e)

        @block.scalar
        def _(e):
            p.replay("scalar", e)

        @block.gpsimd
        def _(e):
            p.replay("gpsimd", e)

    es.close()
    return nc


def kernel(**inputs):
    in_maps = _host_inputs(inputs)
    nc = build()
    if _RUN is not None:
        results = _RUN(nc, in_maps)
    else:
        res = run_bass_kernel_spmd(nc, in_maps, core_ids=list(range(8)))
        results = res.results
    out = np.stack([np.asarray(r["out"]).T for r in results])
    return np.ascontiguousarray(out.astype(np.float32))


# version 02
# speedup vs baseline: 1.0538x; 1.0538x over previous
import numpy as np
import ml_dtypes
import concourse.bass as bass
import concourse.mybir as mybir
from concourse.bass_utils import run_bass_kernel_spmd

F32 = mybir.dt.float32
BF16 = mybir.dt.bfloat16
I32 = mybir.dt.int32
AF = mybir.ActivationFunctionType
OP = mybir.AluOpType
AX = mybir.AxisListType

D = 1024
T = 2048
NC = 8
TG = 512
NTG = T // TG
HEADS = 16
NEGBIG = -1073741824.0

_N_SUB = 6
_RUN = None



def _t(a):
    return np.ascontiguousarray(np.asarray(a, np.float32).T)


def _cols(v, n):
    return np.ascontiguousarray(np.asarray(v, np.float32).reshape(n, 128).T)


def _consts():
    c = {}
    c["ident_f"] = np.eye(128, dtype=np.float32)
    itop = np.zeros((128, 128), dtype=ml_dtypes.bfloat16)
    ibot = np.zeros((128, 128), dtype=ml_dtypes.bfloat16)
    for q in range(64):
        itop[q, q] = 1
        itop[64 + q, q] = 1
        ibot[q, 64 + q] = 1
        ibot[64 + q, 64 + q] = 1
    c["itop"] = itop
    c["ibot"] = ibot
    m1 = np.where(np.arange(128)[None, :] > np.arange(128)[:, None], NEGBIG, 0.0)
    m1 = m1.astype(ml_dtypes.bfloat16)
    c["m1a"] = np.concatenate([m1[0:64], m1[0:64]], axis=0)
    c["m1b"] = np.concatenate([m1[64:128], m1[64:128]], axis=0)
    m2 = np.where(np.arange(128)[None, :] < np.arange(128)[:, None], NEGBIG, 0.0)
    m2 = m2.astype(ml_dtypes.bfloat16)
    c["m2a"] = np.concatenate([m2[0:64], m2[0:64]], axis=0)
    c["m2b"] = np.concatenate([m2[64:128], m2[64:128]], axis=0)
    r064 = np.zeros((128, 128), np.float32)
    r064[0, :] = 1.0
    r064[64, :] = 1.0
    c["r064f"] = r064
    c["r064b"] = r064.astype(ml_dtypes.bfloat16)
    c["invd"] = np.full((128, 1), 1.0 / D, np.float32)
    c["eps5"] = np.full((128, 1), 1e-5, np.float32)
    c["eps6"] = np.full((128, 1), 1e-6, np.float32)
    c["onescol"] = np.ones((128, 1), np.float32)
    c["onesrow"] = np.ones((1, 512), np.float32)
    c["iota8"] = np.broadcast_to(np.arange(8, dtype=np.float32), (128, 8)).copy()
    c["c999"] = np.full((128, 8), 999.0, np.float32)
    c["cneg"] = np.full((128, 8), -1e30, np.float32)
    sel8 = np.zeros((8, 8 * 128), np.float32)
    for e in range(8):
        sel8[e, e * 128:(e + 1) * 128] = 1.0
    c["sel8"] = sel8
    c["sel8b"] = sel8.astype(ml_dtypes.bfloat16)
    return c


CONST_SPECS = [
    ("ident_f", [128, 128], F32), ("itop", [128, 128], BF16), ("ibot", [128, 128], BF16),
    ("m1a", [128, 128], BF16), ("m1b", [128, 128], BF16),
    ("m2a", [128, 128], BF16), ("m2b", [128, 128], BF16),
    ("r064f", [128, 128], F32), ("r064b", [128, 128], BF16),
    ("invd", [128, 1], F32), ("eps5", [128, 1], F32), ("eps6", [128, 1], F32),
    ("onescol", [128, 1], F32), ("onesrow", [1, 512], F32),
    ("iota8", [128, 8], F32), ("c999", [128, 8], F32), ("cneg", [128, 8], F32),
]

GEN_MATS = (("ca", 1024, 3072), ("cp", 1024, 1024), ("cf", 1024, 4096), ("cm", 4096, 1024))


def _host_inputs(inp):
    cst = _consts()
    shared = {}
    for nm, I_, O in GEN_MATS:
        key = "L0_attn_" + nm
        sub = np.asarray(inp[key + "_sub_w"], np.float32)
        shared[nm + "_s0"] = _t(sub[0])
        shared[nm + "_s1"] = _t(sub[1])
        shared[nm + "_W"] = _t(inp[key + "_W_w"])
        shared[nm + "_B"] = _t(inp[key + "_B_w"])
        wl = np.asarray(inp[key + "_Wl_w"], np.float32)
        shared[nm + "_wl"] = np.broadcast_to(wl[0][None, :], (128, 2)).copy()
        nco = O // 128
        shared[nm + "_sb"] = np.concatenate(
            [_cols(inp[key + "_sub_b"][0], nco), _cols(inp[key + "_sub_b"][1], nco)], axis=1)
        shared[nm + "_Wb"] = _cols(inp[key + "_W_b"], nco)
        shared[nm + "_Bb"] = _cols(inp[key + "_B_b"], nco)
        wlb = np.asarray(inp[key + "_Wl_b"], np.float32)
        shared[nm + "_wlb"] = np.broadcast_to(wlb[0][None, :], (128, 2)).copy()
    shared["ln1g"] = _cols(inp["L0_attn_ln1_g"], NC)
    shared["ln1b"] = _cols(inp["L0_attn_ln1_b"], NC)
    shared["ln2g"] = _cols(inp["L0_attn_ln2_g"], NC)
    shared["ln2b"] = _cols(inp["L0_attn_ln2_b"], NC)
    shared["l0g"] = _cols(inp["L0_ffn_ln_gamma"], NC)
    shared["l0b"] = _cols(inp["L0_ffn_ln_beta"], NC)
    shared["fc1q"] = np.ascontiguousarray(np.asarray(inp["L0_ffn_fc1_q"], np.int32))
    shared["fc1s"] = np.ascontiguousarray(np.asarray(inp["L0_ffn_fc1_scales"], np.float32))
    shared["fc1b"] = _cols(inp["L0_ffn_fc1_bias"], 32)
    shared["fc2q"] = np.ascontiguousarray(np.asarray(inp["L0_ffn_fc2_q"], np.int32))
    s2 = np.asarray(inp["L0_ffn_fc2_scales"], np.float32)
    shared["fc2s"] = np.ascontiguousarray(
        s2.reshape(4, 8, 1024).transpose(1, 0, 2).reshape(8, 4096))
    shared["fc2brow"] = np.asarray(inp["L0_ffn_fc2_bias"], np.float32).reshape(1, 1024).copy()
    shared["l1w"] = _cols(inp["L1_attn_norm_w"], NC)
    shared["wf"] = _t(inp["L1_attn_w_forget"])
    shared["wi"] = _t(inp["L1_attn_w_input"])
    shared["wo"] = _t(inp["L1_attn_w_output"])
    shared["gw"] = np.ascontiguousarray(
        _t(inp["L1_ffn_gate_weight"]).reshape(8, 128, 8).transpose(1, 0, 2).reshape(128, 64))
    shared["wg"] = np.ascontiguousarray(np.asarray(inp["L1_ffn_w_gate"], np.float32).transpose(0, 2, 1))
    shared["wu"] = np.ascontiguousarray(np.asarray(inp["L1_ffn_w_up"], np.float32).transpose(0, 2, 1))
    shared["wd"] = np.ascontiguousarray(np.asarray(inp["L1_ffn_w_down"], np.float32).transpose(0, 2, 1))
    shared["wsg"] = _t(inp["L1_ffn_ws_gate"])
    shared["wsu"] = _t(inp["L1_ffn_ws_up"])
    shared["wsd"] = _t(inp["L1_ffn_ws_down"])
    shared.update(cst)

    x = np.asarray(inp["L0_attn_hs"], np.float32)
    in_maps = []
    for c in range(8):
        m = dict(shared)
        m["x"] = _t(x[c])
        in_maps.append(m)
    return in_maps



ENGS = ("sync", "tensor", "vector", "scalar", "gpsimd")
T0 = ("tE", 0)


class P:
    def __init__(self, nc, es):
        self.nc = nc
        self.es = es
        self.q = {e: [] for e in ENGS}
        self.cnt = {}
        self.sems = {}

    def sem(self, name):
        if name not in self.sems:
            self.sems[name] = self.es.enter_context(self.nc.semaphore(name))
            self.cnt[name] = 0
        return name

    def emit(self, eng, fn, waits=(), inc=None, n=1):
        ws = []
        for s, v in waits:
            if v <= 0:
                continue
            assert v <= self.cnt.get(s, 0), f"wait on future {s}:{v}>{self.cnt.get(s, 0)}"
            ws.append((s, v))
        self.q[eng].append((ws, fn, inc, n))
        if inc is not None:
            self.sem(inc)
            self.cnt[inc] += n
            return (inc, self.cnt[inc])
        return None

    def replay(self, ename, engine):
        maxw = {}
        for ws, fn, inc, n in self.q[ename]:
            for s, v in ws:
                if maxw.get(s, 0) < v:
                    engine.wait_ge(self.sems[s], v)
                    maxw[s] = v
            if fn is None:
                continue
            inst = fn(engine)
            if inc is not None:
                inst.then_inc(self.sems[inc], n)


class Dma:

    def __init__(self, p, eng, name, n_slots=1):
        self.p = p
        self.eng = eng
        self.sems = [p.sem(f"{name}{i}") for i in range(n_slots)]

    def load(self, dst, src, waits=(), slot=0):
        return self.p.emit(self.eng, lambda e, d=dst, s=src: e.dma_start(out=d, in_=s),
                           waits=waits, inc=self.sems[slot], n=16)



def build(n_sub=None):
    if n_sub is None:
        n_sub = _N_SUB
    nc = bass.Bass("TRN2", target_bir_lowering=False)
    from contextlib import ExitStack
    es = ExitStack()
    p = P(nc, es)

    def din(name, shape, dt=F32):
        return nc.dram_tensor(name, shape, dt, kind="ExternalInput").ap()

    x_h = din("x", [D, T])
    out_h = nc.dram_tensor("out", [D, T], F32, kind="ExternalOutput").ap()
    cst_h = {n_: din(n_, sh, dt) for n_, sh, dt in CONST_SPECS}
    h_sel8 = din("sel8", [8, 8 * 128])
    genh = {}
    for nm, I_, O in GEN_MATS:
        nco = O // 128
        genh[nm] = dict(
            s0=din(nm + "_s0", [I_, O]), s1=din(nm + "_s1", [I_, O]),
            W=din(nm + "_W", [I_, O]), B=din(nm + "_B", [I_, O]),
            wl=din(nm + "_wl", [128, 2]), sb=din(nm + "_sb", [128, 2 * nco]),
            Wb=din(nm + "_Wb", [128, nco]), Bb=din(nm + "_Bb", [128, nco]),
            wlb=din(nm + "_wlb", [128, 2]), O=O, I=I_, nco=nco)
    smh = {n_: din(n_, sh) for n_, sh in [
        ("ln1g", [128, NC]), ("ln1b", [128, NC]), ("ln2g", [128, NC]), ("ln2b", [128, NC]),
        ("l0g", [128, NC]), ("l0b", [128, NC]), ("fc1b", [128, 32]),
        ("l1w", [128, NC]), ("gw", [128, 64])]}
    h_fc1s = din("fc1s", [8, 4096])
    h_fc2s = din("fc2s", [8, 4096])
    h_fc2brow = din("fc2brow", [1, 1024])
    h_fc1q = din("fc1q", [D, 4096], I32)
    h_fc2q = din("fc2q", [4096, D], I32)
    h_wf = din("wf", [D, D]); h_wi = din("wi", [D, D]); h_wo = din("wo", [D, D])
    h_wg = din("wg", [8, D, 512]); h_wu = din("wu", [8, D, 512]); h_wd = din("wd", [8, 512, D])
    h_wsg = din("wsg", [D, D]); h_wsu = din("wsu", [D, D]); h_wsd = din("wsd", [D, D])

    s_fc1 = nc.dram_tensor("s_fc1", [D, 4096], BF16, kind="Internal").ap()
    s_fc2 = nc.dram_tensor("s_fc2", [4096, D], BF16, kind="Internal").ap()
    s_cf = nc.dram_tensor("s_cf", [D, 4096], BF16, kind="Internal").ap()
    s_cm = nc.dram_tensor("s_cm", [4096, D], BF16, kind="Internal").ap()
    s_qkv = nc.dram_tensor("s_qkv", [D, 3072], BF16, kind="Internal").ap()
    s_cp = nc.dram_tensor("s_cp", [D, D], BF16, kind="Internal").ap()
    s_lnx = nc.dram_tensor("s_lnx", [D, T], F32, kind="Internal").ap()
    s_v = nc.dram_tensor("s_v", [T, HEADS * 65], BF16, kind="Internal").ap()

    def sb(name, shape, dt):
        return es.enter_context(nc.sbuf_tensor(name, shape, dt))

    X = sb("X", [128, NC * T], F32)
    XN = sb("XN", [128, NC * T], BF16)
    C = {n_: sb("c_" + n_, sh, dt) for n_, sh, dt in CONST_SPECS}
    SM = {n_: sb("p_" + n_, sh, F32) for n_, sh in [
        ("ln1g", [128, NC]), ("ln1b", [128, NC]), ("ln2g", [128, NC]), ("ln2b", [128, NC]),
        ("l0g", [128, NC]), ("l0b", [128, NC]), ("fc1b", [128, 32]),
        ("l1w", [128, NC]), ("gw", [128, 64])]}
    GB = {}
    for nm, I_, O in GEN_MATS:
        nco = O // 128
        GB[nm] = dict(sb=sb(nm + "sb", [128, 2 * nco], F32), Wb=sb(nm + "Wb", [128, nco], F32),
                      Bb=sb(nm + "Bb", [128, nco], F32), wl=sb(nm + "wl", [128, 2], F32),
                      wlb=sb(nm + "wlb", [128, 2], F32), bias=sb(nm + "bias", [128, nco], F32))
    ROWS = sb("ROWS", [1, 2048], F32)
    STC = sb("STC", [128, 256], F32)
    SCR = sb("SCR", [128, 1024], F32)
    EPI = sb("EPI", [128, 512], F32)
    NEGBC = sb("NEGBC", [128, 512], BF16)
    PS = [es.enter_context(nc.psum_tensor(f"PS{i}", [128, 512], F32)) for i in range(8)]

    scope_stack = []

    def sopen(*specs):
        cms = []
        ts = []
        for nm_, sh_, dt_ in specs:
            cm = nc.sbuf_tensor(nm_, sh_, dt_)
            ts.append(cm.__enter__())
            cms.append(cm)
        scope_stack.append(cms)
        return ts

    def sclose():
        cms = scope_stack.pop()
        for cm in reversed(cms):
            cm.__exit__(None, None, None)

    def barrier():
        return [(sname, p.cnt[sname]) for sname in list(p.cnt)]

    dW = Dma(p, "sync", "dW")
    dX = Dma(p, "sync", "dX")
    dS = Dma(p, "scalar", "dS")
    dG = Dma(p, "gpsimd", "dG")
    for s in ("tE", "vE", "sE"):
        p.sem(s)

    def TT_(fn, waits=(), inc="tE"):
        return p.emit("tensor", fn, waits=waits, inc=inc)

    def V_(fn, waits=(), inc="vE"):
        return p.emit("vector", fn, waits=waits, inc=inc)

    def S_(fn, waits=(), inc="sE"):
        return p.emit("scalar", fn, waits=waits, inc=inc)

    bank_free = [T0] * 8
    prep_tasks = []

    def pump(k=1):
        for _ in range(min(k, len(prep_tasks))):
            prep_tasks.pop(0)()

    def pump_all():
        while prep_tasks:
            prep_tasks.pop(0)()

    def mm_group(bi, mms, waits=(), out=None):
        w = list(waits) + [bank_free[bi]]
        n = len(mms)
        tok = None
        for i, mm in enumerate(mms):
            lh, rh = mm[0], mm[1]
            oa = mm[2] if len(mm) > 2 else PS[bi][:]
            tok = p.emit("tensor",
                         lambda e, oa=oa, lh=lh, rh=rh, st=(i == 0), sp=(i == n - 1):
                         e.matmul(oa, lh, rh, start=st, stop=sp),
                         waits=(w if i == 0 else ()), inc=("tE" if i == n - 1 else None))
        return tok

    def consume(bi, tok):
        bank_free[bi] = tok

    x_tok = dX.load(bass.AP(X, 0, [[NC * T, 128], [T, NC], [1, T]]),
                    bass.AP(x_h.tensor, 0, [[T, 128], [128 * T, NC], [1, T]]))
    ctok = None
    for n_ in C:
        ctok = dW.load(C[n_][:], cst_h[n_])
    for n_ in SM:
        ctok = dW.load(SM[n_][:], smh[n_])
    for nm, I_, O in GEN_MATS:
        for k2 in ("sb", "Wb", "Bb", "wl", "wlb"):
            ctok = dW.load(GB[nm][k2][:], genh[nm][k2])
    CW = ctok

    gbias = {}
    for nm, I_, O in GEN_MATS:
        b = GB[nm]
        nco = O // 128
        t1 = S_(lambda e, b=b, nco=nco: e.activation(
            out=b["bias"][:, 0:nco], in_=b["sb"][:, nco:2 * nco], func=AF.Copy,
            scale=b["wlb"][:, 1:2]), waits=[CW])
        t2 = V_(lambda e, b=b, nco=nco: e.scalar_tensor_tensor(
            b["bias"][:, 0:nco], b["sb"][:, 0:nco], b["wlb"][:, 0:1],
            b["bias"][:, 0:nco], OP.mult, OP.add), waits=[t1, CW])
        t3 = S_(lambda e, b=b, nco=nco: e.activation(
            out=b["bias"][:, 0:nco], in_=b["bias"][:, 0:nco], func=AF.Tanh), waits=[t2])
        t4 = V_(lambda e, b=b, nco=nco: e.scalar_tensor_tensor(
            b["bias"][:, 0:nco], b["bias"][:, 0:nco], 1.0, b["Wb"][:, 0:nco],
            OP.mult, OP.mult), waits=[t3])
        t5 = V_(lambda e, b=b, nco=nco: e.tensor_add(
            b["bias"][:, 0:nco], b["bias"][:, 0:nco], b["Bb"][:, 0:nco]), waits=[t4])
        gbias[nm] = t5
    gbias["ca"] = V_(lambda e: e.tensor_scalar_mul(
        GB["ca"]["bias"][:, 0:8], GB["ca"]["bias"][:, 0:8], 0.125), waits=[gbias["ca"]])

    rows_free = [T0] * 4

    def col_to_bcast(col_of, n_cols, bi_t, bi_bc, rslot, waits=(), out_bf16=None):
        w = list(waits) + [bank_free[bi_t]]
        tok = None
        for j in range(n_cols):
            tok = p.emit("tensor",
                         lambda e, j=j: e.transpose(
                             PS[bi_t][0:1, 128 * j:128 * (j + 1)], col_of(j), C["ident_f"][:]),
                         waits=(w if j == 0 else ()), inc=("tE" if j == n_cols - 1 else None))
        tokc = V_(lambda e: e.tensor_copy(
            ROWS[0:1, 512 * rslot:512 * rslot + 128 * n_cols],
            PS[bi_t][0:1, 0:128 * n_cols]), waits=[tok, rows_free[rslot]])
        consume(bi_t, tokc)
        tokb = mm_group(bi_bc, [(C["r064f"][0:1, 0:128],
                                 ROWS[0:1, 512 * rslot:512 * rslot + 128 * n_cols],
                                 PS[bi_bc][0:128, 0:128 * n_cols])], waits=[tokc])
        rows_free[rslot] = tokb
        if out_bf16 is not None:
            toko = V_(lambda e: e.tensor_copy(out_bf16, PS[bi_bc][0:128, 0:128 * n_cols]),
                      waits=[tokb])
            consume(bi_bc, toko)
            return tokb, toko
        return tokb, None

    SUMC, SQC, VARC, R1C, TMPC, R2C, RHC, MR1C, RRC = (0, 16, 32, 48, 64, 80, 96, 112, 128)

    def stats(buf, buf_tok, eps2=None, rms_eps=None):
        last_st = None
        for g in range(NTG):
            pump(1)
            tok_s = mm_group(6, [(C["invd"][:, 0:1],
                                  buf[:, T * c + TG * g:T * c + TG * g + TG],
                                  PS[6][0:1, 0:TG]) for c in range(NC)], waits=[buf_tok])
            tok_q = None
            for wave in range(4):
                sq_tok = None
                for cc in range(2):
                    c = 2 * wave + cc
                    sq_tok = S_(lambda e, c=c, cc=cc, g=g: e.activation(
                        out=SCR[:, 512 * cc:512 * cc + 512],
                        in_=buf[:, T * c + TG * g:T * c + TG * g + TG], func=AF.Square),
                        waits=[buf_tok, tok_q or state.get("scr_free", T0)]
                              + ([last_st] if last_st else []))
                w0 = [sq_tok] + ([bank_free[7]] if wave == 0 else [])
                for cc in range(2):
                    tok_q = p.emit("tensor",
                                   lambda e, cc=cc, st=(wave == 0 and cc == 0),
                                   sp=(wave == 3 and cc == 1): e.matmul(
                                       PS[7][0:1, 0:TG], C["invd"][:, 0:1],
                                       SCR[:, 512 * cc:512 * cc + 512],
                                       start=st, stop=sp),
                                   waits=(w0 if cc == 0 else ()),
                                   inc=("tE" if (cc == 1) else None))
            state["scr_free"] = tok_q
            cm1 = V_(lambda e: e.tensor_copy(ROWS[0:1, 0:TG], PS[6][0:1, 0:TG]),
                     waits=[tok_s, rows_free[0]])
            consume(6, cm1)
            cm2 = V_(lambda e: e.tensor_copy(ROWS[0:1, 512:512 + TG], PS[7][0:1, 0:TG]),
                     waits=[tok_q, rows_free[1]])
            consume(7, cm2)
            w1 = [bank_free[5], cm2]
            tokt = None
            for j in range(4):
                tokt = p.emit("tensor", lambda e, j=j: e.transpose(
                    PS[5][0:128, j:j + 1], ROWS[0:1, 128 * j:128 * (j + 1)],
                    C["ident_f"][0:1, 0:1]),
                    waits=(w1 if j == 0 else ()), inc=None)
            for j in range(4):
                tokt = p.emit("tensor", lambda e, j=j: e.transpose(
                    PS[5][0:128, 4 + j:5 + j], ROWS[0:1, 512 + 128 * j:512 + 128 * (j + 1)],
                    C["ident_f"][0:1, 0:1]), waits=(), inc=("tE" if j == 3 else None))
            st1 = V_(lambda e, g=g: e.tensor_copy(STC[:, SUMC + 4 * g:SUMC + 4 * g + 4],
                                                  PS[5][0:128, 0:4]), waits=[tokt])
            last_st = V_(lambda e, g=g: e.tensor_copy(STC[:, SQC + 4 * g:SQC + 4 * g + 4],
                                                      PS[5][0:128, 4:8]), waits=[tokt])
            consume(5, last_st)
            rows_free[0] = tokt
            rows_free[1] = tokt
        a = V_(lambda e: e.scalar_tensor_tensor(
            STC[:, TMPC:TMPC + 16], STC[:, SUMC:SUMC + 16], -1.0, STC[:, SUMC:SUMC + 16],
            OP.mult, OP.mult), waits=[last_st])
        a = V_(lambda e: e.tensor_add(STC[:, VARC:VARC + 16], STC[:, TMPC:TMPC + 16],
                                      STC[:, SQC:SQC + 16]), waits=[a])
        a = S_(lambda e: e.activation(out=STC[:, TMPC:TMPC + 16], in_=STC[:, VARC:VARC + 16],
                                      func=AF.Sqrt, bias=C["eps5"][:, 0:1]), waits=[a])
        a = V_(lambda e: e.reciprocal(STC[:, R1C:R1C + 16], STC[:, TMPC:TMPC + 16]), waits=[a])
        a = V_(lambda e: e.tensor_mul(STC[:, MR1C:MR1C + 16], STC[:, SUMC:SUMC + 16],
                                      STC[:, R1C:R1C + 16]), waits=[a])
        if eps2 is not None or rms_eps is not None:
            a = V_(lambda e: e.tensor_mul(STC[:, TMPC:TMPC + 16], STC[:, R1C:R1C + 16],
                                          STC[:, R1C:R1C + 16]), waits=[a])
            a = V_(lambda e: e.tensor_mul(STC[:, TMPC:TMPC + 16], STC[:, TMPC:TMPC + 16],
                                          STC[:, VARC:VARC + 16]), waits=[a])
            ee = eps2 if eps2 is not None else rms_eps
            tgt = R2C if eps2 is not None else RRC
            eps_ap = C["eps5"] if ee == 1e-5 else C["eps6"]
            a = S_(lambda e, eps_ap=eps_ap: e.activation(out=STC[:, TMPC:TMPC + 16],
                                                 in_=STC[:, TMPC:TMPC + 16],
                                                 func=AF.Sqrt, bias=eps_ap[:, 0:1]), waits=[a])
            a = V_(lambda e, tgt=tgt: e.reciprocal(STC[:, tgt:tgt + 16],
                                                   STC[:, TMPC:TMPC + 16]), waits=[a])
            if eps2 is not None:
                a = V_(lambda e: e.tensor_mul(STC[:, RHC:RHC + 16], STC[:, R1C:R1C + 16],
                                              STC[:, R2C:R2C + 16]), waits=[a])
        return a

    def stc_col(slot, g):
        return lambda j: STC[:, slot + 4 * g + j:slot + 4 * g + j + 1]

    state = {"x": x_tok}

    def materialize_lnx(AUX, stat_tok, bar=()):
        last = None
        for g in range(NTG):
            tr, _ = col_to_bcast(stc_col(R1C, g), 4, 5, 4, 0, waits=[stat_tok])
            tm, _ = col_to_bcast(stc_col(MR1C, g), 4, 5, 3, 1, waits=[stat_tok])
            for c in range(NC):
                tt = V_(lambda e, c=c, g=g: e.scalar_tensor_tensor(
                    AUX[:, T * c + TG * g:T * c + TG * g + TG],
                    X[:, T * c + TG * g:T * c + TG * g + TG], 1.0,
                    PS[4][0:128, 0:TG], OP.mult, OP.mult),
                    waits=[tr, state["x"], stat_tok] + list(bar))
                tt = V_(lambda e, c=c, g=g: e.tensor_sub(
                    AUX[:, T * c + TG * g:T * c + TG * g + TG],
                    AUX[:, T * c + TG * g:T * c + TG * g + TG],
                    PS[3][0:128, 0:TG]), waits=[tm, tt])
                last = tt
            consume(4, last)
            consume(3, last)
        return last

    def materialize_xn(SRC, src_tok, colslot, gamma, beta):
        last = None
        for g in range(NTG):
            tb, _ = col_to_bcast(stc_col(colslot, g), 4, 5, 4, 0, waits=[src_tok])
            for c in range(NC):
                tt = V_(lambda e, c=c, g=g: e.scalar_tensor_tensor(
                    XN[:, T * c + TG * g:T * c + TG * g + TG],
                    SRC[:, T * c + TG * g:T * c + TG * g + TG], 1.0,
                    PS[4][0:128, 0:TG], OP.mult, OP.mult), waits=[tb, src_tok])
                if gamma is not None:
                    if beta is not None:
                        tt = V_(lambda e, c=c, g=g: e.tensor_scalar(
                            XN[:, T * c + TG * g:T * c + TG * g + TG],
                            XN[:, T * c + TG * g:T * c + TG * g + TG],
                            gamma[:, c:c + 1], beta[:, c:c + 1], OP.mult, OP.add),
                            waits=[tt, CW])
                    else:
                        tt = V_(lambda e, c=c, g=g: e.tensor_scalar(
                            XN[:, T * c + TG * g:T * c + TG * g + TG],
                            XN[:, T * c + TG * g:T * c + TG * g + TG],
                            gamma[:, c:c + 1], None, OP.mult),
                            waits=[tt, CW])
                last = tt
            consume(4, last)
        return last

    dq = Dma(p, "gpsimd", "dq")
    dqs = Dma(p, "scalar", "dqs")
    dqsc = Dma(p, "sync", "dqsc", n_slots=2)

    def emit_dequant(DQ, DQO, SEL8, SCL, selw, qh, scales_h, scratch, n_k, n_n, blkmap, bar):
        dq_free = state.get("dq", T0)
        dqo_free = state.get("dqo", T0)
        scl_free = state.get("scl", T0)
        last_store = None
        for kc in range(n_k):
            tq = dq.load(DQ[:, 0:n_n],
                         bass.AP(qh.tensor, kc * 128 * n_n, [[n_n, 128], [1, n_n]]),
                         waits=[dq_free] + bar)
            selp, colb = blkmap(kc)
            td = None
            for j in range(n_n // 512):
                jj = j % 2
                tsc = dqsc.load(SCL[0:8, jj * 512:jj * 512 + 512],
                                bass.AP(scales_h.tensor, colb * n_n + j * 512,
                                        [[4096, 8], [1, 512]]),
                                waits=[state.get("sclb%d" % jj, T0)] + bar, slot=jj)
                tokb = mm_group(2, [(SEL8[0:8, selp * 128:selp * 128 + 128],
                                     SCL[0:8, jj * 512:jj * 512 + 512],
                                     PS[2][0:128, 0:512])], waits=[CW, tsc] + selw)
                state["sclb%d" % jj] = tokb
                td = V_(lambda e, j=j: e.scalar_tensor_tensor(
                    DQO[:, j * 512:(j + 1) * 512], DQ[:, j * 512:(j + 1) * 512],
                    -128.0, PS[2][0:128, 0:512], OP.add, OP.mult),
                    waits=[tokb, tq, dqo_free])
                consume(2, td)
            dq_free = td
            ts = dqs.load(bass.AP(scratch.tensor, kc * 128 * n_n, [[n_n, 128], [1, n_n]]),
                          DQO[:, 0:n_n], waits=[td])
            dqo_free = ts
            last_store = ts
        state["dq"] = dq_free
        state["dqo"] = dqo_free
        return last_store

    def emit_gen_tasks(nm, scratch, GS, GOUT, GSCR, bar, st_, scale_head=None):
        g = genh[nm]
        b = GB[nm]
        O = g["O"]
        I_ = g["I"]
        NRG = I_ // 1024
        genw = Dma(p, "gpsimd", "g_" + nm, n_slots=4)
        gens_ = Dma(p, "scalar", "gs_" + nm)
        NSL = O // 128

        def task(jj):
            j, rg = jj // NRG, jj % NRG
            tks = []
            for i, key in enumerate(("s0", "s1", "W", "B")):
                tks.append(genw.load(
                    bass.AP(GS[i], 0, [[1024, 128], [128, NC], [1, 128]]),
                    bass.AP(g[key].tensor, rg * 1024 * O + j * 128,
                            [[O, 128], [128 * O, NC], [1, 128]]),
                    waits=list(st_["gfree"][i]), slot=i))
            t1 = S_(lambda e: e.activation(out=GSCR[:, 0:1024], in_=GS[1][:, 0:1024],
                                           func=AF.Copy, scale=b["wl"][:, 1:2]),
                    waits=[tks[1], CW, st_["scr"]])
            t2 = V_(lambda e: e.scalar_tensor_tensor(
                GSCR[:, 0:1024], GS[0][:, 0:1024], b["wl"][:, 0:1], GSCR[:, 0:1024],
                OP.mult, OP.add), waits=[t1, tks[0]])
            t3 = S_(lambda e: e.activation(out=GSCR[:, 0:1024], in_=GSCR[:, 0:1024],
                                           func=AF.Tanh), waits=[t2])
            t4 = V_(lambda e: e.scalar_tensor_tensor(
                GSCR[:, 0:1024], GSCR[:, 0:1024], 1.0, GS[2][:, 0:1024],
                OP.mult, OP.mult), waits=[t3, tks[2]])
            gout_free = st_.get("gout", T0)
            if scale_head is not None and j < scale_head[0]:
                t5 = V_(lambda e: e.tensor_add(GSCR[:, 0:1024], GSCR[:, 0:1024],
                                               GS[3][:, 0:1024]), waits=[t4, tks[3]])
                t5 = V_(lambda e, f=scale_head[1]: e.tensor_scalar(
                    GOUT[:, 0:1024], GSCR[:, 0:1024], f, None, OP.mult),
                    waits=[t5, gout_free])
            else:
                t5 = V_(lambda e: e.tensor_add(
                    GOUT[:, 0:1024], GSCR[:, 0:1024], GS[3][:, 0:1024]),
                    waits=[t4, tks[3], gout_free])
            st_["scr"] = t5
            st_["gfree"] = [[t5]] * 4
            ts = gens_.load(bass.AP(scratch.tensor, rg * 1024 * O + j * 128,
                                    [[O, 128], [128 * O, NC], [1, 128]]),
                            bass.AP(GOUT, 0, [[1024, 128], [128, NC], [1, 128]]),
                            waits=[t5])
            st_["gout"] = ts
            state["gout_" + nm] = ts

        for jj in range(NSL * NRG):
            prep_tasks.append(lambda jj=jj: task(jj))

    def ffn_stream(pref, WS, WT, xn_tok, w1_scr, w1_tok, w2_scr, w2_tok,
                   b1_cols, b1_tok, act_fn, b2_row, b2_tok, out_scale, Y,
                   y_free0=T0, bar=(), b2_cols=None):
        w1d = Dma(p, "sync", pref + "w1", n_slots=2)
        w2d = Dma(p, "scalar", pref + "w2", n_slots=2)
        ws_free = [list(bar), list(bar)]
        wt_free = [list(bar), list(bar)]
        y_free = y_free0
        lastx = state["x"]
        for g in range(NTG):
            y_toks = []
            for io in range(16):
                pump(1)
                slot = io % 2
                ldt = w1d.load(
                    bass.AP(WS[slot], 0, [[2048, 128], [256, NC], [1, 256]]),
                    bass.AP(w1_scr.tensor, io * 256, [[4096, 128], [128 * 4096, NC], [1, 256]]),
                    waits=list(ws_free[slot]) + [w1_tok], slot=slot)
                tok = None
                for isub in range(2):
                    bi = isub % 2
                    mms = [(WS[slot][:, kc * 256 + isub * 128:kc * 256 + isub * 128 + 128],
                            XN[:, T * kc + TG * g:T * kc + TG * g + TG]) for kc in range(NC)]
                    tok = mm_group(bi, mms, waits=[ldt, xn_tok])
                    ic = io * 2 + isub
                    ytk = S_(lambda e, bi=bi, ic=ic: e.activation(
                        out=Y[:, ic * TG:(ic + 1) * TG], in_=PS[bi][0:128, 0:TG],
                        func=act_fn, bias=b1_cols[:, ic:ic + 1]),
                        waits=[tok, b1_tok, y_free])
                    consume(bi, ytk)
                    y_toks.append(ytk)
                ws_free[slot] = [tok]
            y_tok = y_toks[-1]
            first = [True] * 8
            slot_tok = {}
            for ks in range(8):
                pump(1)
                slot = ks % 2
                ldt = w2d.load(
                    bass.AP(WT[slot], 0, [[4096, 128], [1024, 4], [1, 1024]]),
                    bass.AP(w2_scr.tensor, ks * 512 * D, [[D, 128], [128 * D, 4], [1, 1024]]),
                    waits=list(wt_free[slot]) + [w2_tok], slot=slot)
                for nc_ in range(8):
                    w0 = [ldt, y_tok]
                    if first[nc_]:
                        w0 = w0 + [bank_free[nc_]]
                    for ksub in range(4):
                        ic = ks * 4 + ksub
                        is_last_of_ks = (nc_ == 7 and ksub == 3)
                        lastmm = (b2_cols is not None and ks == 7 and ksub == 3)
                        tokm = p.emit(
                            "tensor",
                            lambda e, nc_=nc_, slot=slot, ksub=ksub, ic=ic,
                            st=(first[nc_] and ksub == 0), sp=lastmm:
                            e.matmul(PS[nc_][0:128, 0:TG],
                                     WT[slot][:, ksub * 1024 + nc_ * 128:ksub * 1024 + nc_ * 128 + 128],
                                     Y[:, ic * TG:(ic + 1) * TG], start=st, stop=sp),
                            waits=(w0 if ksub == 0 else ()),
                            inc=("tE" if (is_last_of_ks or lastmm) else None))
                        if is_last_of_ks:
                            slot_tok[ks] = tokm
                        if lastmm:
                            slot_tok[(7, nc_)] = tokm
                    first[nc_] = False
                wt_free[slot] = [slot_tok[ks]]
            lasttok = None
            for nc_ in range(8):
                if b2_cols is None:
                    tok = p.emit("tensor",
                                 lambda e, nc_=nc_: e.matmul(
                                     PS[nc_][0:128, 0:TG],
                                     b2_row[0:1, nc_ * 128:nc_ * 128 + 128],
                                     C["onesrow"][0:1, 0:TG], start=False, stop=True),
                                 waits=([b2_tok] if nc_ == 0 else ()), inc="tE")
                else:
                    tok = slot_tok[(7, nc_)]
                xt = V_(lambda e, nc_=nc_, g=g: e.scalar_tensor_tensor(
                    X[:, T * nc_ + TG * g:T * nc_ + TG * g + TG],
                    PS[nc_][0:128, 0:TG], out_scale,
                    X[:, T * nc_ + TG * g:T * nc_ + TG * g + TG], OP.mult, OP.add),
                    waits=[tok, lastx])
                if b2_cols is not None:
                    xt = V_(lambda e, nc_=nc_, g=g: e.tensor_scalar(
                        X[:, T * nc_ + TG * g:T * nc_ + TG * g + TG],
                        X[:, T * nc_ + TG * g:T * nc_ + TG * g + TG],
                        b2_cols[1][:, nc_:nc_ + 1], None, OP.add),
                        waits=[xt, b2_cols[0]])
                consume(nc_, xt)
                lastx = xt
                lasttok = tok
            y_free = lasttok
        state["x"] = lastx
        return lastx

    def emit_fc_dequants(bar):
        DQ, DQO, SEL8, SCL = sopen(
            ("dqDQ", [128, 4096], F32), ("dqO", [128, 4096], BF16),
            ("dqSEL", [8, 8 * 128], F32), ("dqSCL", [8, 1024], F32))
        dsel = Dma(p, "sync", "dsel_fc")
        selt = dsel.load(SEL8[:], h_sel8)
        t1 = emit_dequant(DQ, DQO, SEL8, SCL, [selt], h_fc1q, h_fc1s, s_fc1, 8, 4096,
                          lambda kc: (kc, 0), bar)
        t2 = emit_dequant(DQ, DQO, SEL8, SCL, [selt], h_fc2q, h_fc2s, s_fc2, 32, 1024,
                          lambda kc: (kc % 8, kc // 8), bar)
        sclose()
        return t1, t2

    def sub_l0ffn(pref, fc1_tok, fc2_tok):
        bar = barrier()
        st_tok = stats(X, state["x"], eps2=1e-5)
        AUX, = sopen((pref + "AUX", [128, NC * T], F32))
        lnx_tok = materialize_lnx(AUX, st_tok, bar=bar)
        xn_tok = materialize_xn(AUX, lnx_tok, R2C, SM["l0g"], SM["l0b"])
        lastx = state["x"]
        for c in range(NC):
            lastx = V_(lambda e, c=c: e.scalar_tensor_tensor(
                X[:, T * c:T * c + T], AUX[:, T * c:T * c + T], 0.5,
                X[:, T * c:T * c + T], OP.mult, OP.add), waits=[lnx_tok, lastx])
        state["x"] = lastx
        sclose()
        bar2 = barrier()
        Y, WSa, WSb, WTa, WTb = sopen(
            (pref + "Y", [128, 32 * TG], BF16),
            (pref + "WSa", [128, 2048], BF16), (pref + "WSb", [128, 2048], BF16),
            (pref + "WTa", [128, 4096], BF16), (pref + "WTb", [128, 4096], BF16))
        db2 = Dma(p, "sync", pref + "db2")
        b2t = db2.load(ROWS[0:1, 1024:2048], h_fc2brow,
                       waits=bar2 + [rows_free[2], rows_free[3]])
        rows_free[2] = b2t
        rows_free[3] = b2t
        ffn_stream(pref, [WSa, WSb], [WTa, WTb], xn_tok, s_fc1, fc1_tok, s_fc2,
                   fc2_tok, SM["fc1b"], CW, AF.Silu, ROWS[0:1, 1024:2048], b2t,
                   0.25, Y, y_free0=state["x"], bar=bar2)
        sclose()

    if n_sub >= 1:
        t1, t2 = emit_fc_dequants([])
    if n_sub >= 2:
        GSg = sopen(("gGS0", [128, 1024], F32), ("gGS1", [128, 1024], F32),
                    ("gGS2", [128, 1024], F32), ("gGS3", [128, 1024], F32),
                    ("gOUT", [128, 1024], BF16), ("gSCR", [128, 1024], F32))
        gbar = barrier()
        gen_st = {"gfree": [list(gbar)] * 4, "scr": T0}
        emit_gen_tasks("ca", s_qkv, GSg[0:4], GSg[4], GSg[5], gbar, gen_st,
                       scale_head=(8, 0.125))
        emit_gen_tasks("cp", s_cp, GSg[0:4], GSg[4], GSg[5], gbar, gen_st)
        emit_gen_tasks("cf", s_cf, GSg[0:4], GSg[4], GSg[5], gbar, gen_st)
        emit_gen_tasks("cm", s_cm, GSg[0:4], GSg[4], GSg[5], gbar, gen_st)
    if n_sub >= 1:
        sub_l0ffn("f1", t1, t2)


    def materialize_ln_full(SRC, src_tok, gamma, beta, stat_tok):
        last = None
        for g in range(NTG):
            tr, _ = col_to_bcast(stc_col(R1C, g), 4, 5, 4, 0, waits=[stat_tok])
            tm, _ = col_to_bcast(stc_col(MR1C, g), 4, 5, 3, 1, waits=[stat_tok])
            for c in range(NC):
                sl = SCR[:, 512 * (c % 2):512 * (c % 2) + 512]
                t1 = V_(lambda e, c=c, g=g, sl=sl: e.scalar_tensor_tensor(
                    sl, SRC[:, T * c + TG * g:T * c + TG * g + TG], 1.0,
                    PS[4][0:128, 0:TG], OP.mult, OP.mult),
                    waits=[tr, src_tok, state.get("scr_free", T0)]
                          + ([last] if last else []))
                t1 = V_(lambda e, c=c, sl=sl: e.tensor_sub(sl, sl, PS[3][0:128, 0:TG]),
                        waits=[tm, t1])
                t1 = V_(lambda e, c=c, g=g, sl=sl: e.tensor_scalar(
                    XN[:, T * c + TG * g:T * c + TG * g + TG], sl,
                    gamma[:, c:c + 1], beta[:, c:c + 1], OP.mult, OP.add),
                    waits=[t1, CW])
                last = t1
            consume(4, last)
            consume(3, last)
        state["scr_free"] = last
        return last

    def sub_l0attn():
        CH = HEADS // 2
        dLx = Dma(p, "scalar", "dLx")
        bar = barrier()
        st_tok = stats(X, state["x"], eps2=1e-5)
        AUX, = sopen(("a2AUX", [128, NC * T], F32))
        lnx_tok = materialize_lnx(AUX, st_tok, bar=bar)
        h1_tok = materialize_xn(AUX, lnx_tok, R2C, SM["ln1g"], SM["ln1b"])
        slnx = dLx.load(bass.AP(s_lnx.tensor, 0, [[T, 128], [128 * T, NC], [1, T]]),
                        bass.AP(AUX, 0, [[NC * T, 128], [T, NC], [1, T]]),
                        waits=[lnx_tok])
        sclose()
        bar2 = barrier()
        QF, KF, VST0, VST1, VTMh0, VTMh1, PT, QW0, QW1 = sopen(
            ("aQF", [128, NC * T], BF16), ("aKF", [128, NC * T], BF16),
            ("aVST0", [128, 8 * 65], BF16), ("aVST1", [128, 8 * 65], BF16),
            ("aVTMh0", [128, 16 * 65], BF16), ("aVTMh1", [128, 16 * 65], BF16),
            ("aPT", [128, 2 * 512], BF16),
            ("aQW0", [128, 8 * 256], BF16), ("aQW1", [128, 8 * 256], BF16))
        QW = [QW0, QW1]
        VTMh = [VTMh0, VTMh1]
        VSTs = [VST0, VST1]
        ones_tok = [V_(lambda e, i=i: e.memset(
            bass.AP(VSTs[i], 64, [[8 * 65, 128], [65, 8], [1, 1]]), 1.0),
            waits=bar2) for i in range(2)]
        vbrow_tok = None
        for grp in range(2):
            wvb = [bank_free[5], gbias["ca"]] + bar2
            tokt = None
            for j in range(4):
                jj = 4 * grp + j
                tokt = p.emit("tensor", lambda e, j=j, jj=jj: e.transpose(
                    PS[5][0:1, 128 * j:128 * (j + 1)], GB["ca"]["bias"][:, 16 + jj:17 + jj],
                    C["ident_f"][:]), waits=(wvb if j == 0 else ()),
                    inc=("tE" if j == 3 else None))
            vbrow_tok = V_(lambda e, grp=grp: e.tensor_copy(
                ROWS[0:1, 1024 + 512 * grp:1536 + 512 * grp], PS[5][0:1, 0:512]),
                waits=[tokt, rows_free[2 + grp]])
            consume(5, vbrow_tok)
            rows_free[2 + grp] = vbrow_tok
        qkvw = Dma(p, "sync", "qkvw", n_slots=2)
        qkv_done = None
        vstore = Dma(p, "scalar", "vst", n_slots=2)
        vst_free = [[ones_tok[0]], [ones_tok[1]]]
        for nsl in range(12):
            slot = nsl % 2
            ldt = qkvw.load(
                bass.AP(QW[slot], 0, [[8 * 256, 128], [256, NC], [1, 256]]),
                bass.AP(s_qkv.tensor, nsl * 256, [[3072, 128], [128 * 3072, NC], [1, 256]]),
                waits=list(state.get("qw%d" % slot, [T0])) + bar2, slot=slot)
            if nsl < 8:
                tok = None
                for nsub in range(2):
                    for g in range(NTG):
                        bi = (nsub * NTG + g) % 2
                        n_global = nsl * 256 + nsub * 128
                        mms = [(QW[slot][:, kc * 256 + nsub * 128:kc * 256 + nsub * 128 + 128],
                                XN[:, T * kc + TG * g:T * kc + TG * g + TG])
                               for kc in range(NC)]
                        tok = mm_group(bi, mms, waits=[ldt, h1_tok])
                        dst = QF if n_global < 1024 else KF
                        colg = n_global if n_global < 1024 else n_global - 1024
                        ch = colg // 128
                        etk = V_(lambda e, bi=bi, dst=dst, ch=ch, g=g,
                                 col=n_global // 128: e.tensor_scalar(
                            dst[:, ch * T + TG * g:ch * T + TG * g + TG],
                            PS[bi][0:128, 0:TG],
                            GB["ca"]["bias"][:, col:col + 1], None, OP.add),
                            waits=[tok, gbias["ca"]])
                        consume(bi, etk)
                qkv_done = tok
            else:
                vsl = nsl - 8
                for tt in range(16):
                    bi = tt % 2
                    w0 = [ldt, h1_tok, bank_free[bi]]
                    tokv = None
                    for kc in range(NC):
                        tokv = p.emit("tensor",
                                      lambda e, kc=kc, tt=tt, slot=slot, bi=bi,
                                      st=(kc == 0): e.matmul(
                                          PS[bi][0:128, 0:256],
                                          XN[:, T * kc + 128 * tt:T * kc + 128 * tt + 128],
                                          QW[slot][:, kc * 256:kc * 256 + 256],
                                          start=st, stop=False),
                                      waits=(w0 if kc == 0 else ()), inc=None)
                    tokv = p.emit("tensor",
                                  lambda e, vsl=vsl, bi=bi: e.matmul(
                                      PS[bi][0:128, 0:256], C["r064f"][0:1, 0:128],
                                      ROWS[0:1, 1024 + 256 * vsl:1024 + 256 * vsl + 256],
                                      start=False, stop=True),
                                  waits=[vbrow_tok], inc="tE")
                    vslot = tt % 2
                    evk = V_(lambda e, bi=bi, vslot=vslot: e.tensor_copy(
                        bass.AP(VSTs[vslot], 0, [[8 * 65, 128], [65, 4], [1, 64]]),
                        bass.AP(PS[bi], 0, [[512, 128], [64, 4], [1, 64]])),
                        waits=[tokv] + list(vst_free[vslot]))
                    consume(bi, evk)
                    tokst = vstore.load(
                        bass.AP(s_v.tensor, tt * 128 * 1040 + (4 * vsl) * 65,
                                [[1040, 128], [1, 4 * 65]]),
                        bass.AP(VSTs[vslot], 0, [[8 * 65, 128], [1, 4 * 65]]),
                        waits=[evk], slot=vslot)
                    vst_free[vslot] = [tokst]
                qkv_done = tokv
            state["qw%d" % slot] = [tok if nsl < 8 else tokv]
        vload = Dma(p, "sync", "vload", n_slots=2)
        vput = [T0, T0]
        negbc_free = T0
        o_first = [qkv_done]
        exp_toks = {}
        NEGH = [144, 160]
        p1_tasks = []

        def emit_p1_head(h):
            ch, b_ = h // 2, 64 * (h % 2)
            base = NEGH[h % 2]

            def tile_task(qt, kg):
                width = 512 if kg < qt // 4 else (qt % 4) * 128 + 128
                bi = (qt + kg) % 2
                mms = [(QF[b_:b_ + 64, ch * T + 128 * qt:ch * T + 128 * qt + 128],
                        KF[b_:b_ + 64, ch * T + 512 * kg:ch * T + 512 * kg + width],
                        PS[bi][0:128, 0:width])]
                if kg == qt // 4:
                    moff = (qt % 4) * 128
                    mms.append((C["itop"][b_:b_ + 64, :], C["m1a"][b_:b_ + 64, :],
                                PS[bi][0:128, moff:moff + 128]))
                    mms.append((C["ibot"][b_:b_ + 64, :], C["m1b"][b_:b_ + 64, :],
                                PS[bi][0:128, moff:moff + 128]))
                tok = mm_group(bi, mms, waits=[qkv_done])
                if kg == 0:
                    nm_tok = V_(lambda e, bi=bi, qt=qt, width=width: e.reduce_max(
                        STC[:, base + qt % 16:base + qt % 16 + 1],
                        PS[bi][0:128, 0:width], axis=AX.X, negate=True),
                        waits=[tok, state.get("negq_%d" % (h % 2), T0)])
                else:
                    t2 = V_(lambda e, bi=bi, width=width: e.reduce_max(
                        STC[:, 176:177], PS[bi][0:128, 0:width], axis=AX.X,
                        negate=True), waits=[tok, state.get("negq_tmp", T0)])
                    nm_tok = V_(lambda e, qt=qt: e.tensor_tensor(
                        STC[:, base + qt % 16:base + qt % 16 + 1],
                        STC[:, base + qt % 16:base + qt % 16 + 1],
                        STC[:, 176:177], OP.min), waits=[t2])
                    state["negq_tmp"] = nm_tok
                consume(bi, nm_tok)
                state["negq_%d" % (h % 2)] = nm_tok

            for qt in range(16):
                for kg in range(qt // 4 + 1):
                    p1_tasks.append(lambda qt=qt, kg=kg: tile_task(qt, kg))

        def p1_pump(k=1):
            for _ in range(min(k, len(p1_tasks))):
                p1_tasks.pop(0)()

        emit_p1_head(0)
        p1_pump(len(p1_tasks))
        for h in range(HEADS):
            ch, b_ = h // 2, 64 * (h % 2)
            vslot = h % 2
            vld = vload.load(
                bass.AP(VTMh[vslot], 0, [[16 * 65, 128], [65, 16], [1, 65]]),
                bass.AP(s_v.tensor, h * 65, [[1040, 128], [128 * 1040, 16], [1, 65]]),
                waits=[vput[vslot], vst_free[0][0], vst_free[1][0]], slot=vslot)
            p1_pump(len(p1_tasks))
            if h + 1 < HEADS:
                emit_p1_head(h + 1)
            p1_done = state["negq_%d" % (h % 2)]
            pv_tok = None
            for qg in range(4):
                base = NEGH[h % 2]
                tb, negbc_tok = col_to_bcast(
                    lambda j, base=base, qg=qg: STC[:, base + 4 * qg + j:base + 4 * qg + j + 1],
                    4, 4, 5, 2, waits=[p1_done, negbc_free],
                    out_bf16=NEGBC[:, 0:512])
                nkc = 4 * qg + 4
                pvb = 6 + (qg % 2)
                for kc in range(nkc):
                    p1_pump(1)
                    off = 128 * (kc - 4 * qg) if kc >= 4 * qg else 0
                    bi = 2 + kc % 2
                    mms = [(KF[b_:b_ + 64, ch * T + 128 * kc:ch * T + 128 * kc + 128],
                            QF[b_:b_ + 64, ch * T + 512 * qg + off:ch * T + 512 * qg + 512],
                            PS[bi][0:128, off:512]),
                           (C["r064b"][b_:b_ + 64, 0:128],
                            NEGBC[b_:b_ + 64, off:512],
                            PS[bi][0:128, off:512])]
                    if kc >= 4 * qg:
                        mms.append((C["itop"][b_:b_ + 64, :], C["m2a"][b_:b_ + 64, :],
                                    PS[bi][0:128, off:off + 128]))
                        mms.append((C["ibot"][b_:b_ + 64, :], C["m2b"][b_:b_ + 64, :],
                                    PS[bi][0:128, off:off + 128]))
                    tok = mm_group(bi, mms, waits=[negbc_tok, qkv_done])
                    pslot = kc % 2
                    etk = S_(lambda e, bi=bi, off=off, pslot=pslot: e.activation(
                        out=PT[0:128, 512 * pslot + off:512 * pslot + 512],
                        in_=PS[bi][0:128, off:512], func=AF.Exp),
                        waits=[tok, exp_toks.get(pslot, T0)])
                    consume(bi, etk)
                    pv_w = [etk, vld]
                    if kc == 0:
                        pv_w.append(bank_free[pvb])
                    pv_tok = p.emit("tensor",
                                    lambda e, kc=kc, off=off, pslot=pslot, pvb=pvb,
                                    vslot=vslot,
                                    st=(kc == 0), sp=(kc == nkc - 1): e.matmul(
                                        PS[pvb][0:65, off:512],
                                        VTMh[vslot][:, kc * 65:kc * 65 + 65],
                                        PT[0:128, 512 * pslot + off:512 * pslot + 512],
                                        start=st, stop=sp),
                                    waits=pv_w, inc="tE")
                    exp_toks[pslot] = pv_tok
                rc = V_(lambda e, pvb=pvb: e.reciprocal(EPI[64:65, 0:512],
                                                        PS[pvb][64:65, 0:512]),
                        waits=[pv_tok, state.get("epi_free", T0)])
                tb2 = mm_group(5, [(C["r064f"][64:65, 0:128], EPI[64:65, 0:512],
                                    PS[5][0:128, 0:512])], waits=[rc])
                cpy = V_(lambda e: e.tensor_copy(EPI[:, 0:512], PS[5][0:128, 0:512]),
                         waits=[tb2])
                consume(5, cpy)
                ot = V_(lambda e, b_=b_, ch=ch, qg=qg, pvb=pvb: e.tensor_mul(
                    XN[b_:b_ + 64, ch * T + 512 * qg:ch * T + 512 * qg + 512],
                    PS[pvb][0:64, 0:512], EPI[0:64, 0:512]),
                    waits=[cpy] + o_first)
                o_first = []
                consume(pvb, ot)
                state["epi_free"] = ot
                negbc_free = tok
            vput[vslot] = pv_tok
        o_tok = ("vE", p.cnt["vE"])
        sclose()
        bar3 = barrier()
        AUX, CP = sopen(("a3AUX", [128, NC * T], F32), ("aCP", [128, NC * 1024], BF16))
        rl = dLx.load(bass.AP(AUX, 0, [[NC * T, 128], [T, NC], [1, T]]),
                      bass.AP(s_lnx.tensor, 0, [[T, 128], [128 * T, NC], [1, T]]),
                      waits=[slnx] + bar3)
        cpw = Dma(p, "sync", "cpw")
        cpl = cpw.load(bass.AP(CP, 0, [[NC * 1024, 128], [1024, NC], [1, 1024]]),
                       bass.AP(s_cp.tensor, 0, [[1024, 128], [128 * 1024, NC], [1, 1024]]),
                       waits=[state.get("gout_cp", T0)] + bar3)
        lastx = state["x"]
        hm_tok = None
        for nc_ in range(NC):
            for g in range(NTG):
                bi = (nc_ * NTG + g) % 2
                mms = [(CP[:, kc * 1024 + nc_ * 128:kc * 1024 + nc_ * 128 + 128],
                        XN[:, T * kc + TG * g:T * kc + TG * g + TG])
                       for kc in range(NC)]
                tok = mm_group(bi, mms, waits=[cpl, o_tok])
                hm = V_(lambda e, bi=bi, nc_=nc_, g=g: e.scalar_tensor_tensor(
                    AUX[:, T * nc_ + TG * g:T * nc_ + TG * g + TG],
                    PS[bi][0:128, 0:TG], 1.0,
                    AUX[:, T * nc_ + TG * g:T * nc_ + TG * g + TG], OP.mult, OP.add),
                    waits=[tok, rl])
                consume(bi, hm)
                hm_tok = hm
            hm_tok = V_(lambda e, nc_=nc_: e.tensor_scalar(
                AUX[:, T * nc_:T * nc_ + T], AUX[:, T * nc_:T * nc_ + T],
                GB["cp"]["bias"][:, nc_:nc_ + 1], None, OP.add),
                waits=[hm_tok, gbias["cp"]])
            lastx = V_(lambda e, nc_=nc_: e.tensor_add(
                X[:, T * nc_:T * nc_ + T], X[:, T * nc_:T * nc_ + T],
                AUX[:, T * nc_:T * nc_ + T]), waits=[hm_tok, lastx])
        state["x"] = lastx
        st2 = stats(AUX, hm_tok)
        xn2_tok = materialize_ln_full(AUX, hm_tok, SM["ln2g"], SM["ln2b"], st2)
        sclose()
        bar4 = barrier()
        Y, WSa, WSb, WTa, WTb = sopen(
            ("a4Y", [128, 32 * TG], BF16),
            ("a4WSa", [128, 2048], BF16), ("a4WSb", [128, 2048], BF16),
            ("a4WTa", [128, 4096], BF16), ("a4WTb", [128, 4096], BF16))
        ffn_stream("a4", [WSa, WSb], [WTa, WTb], xn2_tok, s_cf,
                   state.get("gout_cf", T0), s_cm, state.get("gout_cm", T0),
                   GB["cf"]["bias"], gbias["cf"], AF.Gelu_apprx_tanh,
                   None, T0, 1.0, Y, y_free0=xn2_tok, bar=bar4,
                   b2_cols=(gbias["cm"], GB["cm"]["bias"]))
        sclose()

    if n_sub >= 2:
        pump_all()
        sclose()
        sub_l0attn()

    h_sel8b = din("sel8b", [8, 8 * 128], BF16)
    GM1, GA1, GM2, GA2, GW1, GW2, GTM = 144, 160, 176, 192, 208, 224, 240

    def sub_l1ffn(pref):
        bar = barrier()
        st_tok = stats(X, state["x"])
        CFM, SEL8B = sopen((pref + "CFM", [8, 2048], BF16),
                           (pref + "SEL8B", [8, 8 * 128], BF16))
        dsel = Dma(p, "sync", pref + "dsel")
        selt = dsel.load(SEL8B[:], h_sel8b, waits=bar)
        AUX, SCO, IT8, FT8, FT8b = sopen(
            (pref + "AUX", [128, NC * T], F32), (pref + "SCO", [128, 128], F32),
            (pref + "IT8", [128, 8], I32), (pref + "FT8", [128, 8], F32),
            (pref + "FT8b", [128, 8], F32))
        lnx_tok = materialize_lnx(AUX, st_tok, bar=bar)
        xn_tok = None
        for c in range(NC):
            xn_tok = V_(lambda e, c=c: e.tensor_copy(
                XN[:, T * c:T * c + T], AUX[:, T * c:T * c + T]), waits=[lnx_tok])
        glast = None
        for tt in range(16):
            mms = [(AUX[:, T * kc + 128 * tt:T * kc + 128 * tt + 128],
                    SM["gw"][:, kc * 8:kc * 8 + 8],
                    PS[4][0:128, 8 * tt:8 * tt + 8]) for kc in range(NC)]
            glast = mm_group(4, mms, waits=[lnx_tok, CW])
        sg_tok = S_(lambda e: e.activation(out=SCO[:], in_=PS[4][0:128, 0:128],
                                           func=AF.Sigmoid), waits=[glast])
        consume(4, sg_tok)
        a = sg_tok
        for tt in range(16):
            sl = SCO[:, 8 * tt:8 * tt + 8]
            a = V_(lambda e, sl=sl, tt=tt: e.reduce_max(
                STC[:, GM1 + tt:GM1 + tt + 1], sl, axis=AX.X), waits=[a])
            a = V_(lambda e, sl=sl, tt=tt: e.tensor_scalar(
                IT8[:], sl, STC[:, GM1 + tt:GM1 + tt + 1], None, OP.is_equal),
                waits=[a])
            a = V_(lambda e: e.select(FT8[:], IT8[:], C["iota8"][:], C["c999"][:],
                                      add_drain=True), waits=[a])
            a = V_(lambda e, tt=tt: e.tensor_reduce(
                STC[:, GA1 + tt:GA1 + tt + 1], FT8[:], AX.X, OP.min), waits=[a])
            a = V_(lambda e, tt=tt: e.tensor_scalar(
                IT8[:], C["iota8"][:], STC[:, GA1 + tt:GA1 + tt + 1], None,
                OP.is_equal), waits=[a])
            a = V_(lambda e, sl=sl: e.select(FT8[:], IT8[:], C["cneg"][:], sl,
                                             add_drain=True), waits=[a])
            a = V_(lambda e, tt=tt: e.reduce_max(
                STC[:, GM2 + tt:GM2 + tt + 1], FT8[:], axis=AX.X), waits=[a])
            a = V_(lambda e, tt=tt: e.tensor_scalar(
                IT8[:], FT8[:], STC[:, GM2 + tt:GM2 + tt + 1], None, OP.is_equal),
                waits=[a])
            a = V_(lambda e: e.select(FT8[:], IT8[:], C["iota8"][:], C["c999"][:],
                                      add_drain=True), waits=[a])
            a = V_(lambda e, tt=tt: e.tensor_reduce(
                STC[:, GA2 + tt:GA2 + tt + 1], FT8[:], AX.X, OP.min), waits=[a])
        a = V_(lambda e: e.tensor_add(STC[:, GTM:GTM + 16], STC[:, GM1:GM1 + 16],
                                      STC[:, GM2:GM2 + 16]), waits=[a])
        a = V_(lambda e: e.tensor_scalar(STC[:, GTM:GTM + 16], STC[:, GTM:GTM + 16],
                                         1e-20, None, OP.add), waits=[a])
        a = V_(lambda e: e.reciprocal(STC[:, GTM:GTM + 16], STC[:, GTM:GTM + 16]),
               waits=[a])
        a = V_(lambda e: e.scalar_tensor_tensor(
            STC[:, GW1:GW1 + 16], STC[:, GM1:GM1 + 16], 2.5, STC[:, GTM:GTM + 16],
            OP.mult, OP.mult), waits=[a])
        a = V_(lambda e: e.scalar_tensor_tensor(
            STC[:, GW2:GW2 + 16], STC[:, GM2:GM2 + 16], 2.5, STC[:, GTM:GTM + 16],
            OP.mult, OP.mult), waits=[a])
        for tt in range(16):
            a = V_(lambda e, tt=tt: e.tensor_scalar(
                FT8[:], C["iota8"][:], STC[:, GA1 + tt:GA1 + tt + 1], None,
                OP.is_equal), waits=[a])
            a = V_(lambda e, tt=tt: e.tensor_scalar(
                FT8[:], FT8[:], STC[:, GW1 + tt:GW1 + tt + 1], None, OP.mult),
                waits=[a])
            a = V_(lambda e, tt=tt: e.tensor_scalar(
                FT8b[:], C["iota8"][:], STC[:, GA2 + tt:GA2 + tt + 1], None,
                OP.is_equal), waits=[a])
            a = V_(lambda e, tt=tt: e.tensor_scalar(
                FT8b[:], FT8b[:], STC[:, GW2 + tt:GW2 + tt + 1], None, OP.mult),
                waits=[a])
            a = V_(lambda e: e.tensor_add(FT8[:], FT8[:], FT8b[:]), waits=[a])
            tokt = TT_(lambda e: e.transpose(PS[5][0:8, 0:128], FT8[:],
                                             C["ident_f"][:]),
                       waits=[a, bank_free[5]])
            a = V_(lambda e, tt=tt: e.tensor_copy(CFM[0:8, 128 * tt:128 * tt + 128],
                                                  PS[5][0:8, 0:128]), waits=[tokt])
            consume(5, a)
        cfm_tok = a
        sclose()
        bar2 = barrier()
        bufs = sopen(
            (pref + "HMS", [128, 8 * 512], BF16),
            (pref + "HM0", [128, 4 * 512], BF16), (pref + "HM1", [128, 4 * 512], BF16),
            (pref + "HM2", [128, 4 * 512], BF16), (pref + "HM3", [128, 4 * 512], BF16),
            (pref + "HM4", [128, 4 * 512], BF16), (pref + "HM5", [128, 4 * 512], BF16),
            (pref + "HM6", [128, 4 * 512], BF16), (pref + "HM7", [128, 4 * 512], BF16),
            (pref + "SGB", [128, 8 * 512], BF16),
            (pref + "GUW0", [128, 8 * 256], BF16), (pref + "GUW1", [128, 8 * 256], BF16),
            (pref + "WD0", [128, 1024], BF16), (pref + "WD1", [128, 1024], BF16))
        HMS = bufs[0]
        HM = bufs[1:9]
        SGB, GUW0, GUW1, WD0, WD1 = bufs[9:]
        GUW = [GUW0, GUW1]
        WD = [WD0, WD1]
        guw = Dma(p, "gpsimd", pref + "guw", n_slots=2)
        wdd = Dma(p, "gpsimd", pref + "wdd", n_slots=2)
        lastx = state["x"]
        guw_free = [list(bar2), list(bar2)]
        wd_free = [list(bar2), list(bar2)]
        hm_free = {}
        sg_free = {}
        slot_ctr = [0]

        def gu_block(g, wgh, wuh, base_off, NI, HBUF, key, cbc):
            nonlocal lastx
            cbc_tok = None
            if cbc is not None:
                cbc_tok = mm_group(4, [(SEL8B[0:8, cbc * 128:cbc * 128 + 128],
                                        CFM[0:8, TG * g:TG * g + TG],
                                        PS[4][0:128, 0:TG])],
                                   waits=[cfm_tok, selt])
            sg_last = None
            for which, wh in ((0, wgh), (1, wuh)):
                for sl_ in range(NI // 256):
                    slot = slot_ctr[0] % 2
                    slot_ctr[0] += 1
                    ldt = guw.load(
                        bass.AP(GUW[slot], 0, [[8 * 256, 128], [256, NC], [1, 256]]),
                        bass.AP(wh.tensor, base_off + sl_ * 256,
                                [[NI, 128], [128 * NI, NC], [1, 256]]),
                        waits=list(guw_free[slot]), slot=slot)
                    for nsub in range(2):
                        ic = sl_ * 2 + nsub
                        bi = ic % 4 if ic < 8 else ic % 2
                        mms = [(GUW[slot][:, kc * 256 + nsub * 128:kc * 256 + nsub * 128 + 128],
                                XN[:, T * kc + TG * g:T * kc + TG * g + TG])
                               for kc in range(NC)]
                        tok = mm_group(bi, mms, waits=[ldt, xn_tok])
                        if which == 0:
                            sgt = S_(lambda e, bi=bi, ic=ic: e.activation(
                                out=SGB[:, ic * 512:ic * 512 + 512],
                                in_=PS[bi][0:128, 0:TG], func=AF.Silu),
                                waits=[tok, sg_free.get(ic, T0)])
                            consume(bi, sgt)
                            sg_free[ic] = sgt
                            sg_last = sgt
                        else:
                            hmt = V_(lambda e, bi=bi, ic=ic, HBUF=HBUF:
                                     e.scalar_tensor_tensor(
                                         HBUF[:, ic * 512:ic * 512 + 512],
                                         PS[bi][0:128, 0:TG], 1.0,
                                         SGB[:, ic * 512:ic * 512 + 512],
                                         OP.mult, OP.mult),
                                     waits=[tok, sg_free[ic],
                                            hm_free.get((key, ic), T0)])
                            if cbc is not None:
                                hmt = V_(lambda e, ic=ic, HBUF=HBUF: e.tensor_mul(
                                    HBUF[:, ic * 512:ic * 512 + 512],
                                    HBUF[:, ic * 512:ic * 512 + 512],
                                    PS[4][0:128, 0:TG]), waits=[hmt, cbc_tok])
                            consume(bi, hmt)
                            hm_free[(key, ic)] = hmt
                    guw_free[slot] = [tok]
            if cbc is not None:
                consume(4, hm_free[(key, NI // 128 - 1)])
            return

        for g in range(NTG):
            gu_block(g, h_wsg, h_wsu, 0, 1024, HMS, "s", None)
            for e_ in range(8):
                gu_block(g, h_wg, h_wu, e_ * D * 512, 512, HM[e_], "e%d" % e_, e_)
            dn = [("s", HMS, ic, h_wsd.tensor, ic * 128 * 1024) for ic in range(8)]
            for e_ in range(8):
                dn += [("e%d" % e_, HM[e_], ic, h_wd.tensor,
                        e_ * 512 * 1024 + ic * 128 * 1024) for ic in range(4)]
            z_toks = [None] * 8
            for di, (key, HBUF, ic, wt, woff) in enumerate(dn):
                slot = di % 2
                ldt = wdd.load(bass.AP(WD[slot], 0, [[1024, 128], [1, 1024]]),
                               bass.AP(wt, woff, [[1024, 128], [1, 1024]]),
                               waits=list(wd_free[slot]), slot=slot)
                lastmm = None
                for nc_ in range(8):
                    if nc_ == 0:
                        w0 = [ldt, hm_free[(key, ic)]]
                    else:
                        w0 = []
                    if di == 0:
                        w0 = w0 + [bank_free[nc_]]
                    lastmm = p.emit(
                        "tensor",
                        lambda e, nc_=nc_, slot=slot, HBUF=HBUF, ic=ic,
                        st=(di == 0), sp=(di == len(dn) - 1): e.matmul(
                            PS[nc_][0:128, 0:TG],
                            WD[slot][:, nc_ * 128:nc_ * 128 + 128],
                            HBUF[:, ic * 512:ic * 512 + 512], start=st, stop=sp),
                        waits=w0,
                        inc=("tE" if (nc_ == 7 or di == len(dn) - 1) else None))
                    if di == len(dn) - 1:
                        z_toks[nc_] = lastmm
                wd_free[slot] = [lastmm]
            for nc_ in range(8):
                xt = V_(lambda e, nc_=nc_, g=g: e.scalar_tensor_tensor(
                    X[:, T * nc_ + TG * g:T * nc_ + TG * g + TG],
                    PS[nc_][0:128, 0:TG], 0.5,
                    X[:, T * nc_ + TG * g:T * nc_ + TG * g + TG], OP.mult, OP.add),
                    waits=[z_toks[nc_], lastx])
                consume(nc_, xt)
                lastx = xt
            for key, HBUF, ic, wt, woff in dn:
                hm_free[(key, ic)] = z_toks[7]
            for ic in range(8):
                sg_free[ic] = z_toks[7]
        state["x"] = lastx
        sclose()
        sclose()

    def sub_l1attn():
        bar = barrier()
        st_tok = stats(X, state["x"], rms_eps=1e-6)
        AUX, = sopen(("l5AUX", [128, NC * T], F32))
        lnx_tok = materialize_lnx(AUX, st_tok, bar=bar)
        xn_tok = materialize_xn(AUX, lnx_tok, RRC, SM["l1w"], None)
        lastx = state["x"]
        for c in range(NC):
            lastx = V_(lambda e, c=c: e.tensor_add(
                X[:, T * c:T * c + T], X[:, T * c:T * c + T],
                AUX[:, T * c:T * c + T]), waits=[lnx_tok, lastx])
        state["x"] = lastx
        sclose()
        bar2 = barrier()
        FB, OB, GBF, SB_, WSL0, WSL1, WSMB, WSMF = sopen(
            ("l5FB", [128, T], F32), ("l5OB", [128, T], F32),
            ("l5GB", [128, T], F32), ("l5SB", [128, T], F32),
            ("l5W0", [128, 8 * 128], BF16), ("l5W1", [128, 8 * 128], BF16),
            ("l5WSM", [128, 8], BF16), ("l5WSF", [128, 8], F32))
        WSL = [WSL0, WSL1]
        wl = Dma(p, "gpsimd", "l5wl", n_slots=2)
        wl_free = [list(bar2), list(bar2)]
        fb_free = list(bar2)
        ob_free = list(bar2)
        gb_free = list(bar2)
        sb_free = list(bar2)
        wsm_free = T0
        sctr = [0]

        def wload(wh, c):
            slot = sctr[0] % 2
            sctr[0] += 1
            t = wl.load(bass.AP(WSL[slot], 0, [[8 * 128, 128], [128, NC], [1, 128]]),
                        bass.AP(wh.tensor, c * 128, [[D, 128], [128 * D, NC], [1, 128]]),
                        waits=list(wl_free[slot]), slot=slot)
            return slot, t

        for c in range(NC):
            slot, ldt = wload(h_wf, c)
            fts = []
            for g in range(NTG):
                bi = g % 2
                mms = [(WSL[slot][:, kc * 128:kc * 128 + 128],
                        XN[:, T * kc + TG * g:T * kc + TG * g + TG])
                       for kc in range(NC)]
                tok = mm_group(bi, mms, waits=[ldt, xn_tok])
                ft = S_(lambda e, bi=bi, g=g: e.activation(
                    out=FB[:, TG * g:TG * g + TG], in_=PS[bi][0:128, 0:TG],
                    func=AF.Sigmoid), waits=[tok] + (fb_free if g == 0 else []))
                consume(bi, ft)
                fts.append(ft)
            wl_free[slot] = [tok]
            fmax = V_(lambda e: e.tensor_scalar(FB[:], FB[:], 0.9, None, OP.max),
                      waits=[fts[-1]])
            slot, ldt = wload(h_wo, c)
            ots = []
            for g in range(NTG):
                bi = g % 2
                mms = [(WSL[slot][:, kc * 128:kc * 128 + 128],
                        XN[:, T * kc + TG * g:T * kc + TG * g + TG])
                       for kc in range(NC)]
                tok = mm_group(bi, mms, waits=[ldt, xn_tok])
                ot = S_(lambda e, bi=bi, g=g: e.activation(
                    out=OB[:, TG * g:TG * g + TG], in_=PS[bi][0:128, 0:TG],
                    func=AF.Sigmoid), waits=[tok] + (ob_free if g == 0 else []))
                consume(bi, ot)
                ots.append(ot)
            wl_free[slot] = [tok]
            slot, ldt = wload(h_wi, c)
            wst = None
            for kc in range(NC):
                wst = V_(lambda e, kc=kc, slot=slot: e.reduce_sum(
                    WSMF[:, kc:kc + 1], WSL[slot][:, kc * 128:kc * 128 + 128],
                    axis=AX.X), waits=[ldt, wsm_free])
            wst = V_(lambda e: e.tensor_copy(WSMB[:], WSMF[:]), waits=[wst])
            wl_free[slot] = [wst]
            ginit = V_(lambda e: e.tensor_scalar(GBF[:], FB[:], -1.0, 1.0,
                                                 OP.mult, OP.add),
                       waits=[fmax] + gb_free)
            glast = None
            for g in range(NTG):
                tokv = mm_group(6, [(WSMB[:, kc:kc + 1],
                                     XN[:, T * kc + TG * g:T * kc + TG * g + TG],
                                     PS[6][0:1, 0:TG]) for kc in range(NC)],
                                waits=[wst, xn_tok])
                cpv = V_(lambda e: e.tensor_copy(ROWS[0:1, 0:TG], PS[6][0:1, 0:TG]),
                         waits=[tokv, rows_free[0]])
                consume(6, cpv)
                tokb = mm_group(5, [(C["r064f"][0:1, 0:128], ROWS[0:1, 0:TG],
                                     PS[5][0:128, 0:TG])], waits=[cpv])
                rows_free[0] = tokb
                glast = V_(lambda e, g=g: e.tensor_mul(
                    GBF[:, TG * g:TG * g + TG], GBF[:, TG * g:TG * g + TG],
                    PS[5][0:128, 0:TG]), waits=[tokb, ginit])
                consume(5, glast)
            wsm_free = glast
            sc = V_(lambda e: e.tensor_tensor_scan(SB_[:], FB[:], GBF[:], 0.0,
                                                   OP.mult, OP.add),
                    waits=[glast, fmax] + sb_free)
            fb_free = [sc]
            gb_free = [sc]
            yv = V_(lambda e: e.tensor_mul(SB_[:], SB_[:], OB[:]), waits=[sc, ots[-1]])
            ob_free = [yv]
            lastx = V_(lambda e, c=c: e.tensor_add(
                X[:, T * c:T * c + T], X[:, T * c:T * c + T], SB_[:]),
                waits=[yv, lastx])
            sb_free = [lastx]
        state["x"] = lastx
        sclose()

    if n_sub >= 3:
        sub_l0ffn("f3", t1, t2)
    if n_sub >= 4:
        sub_l1ffn("m4")
    if n_sub >= 5:
        sub_l1attn()
    if n_sub >= 6:
        sub_l1ffn("m6")


    st = dX.load(bass.AP(out_h.tensor, 0, [[T, 128], [128 * T, NC], [1, T]]),
                 bass.AP(X, 0, [[NC * T, 128], [T, NC], [1, T]]), waits=[state["x"]],
                 slot=0)
    p.emit("sync", None, waits=[st])

    with nc.Block() as block:
        @block.sync
        def _(e):
            p.replay("sync", e)

        @block.tensor
        def _(e):
            p.replay("tensor", e)

        @block.vector
        def _(e):
            p.replay("vector", e)

        @block.scalar
        def _(e):
            p.replay("scalar", e)

        @block.gpsimd
        def _(e):
            p.replay("gpsimd", e)

    es.close()
    return nc


def kernel(**inputs):
    in_maps = _host_inputs(inputs)
    nc = build()
    if _RUN is not None:
        results = _RUN(nc, in_maps)
    else:
        res = run_bass_kernel_spmd(nc, in_maps, core_ids=list(range(8)))
        results = res.results
    out = np.stack([np.asarray(r["out"]).T for r in results])
    return np.ascontiguousarray(out.astype(np.float32))


# version 04
# speedup vs baseline: 1.0798x; 1.0246x over previous
import numpy as np
import ml_dtypes
import concourse.bass as bass
import concourse.mybir as mybir
from concourse.bass_utils import run_bass_kernel_spmd

F32 = mybir.dt.float32
BF16 = mybir.dt.bfloat16
I32 = mybir.dt.int32
AF = mybir.ActivationFunctionType
OP = mybir.AluOpType
AX = mybir.AxisListType

D = 1024
T = 2048
NC = 8
TG = 512
NTG = T // TG
HEADS = 16
NEGBIG = -1073741824.0

_N_SUB = 6
_RUN = None



def _t(a):
    return np.ascontiguousarray(np.asarray(a, np.float32).T)


def _cols(v, n):
    return np.ascontiguousarray(np.asarray(v, np.float32).reshape(n, 128).T)


def _consts():
    c = {}
    c["ident_f"] = np.eye(128, dtype=np.float32)
    itop = np.zeros((128, 128), dtype=ml_dtypes.bfloat16)
    ibot = np.zeros((128, 128), dtype=ml_dtypes.bfloat16)
    for q in range(64):
        itop[q, q] = 1
        itop[64 + q, q] = 1
        ibot[q, 64 + q] = 1
        ibot[64 + q, 64 + q] = 1
    c["itop"] = itop
    c["ibot"] = ibot
    m1 = np.where(np.arange(128)[None, :] > np.arange(128)[:, None], NEGBIG, 0.0)
    m1 = m1.astype(ml_dtypes.bfloat16)
    c["m1a"] = np.concatenate([m1[0:64], m1[0:64]], axis=0)
    c["m1b"] = np.concatenate([m1[64:128], m1[64:128]], axis=0)
    m2 = np.where(np.arange(128)[None, :] < np.arange(128)[:, None], NEGBIG, 0.0)
    m2 = m2.astype(ml_dtypes.bfloat16)
    c["m2a"] = np.concatenate([m2[0:64], m2[0:64]], axis=0)
    c["m2b"] = np.concatenate([m2[64:128], m2[64:128]], axis=0)
    r064 = np.zeros((128, 128), np.float32)
    r064[0, :] = 1.0
    r064[64, :] = 1.0
    c["r064f"] = r064
    c["r064b"] = r064.astype(ml_dtypes.bfloat16)
    c["invd"] = np.full((128, 1), 1.0 / D, np.float32)
    c["invdb"] = np.full((128, 1), 1.0 / D, ml_dtypes.bfloat16)
    c["eps5"] = np.full((128, 1), 1e-5, np.float32)
    c["eps6"] = np.full((128, 1), 1e-6, np.float32)
    c["onescol"] = np.ones((128, 1), np.float32)
    c["onesrow"] = np.ones((1, 512), np.float32)
    c["iota8"] = np.broadcast_to(np.arange(8, dtype=np.float32), (128, 8)).copy()
    c["c999"] = np.full((128, 8), 999.0, np.float32)
    c["cneg"] = np.full((128, 8), -1e30, np.float32)
    sel8 = np.zeros((8, 8 * 128), np.float32)
    for e in range(8):
        sel8[e, e * 128:(e + 1) * 128] = 1.0
    c["sel8"] = sel8
    c["sel8b"] = sel8.astype(ml_dtypes.bfloat16)
    return c


CONST_SPECS = [
    ("ident_f", [128, 128], F32), ("itop", [128, 128], BF16), ("ibot", [128, 128], BF16),
    ("m1a", [128, 128], BF16), ("m1b", [128, 128], BF16),
    ("m2a", [128, 128], BF16), ("m2b", [128, 128], BF16),
    ("r064f", [128, 128], F32), ("r064b", [128, 128], BF16),
    ("invd", [128, 1], F32), ("invdb", [128, 1], BF16),
    ("eps5", [128, 1], F32), ("eps6", [128, 1], F32),
    ("onescol", [128, 1], F32), ("onesrow", [1, 512], F32),
    ("iota8", [128, 8], F32), ("c999", [128, 8], F32), ("cneg", [128, 8], F32),
]

GEN_MATS = (("ca", 1024, 3072), ("cp", 1024, 1024), ("cf", 1024, 4096), ("cm", 4096, 1024))


def _host_inputs(inp):
    cst = _consts()
    shared = {}
    for nm, I_, O in GEN_MATS:
        key = "L0_attn_" + nm
        sub = np.asarray(inp[key + "_sub_w"], np.float32)
        shared[nm + "_s0"] = _t(sub[0])
        shared[nm + "_s1"] = _t(sub[1])
        shared[nm + "_W"] = _t(inp[key + "_W_w"])
        shared[nm + "_B"] = _t(inp[key + "_B_w"])
        wl = np.asarray(inp[key + "_Wl_w"], np.float32)
        shared[nm + "_wl"] = np.broadcast_to(wl[0][None, :], (128, 2)).copy()
        nco = O // 128
        shared[nm + "_sb"] = np.concatenate(
            [_cols(inp[key + "_sub_b"][0], nco), _cols(inp[key + "_sub_b"][1], nco)], axis=1)
        shared[nm + "_Wb"] = _cols(inp[key + "_W_b"], nco)
        shared[nm + "_Bb"] = _cols(inp[key + "_B_b"], nco)
        wlb = np.asarray(inp[key + "_Wl_b"], np.float32)
        shared[nm + "_wlb"] = np.broadcast_to(wlb[0][None, :], (128, 2)).copy()
    shared["ln1g"] = _cols(inp["L0_attn_ln1_g"], NC)
    shared["ln1b"] = _cols(inp["L0_attn_ln1_b"], NC)
    shared["ln2g"] = _cols(inp["L0_attn_ln2_g"], NC)
    shared["ln2b"] = _cols(inp["L0_attn_ln2_b"], NC)
    shared["l0g"] = _cols(inp["L0_ffn_ln_gamma"], NC)
    shared["l0b"] = _cols(inp["L0_ffn_ln_beta"], NC)
    shared["fc1q"] = np.ascontiguousarray(np.asarray(inp["L0_ffn_fc1_q"], np.int32))
    shared["fc1s"] = np.ascontiguousarray(np.asarray(inp["L0_ffn_fc1_scales"], np.float32))
    shared["fc1b"] = _cols(inp["L0_ffn_fc1_bias"], 32)
    shared["fc2q"] = np.ascontiguousarray(np.asarray(inp["L0_ffn_fc2_q"], np.int32))
    s2 = np.asarray(inp["L0_ffn_fc2_scales"], np.float32)
    shared["fc2s"] = np.ascontiguousarray(
        s2.reshape(4, 8, 1024).transpose(1, 0, 2).reshape(8, 4096))
    shared["fc2brow"] = np.asarray(inp["L0_ffn_fc2_bias"], np.float32).reshape(1, 1024).copy()
    shared["l1w"] = _cols(inp["L1_attn_norm_w"], NC)
    shared["wf"] = _t(inp["L1_attn_w_forget"])
    shared["wi"] = _t(inp["L1_attn_w_input"])
    shared["wo"] = _t(inp["L1_attn_w_output"])
    shared["gw"] = np.ascontiguousarray(
        _t(inp["L1_ffn_gate_weight"]).reshape(8, 128, 8).transpose(1, 0, 2).reshape(128, 64))
    shared["wg"] = np.ascontiguousarray(np.asarray(inp["L1_ffn_w_gate"], np.float32).transpose(0, 2, 1))
    shared["wu"] = np.ascontiguousarray(np.asarray(inp["L1_ffn_w_up"], np.float32).transpose(0, 2, 1))
    shared["wd"] = np.ascontiguousarray(np.asarray(inp["L1_ffn_w_down"], np.float32).transpose(0, 2, 1))
    shared["wsg"] = _t(inp["L1_ffn_ws_gate"])
    shared["wsu"] = _t(inp["L1_ffn_ws_up"])
    shared["wsd"] = _t(inp["L1_ffn_ws_down"])
    shared.update(cst)

    x = np.asarray(inp["L0_attn_hs"], np.float32)
    in_maps = []
    for c in range(8):
        m = dict(shared)
        m["x"] = _t(x[c])
        in_maps.append(m)
    return in_maps



ENGS = ("sync", "tensor", "vector", "scalar", "gpsimd")
T0 = ("tE", 0)


class P:
    def __init__(self, nc, es):
        self.nc = nc
        self.es = es
        self.q = {e: [] for e in ENGS}
        self.cnt = {}
        self.sems = {}

    def sem(self, name):
        if name not in self.sems:
            self.sems[name] = self.es.enter_context(self.nc.semaphore(name))
            self.cnt[name] = 0
        return name

    def emit(self, eng, fn, waits=(), inc=None, n=1):
        ws = []
        for s, v in waits:
            if v <= 0:
                continue
            assert v <= self.cnt.get(s, 0), f"wait on future {s}:{v}>{self.cnt.get(s, 0)}"
            ws.append((s, v))
        self.q[eng].append((ws, fn, inc, n))
        if inc is not None:
            self.sem(inc)
            self.cnt[inc] += n
            return (inc, self.cnt[inc])
        return None

    def replay(self, ename, engine):
        maxw = {}
        for ws, fn, inc, n in self.q[ename]:
            for s, v in ws:
                if maxw.get(s, 0) < v:
                    engine.wait_ge(self.sems[s], v)
                    maxw[s] = v
            if fn is None:
                continue
            inst = fn(engine)
            if inc is not None:
                inst.then_inc(self.sems[inc], n)


class Dma:

    def __init__(self, p, eng, name, n_slots=1):
        self.p = p
        self.eng = eng
        self.sems = [p.sem(f"{name}{i}") for i in range(n_slots)]

    def load(self, dst, src, waits=(), slot=0):
        return self.p.emit(self.eng, lambda e, d=dst, s=src: e.dma_start(out=d, in_=s),
                           waits=waits, inc=self.sems[slot], n=16)



def build(n_sub=None):
    if n_sub is None:
        n_sub = _N_SUB
    nc = bass.Bass("TRN2", target_bir_lowering=False)
    from contextlib import ExitStack
    es = ExitStack()
    p = P(nc, es)

    def din(name, shape, dt=F32):
        return nc.dram_tensor(name, shape, dt, kind="ExternalInput").ap()

    x_h = din("x", [D, T])
    out_h = nc.dram_tensor("out", [D, T], F32, kind="ExternalOutput").ap()
    cst_h = {n_: din(n_, sh, dt) for n_, sh, dt in CONST_SPECS}
    h_sel8 = din("sel8", [8, 8 * 128])
    genh = {}
    for nm, I_, O in GEN_MATS:
        nco = O // 128
        genh[nm] = dict(
            s0=din(nm + "_s0", [I_, O]), s1=din(nm + "_s1", [I_, O]),
            W=din(nm + "_W", [I_, O]), B=din(nm + "_B", [I_, O]),
            wl=din(nm + "_wl", [128, 2]), sb=din(nm + "_sb", [128, 2 * nco]),
            Wb=din(nm + "_Wb", [128, nco]), Bb=din(nm + "_Bb", [128, nco]),
            wlb=din(nm + "_wlb", [128, 2]), O=O, I=I_, nco=nco)
    smh = {n_: din(n_, sh) for n_, sh in [
        ("ln1g", [128, NC]), ("ln1b", [128, NC]), ("ln2g", [128, NC]), ("ln2b", [128, NC]),
        ("l0g", [128, NC]), ("l0b", [128, NC]), ("fc1b", [128, 32]),
        ("l1w", [128, NC]), ("gw", [128, 64])]}
    h_fc1s = din("fc1s", [8, 4096])
    h_fc2s = din("fc2s", [8, 4096])
    h_fc2brow = din("fc2brow", [1, 1024])
    h_fc1q = din("fc1q", [D, 4096], I32)
    h_fc2q = din("fc2q", [4096, D], I32)
    h_wf = din("wf", [D, D]); h_wi = din("wi", [D, D]); h_wo = din("wo", [D, D])
    h_wg = din("wg", [8, D, 512]); h_wu = din("wu", [8, D, 512]); h_wd = din("wd", [8, 512, D])
    h_wsg = din("wsg", [D, D]); h_wsu = din("wsu", [D, D]); h_wsd = din("wsd", [D, D])

    s_fc1 = nc.dram_tensor("s_fc1", [D, 4096], BF16, kind="Internal").ap()
    s_fc2 = nc.dram_tensor("s_fc2", [4096, D], BF16, kind="Internal").ap()
    s_cf = nc.dram_tensor("s_cf", [D, 4096], BF16, kind="Internal").ap()
    s_cm = nc.dram_tensor("s_cm", [4096, D], BF16, kind="Internal").ap()
    s_qkv = nc.dram_tensor("s_qkv", [D, 3072], BF16, kind="Internal").ap()
    s_cp = nc.dram_tensor("s_cp", [D, D], BF16, kind="Internal").ap()
    s_lnx = nc.dram_tensor("s_lnx", [D, T], F32, kind="Internal").ap()
    s_v = nc.dram_tensor("s_v", [T, HEADS * 65], BF16, kind="Internal").ap()

    def sb(name, shape, dt):
        return es.enter_context(nc.sbuf_tensor(name, shape, dt))

    X = sb("X", [128, NC * T], F32)
    XN = sb("XN", [128, NC * T], BF16)
    C = {n_: sb("c_" + n_, sh, dt) for n_, sh, dt in CONST_SPECS}
    SM = {n_: sb("p_" + n_, sh, F32) for n_, sh in [
        ("ln1g", [128, NC]), ("ln1b", [128, NC]), ("ln2g", [128, NC]), ("ln2b", [128, NC]),
        ("l0g", [128, NC]), ("l0b", [128, NC]), ("fc1b", [128, 32]),
        ("l1w", [128, NC]), ("gw", [128, 64])]}
    GB = {}
    for nm, I_, O in GEN_MATS:
        nco = O // 128
        GB[nm] = dict(sb=sb(nm + "sb", [128, 2 * nco], F32), Wb=sb(nm + "Wb", [128, nco], F32),
                      Bb=sb(nm + "Bb", [128, nco], F32), wl=sb(nm + "wl", [128, 2], F32),
                      wlb=sb(nm + "wlb", [128, 2], F32), bias=sb(nm + "bias", [128, nco], F32))
    ROWS = sb("ROWS", [1, 2048], F32)
    STC = sb("STC", [128, 256], F32)
    SCR = sb("SCR", [128, 1024], F32)
    SCRB = sb("SCRB", [128, 1024], BF16)
    PS = [es.enter_context(nc.psum_tensor(f"PS{i}", [128, 512], F32)) for i in range(8)]

    scope_stack = []

    def sopen(*specs):
        cms = []
        ts = []
        for nm_, sh_, dt_ in specs:
            cm = nc.sbuf_tensor(nm_, sh_, dt_)
            ts.append(cm.__enter__())
            cms.append(cm)
        scope_stack.append(cms)
        return ts

    def sclose():
        cms = scope_stack.pop()
        for cm in reversed(cms):
            cm.__exit__(None, None, None)

    def barrier():
        return [(sname, p.cnt[sname]) for sname in list(p.cnt)]

    dW = Dma(p, "sync", "dW")
    dX = Dma(p, "sync", "dX")
    dS = Dma(p, "scalar", "dS")
    dG = Dma(p, "gpsimd", "dG")
    for s in ("tE", "vE", "sE"):
        p.sem(s)

    def TT_(fn, waits=(), inc="tE"):
        return p.emit("tensor", fn, waits=waits, inc=inc)

    def V_(fn, waits=(), inc="vE"):
        return p.emit("vector", fn, waits=waits, inc=inc)

    def S_(fn, waits=(), inc="sE"):
        return p.emit("scalar", fn, waits=waits, inc=inc)

    bank_free = [T0] * 8
    prep_tasks = []

    def pump(k=1):
        for _ in range(min(k, len(prep_tasks))):
            prep_tasks.pop(0)()

    def pump_all():
        while prep_tasks:
            prep_tasks.pop(0)()

    def mm_group(bi, mms, waits=(), out=None):
        w = list(waits) + [bank_free[bi]]
        n = len(mms)
        tok = None
        for i, mm in enumerate(mms):
            lh, rh = mm[0], mm[1]
            oa = mm[2] if len(mm) > 2 else PS[bi][:]
            tok = p.emit("tensor",
                         lambda e, oa=oa, lh=lh, rh=rh, st=(i == 0), sp=(i == n - 1):
                         e.matmul(oa, lh, rh, start=st, stop=sp),
                         waits=(w if i == 0 else ()), inc=("tE" if i == n - 1 else None))
        return tok

    def consume(bi, tok):
        bank_free[bi] = tok

    x_tok = dX.load(bass.AP(X, 0, [[NC * T, 128], [T, NC], [1, T]]),
                    bass.AP(x_h.tensor, 0, [[T, 128], [128 * T, NC], [1, T]]))
    ctok = None
    for n_ in C:
        ctok = dW.load(C[n_][:], cst_h[n_])
    for n_ in SM:
        ctok = dW.load(SM[n_][:], smh[n_])
    for nm, I_, O in GEN_MATS:
        for k2 in ("sb", "Wb", "Bb", "wl", "wlb"):
            ctok = dW.load(GB[nm][k2][:], genh[nm][k2])
    CW = ctok

    gbias = {}
    for nm, I_, O in GEN_MATS:
        b = GB[nm]
        nco = O // 128
        t1 = S_(lambda e, b=b, nco=nco: e.activation(
            out=b["bias"][:, 0:nco], in_=b["sb"][:, nco:2 * nco], func=AF.Copy,
            scale=b["wlb"][:, 1:2]), waits=[CW])
        t2 = V_(lambda e, b=b, nco=nco: e.scalar_tensor_tensor(
            b["bias"][:, 0:nco], b["sb"][:, 0:nco], b["wlb"][:, 0:1],
            b["bias"][:, 0:nco], OP.mult, OP.add), waits=[t1, CW])
        t3 = S_(lambda e, b=b, nco=nco: e.activation(
            out=b["bias"][:, 0:nco], in_=b["bias"][:, 0:nco], func=AF.Tanh), waits=[t2])
        t4 = V_(lambda e, b=b, nco=nco: e.scalar_tensor_tensor(
            b["bias"][:, 0:nco], b["bias"][:, 0:nco], 1.0, b["Wb"][:, 0:nco],
            OP.mult, OP.mult), waits=[t3])
        t5 = V_(lambda e, b=b, nco=nco: e.tensor_add(
            b["bias"][:, 0:nco], b["bias"][:, 0:nco], b["Bb"][:, 0:nco]), waits=[t4])
        gbias[nm] = t5
    gbias["ca"] = V_(lambda e: e.tensor_scalar_mul(
        GB["ca"]["bias"][:, 0:8], GB["ca"]["bias"][:, 0:8], 0.125), waits=[gbias["ca"]])

    rows_free = [T0] * 4

    def col_to_bcast(col_of, n_cols, bi_t, bi_bc, rslot, waits=(), out_bf16=None):
        w = list(waits) + [bank_free[bi_t]]
        tok = None
        for j in range(n_cols):
            tok = p.emit("tensor",
                         lambda e, j=j: e.transpose(
                             PS[bi_t][0:1, 128 * j:128 * (j + 1)], col_of(j), C["ident_f"][:]),
                         waits=(w if j == 0 else ()), inc=("tE" if j == n_cols - 1 else None))
        tokc = V_(lambda e: e.tensor_copy(
            ROWS[0:1, 512 * rslot:512 * rslot + 128 * n_cols],
            PS[bi_t][0:1, 0:128 * n_cols]), waits=[tok, rows_free[rslot]])
        consume(bi_t, tokc)
        tokb = mm_group(bi_bc, [(C["r064f"][0:1, 0:128],
                                 ROWS[0:1, 512 * rslot:512 * rslot + 128 * n_cols],
                                 PS[bi_bc][0:128, 0:128 * n_cols])], waits=[tokc])
        rows_free[rslot] = tokb
        if out_bf16 is not None:
            toko = V_(lambda e: e.tensor_copy(out_bf16, PS[bi_bc][0:128, 0:128 * n_cols]),
                      waits=[tokb])
            consume(bi_bc, toko)
            return tokb, toko
        return tokb, None

    SUMC, SQC, VARC, R1C, TMPC, R2C, RHC, MR1C, RRC = (0, 16, 32, 48, 64, 80, 96, 112, 128)

    def stats(buf, buf_tok, eps2=None, rms_eps=None):
        last_st = None
        for g in range(NTG):
            pump(1)
            casts = []
            for c in range(NC):
                casts.append(V_(lambda e, c=c, g=g: e.tensor_copy(
                    XN[:, T * c + TG * g:T * c + TG * g + TG],
                    buf[:, T * c + TG * g:T * c + TG * g + TG]),
                    waits=[buf_tok]))
            tok_s = mm_group(6, [(C["invdb"][:, 0:1],
                                  XN[:, T * c + TG * g:T * c + TG * g + TG],
                                  PS[6][0:1, 0:TG]) for c in range(NC)],
                             waits=[casts[-1]])
            tok_q = None
            for wave in range(4):
                sq_tok = None
                for cc in range(2):
                    c = 2 * wave + cc
                    sq_tok = S_(lambda e, c=c, cc=cc, g=g: e.activation(
                        out=SCRB[:, 512 * cc:512 * cc + 512],
                        in_=XN[:, T * c + TG * g:T * c + TG * g + TG], func=AF.Square),
                        waits=[casts[c], tok_q or state.get("scr_free", T0)]
                              + ([last_st] if last_st else []))
                w0 = [sq_tok] + ([bank_free[7]] if wave == 0 else [])
                for cc in range(2):
                    tok_q = p.emit("tensor",
                                   lambda e, cc=cc, st=(wave == 0 and cc == 0),
                                   sp=(wave == 3 and cc == 1): e.matmul(
                                       PS[7][0:1, 0:TG], C["invdb"][:, 0:1],
                                       SCRB[:, 512 * cc:512 * cc + 512],
                                       start=st, stop=sp),
                                   waits=(w0 if cc == 0 else ()),
                                   inc=("tE" if (cc == 1) else None))
            state["scr_free"] = tok_q
            cm1 = V_(lambda e: e.tensor_copy(ROWS[0:1, 0:TG], PS[6][0:1, 0:TG]),
                     waits=[tok_s, rows_free[0]])
            consume(6, cm1)
            cm2 = V_(lambda e: e.tensor_copy(ROWS[0:1, 512:512 + TG], PS[7][0:1, 0:TG]),
                     waits=[tok_q, rows_free[1]])
            consume(7, cm2)
            w1 = [bank_free[5], cm2]
            tokt = None
            for j in range(4):
                tokt = p.emit("tensor", lambda e, j=j: e.transpose(
                    PS[5][0:128, j:j + 1], ROWS[0:1, 128 * j:128 * (j + 1)],
                    C["ident_f"][0:1, 0:1]),
                    waits=(w1 if j == 0 else ()), inc=None)
            for j in range(4):
                tokt = p.emit("tensor", lambda e, j=j: e.transpose(
                    PS[5][0:128, 4 + j:5 + j], ROWS[0:1, 512 + 128 * j:512 + 128 * (j + 1)],
                    C["ident_f"][0:1, 0:1]), waits=(), inc=("tE" if j == 3 else None))
            st1 = V_(lambda e, g=g: e.tensor_copy(STC[:, SUMC + 4 * g:SUMC + 4 * g + 4],
                                                  PS[5][0:128, 0:4]), waits=[tokt])
            last_st = V_(lambda e, g=g: e.tensor_copy(STC[:, SQC + 4 * g:SQC + 4 * g + 4],
                                                      PS[5][0:128, 4:8]), waits=[tokt])
            consume(5, last_st)
            rows_free[0] = tokt
            rows_free[1] = tokt
        a = V_(lambda e: e.scalar_tensor_tensor(
            STC[:, TMPC:TMPC + 16], STC[:, SUMC:SUMC + 16], -1.0, STC[:, SUMC:SUMC + 16],
            OP.mult, OP.mult), waits=[last_st])
        a = V_(lambda e: e.tensor_add(STC[:, VARC:VARC + 16], STC[:, TMPC:TMPC + 16],
                                      STC[:, SQC:SQC + 16]), waits=[a])
        a = S_(lambda e: e.activation(out=STC[:, TMPC:TMPC + 16], in_=STC[:, VARC:VARC + 16],
                                      func=AF.Sqrt, bias=C["eps5"][:, 0:1]), waits=[a])
        a = V_(lambda e: e.reciprocal(STC[:, R1C:R1C + 16], STC[:, TMPC:TMPC + 16]), waits=[a])
        a = V_(lambda e: e.tensor_mul(STC[:, MR1C:MR1C + 16], STC[:, SUMC:SUMC + 16],
                                      STC[:, R1C:R1C + 16]), waits=[a])
        if eps2 is not None or rms_eps is not None:
            a = V_(lambda e: e.tensor_mul(STC[:, TMPC:TMPC + 16], STC[:, R1C:R1C + 16],
                                          STC[:, R1C:R1C + 16]), waits=[a])
            a = V_(lambda e: e.tensor_mul(STC[:, TMPC:TMPC + 16], STC[:, TMPC:TMPC + 16],
                                          STC[:, VARC:VARC + 16]), waits=[a])
            ee = eps2 if eps2 is not None else rms_eps
            tgt = R2C if eps2 is not None else RRC
            eps_ap = C["eps5"] if ee == 1e-5 else C["eps6"]
            a = S_(lambda e, eps_ap=eps_ap: e.activation(out=STC[:, TMPC:TMPC + 16],
                                                 in_=STC[:, TMPC:TMPC + 16],
                                                 func=AF.Sqrt, bias=eps_ap[:, 0:1]), waits=[a])
            a = V_(lambda e, tgt=tgt: e.reciprocal(STC[:, tgt:tgt + 16],
                                                   STC[:, TMPC:TMPC + 16]), waits=[a])
            if eps2 is not None:
                a = V_(lambda e: e.tensor_mul(STC[:, RHC:RHC + 16], STC[:, R1C:R1C + 16],
                                              STC[:, R2C:R2C + 16]), waits=[a])
        return a

    def stc_col(slot, g):
        return lambda j: STC[:, slot + 4 * g + j:slot + 4 * g + j + 1]

    state = {"x": x_tok}

    def materialize_lnx(AUX, stat_tok, bar=()):
        last = None
        for g in range(NTG):
            tr, _ = col_to_bcast(stc_col(R1C, g), 4, 5, 4, 0, waits=[stat_tok])
            tm, _ = col_to_bcast(stc_col(MR1C, g), 4, 5, 3, 1, waits=[stat_tok])
            for c in range(NC):
                tt = V_(lambda e, c=c, g=g: e.scalar_tensor_tensor(
                    AUX[:, T * c + TG * g:T * c + TG * g + TG],
                    X[:, T * c + TG * g:T * c + TG * g + TG], 1.0,
                    PS[4][0:128, 0:TG], OP.mult, OP.mult),
                    waits=[tr, state["x"], stat_tok] + list(bar))
                tt = V_(lambda e, c=c, g=g: e.tensor_sub(
                    AUX[:, T * c + TG * g:T * c + TG * g + TG],
                    AUX[:, T * c + TG * g:T * c + TG * g + TG],
                    PS[3][0:128, 0:TG]), waits=[tm, tt])
                last = tt
            consume(4, last)
            consume(3, last)
        return last

    def materialize_xn(SRC, src_tok, colslot, gamma, beta):
        last = None
        for g in range(NTG):
            tb, _ = col_to_bcast(stc_col(colslot, g), 4, 5, 4, 0, waits=[src_tok])
            for c in range(NC):
                tt = V_(lambda e, c=c, g=g: e.scalar_tensor_tensor(
                    XN[:, T * c + TG * g:T * c + TG * g + TG],
                    SRC[:, T * c + TG * g:T * c + TG * g + TG], 1.0,
                    PS[4][0:128, 0:TG], OP.mult, OP.mult), waits=[tb, src_tok])
                if gamma is not None:
                    if beta is not None:
                        tt = V_(lambda e, c=c, g=g: e.tensor_scalar(
                            XN[:, T * c + TG * g:T * c + TG * g + TG],
                            XN[:, T * c + TG * g:T * c + TG * g + TG],
                            gamma[:, c:c + 1], beta[:, c:c + 1], OP.mult, OP.add),
                            waits=[tt, CW])
                    else:
                        tt = V_(lambda e, c=c, g=g: e.tensor_scalar(
                            XN[:, T * c + TG * g:T * c + TG * g + TG],
                            XN[:, T * c + TG * g:T * c + TG * g + TG],
                            gamma[:, c:c + 1], None, OP.mult),
                            waits=[tt, CW])
                last = tt
            consume(4, last)
        return last

    dq = Dma(p, "gpsimd", "dq")
    dqs = Dma(p, "scalar", "dqs")
    dqsc = Dma(p, "sync", "dqsc", n_slots=2)

    def emit_dequant(DQ, DQO, SEL8, SCL, selw, qh, scales_h, scratch, n_k, n_n, blkmap, bar):
        dq_free = state.get("dq", T0)
        dqo_free = state.get("dqo", T0)
        scl_free = state.get("scl", T0)
        last_store = None
        for kc in range(n_k):
            tq = dq.load(DQ[:, 0:n_n],
                         bass.AP(qh.tensor, kc * 128 * n_n, [[n_n, 128], [1, n_n]]),
                         waits=[dq_free] + bar)
            selp, colb = blkmap(kc)
            td = None
            for j in range(n_n // 512):
                jj = j % 2
                tsc = dqsc.load(SCL[0:8, jj * 512:jj * 512 + 512],
                                bass.AP(scales_h.tensor, colb * n_n + j * 512,
                                        [[4096, 8], [1, 512]]),
                                waits=[state.get("sclb%d" % jj, T0)] + bar, slot=jj)
                tokb = mm_group(2, [(SEL8[0:8, selp * 128:selp * 128 + 128],
                                     SCL[0:8, jj * 512:jj * 512 + 512],
                                     PS[2][0:128, 0:512])], waits=[CW, tsc] + selw)
                state["sclb%d" % jj] = tokb
                td = V_(lambda e, j=j: e.scalar_tensor_tensor(
                    DQO[:, j * 512:(j + 1) * 512], DQ[:, j * 512:(j + 1) * 512],
                    -128.0, PS[2][0:128, 0:512], OP.add, OP.mult),
                    waits=[tokb, tq, dqo_free])
                consume(2, td)
            dq_free = td
            ts = dqs.load(bass.AP(scratch.tensor, kc * 128 * n_n, [[n_n, 128], [1, n_n]]),
                          DQO[:, 0:n_n], waits=[td])
            dqo_free = ts
            last_store = ts
        state["dq"] = dq_free
        state["dqo"] = dqo_free
        return last_store

    def emit_gen_tasks(nm, scratch, GS, GOUT, GSCR, bar, st_, scale_head=None):
        g = genh[nm]
        b = GB[nm]
        O = g["O"]
        I_ = g["I"]
        NRG = I_ // 1024
        genw = Dma(p, "gpsimd", "g_" + nm, n_slots=4)
        gens_ = Dma(p, "scalar", "gs_" + nm)
        NSL = O // 128

        def task(jj):
            j, rg = jj // NRG, jj % NRG
            tks = []
            for i, key in enumerate(("s0", "s1", "W", "B")):
                tks.append(genw.load(
                    bass.AP(GS[i], 0, [[1024, 128], [128, NC], [1, 128]]),
                    bass.AP(g[key].tensor, rg * 1024 * O + j * 128,
                            [[O, 128], [128 * O, NC], [1, 128]]),
                    waits=list(st_["gfree"][i]), slot=i))
            t1 = S_(lambda e: e.activation(out=GSCR[:, 0:1024], in_=GS[1][:, 0:1024],
                                           func=AF.Copy, scale=b["wl"][:, 1:2]),
                    waits=[tks[1], CW, st_["scr"]])
            t2 = V_(lambda e: e.scalar_tensor_tensor(
                GSCR[:, 0:1024], GS[0][:, 0:1024], b["wl"][:, 0:1], GSCR[:, 0:1024],
                OP.mult, OP.add), waits=[t1, tks[0]])
            t3 = S_(lambda e: e.activation(out=GSCR[:, 0:1024], in_=GSCR[:, 0:1024],
                                           func=AF.Tanh), waits=[t2])
            t4 = V_(lambda e: e.scalar_tensor_tensor(
                GSCR[:, 0:1024], GSCR[:, 0:1024], 1.0, GS[2][:, 0:1024],
                OP.mult, OP.mult), waits=[t3, tks[2]])
            gout_free = st_.get("gout", T0)
            if scale_head is not None and j < scale_head[0]:
                t5 = V_(lambda e: e.tensor_add(GSCR[:, 0:1024], GSCR[:, 0:1024],
                                               GS[3][:, 0:1024]), waits=[t4, tks[3]])
                t5 = V_(lambda e, f=scale_head[1]: e.tensor_scalar(
                    GOUT[:, 0:1024], GSCR[:, 0:1024], f, None, OP.mult),
                    waits=[t5, gout_free])
            else:
                t5 = V_(lambda e: e.tensor_add(
                    GOUT[:, 0:1024], GSCR[:, 0:1024], GS[3][:, 0:1024]),
                    waits=[t4, tks[3], gout_free])
            st_["scr"] = t5
            st_["gfree"] = [[t5]] * 4
            ts = gens_.load(bass.AP(scratch.tensor, rg * 1024 * O + j * 128,
                                    [[O, 128], [128 * O, NC], [1, 128]]),
                            bass.AP(GOUT, 0, [[1024, 128], [128, NC], [1, 128]]),
                            waits=[t5])
            st_["gout"] = ts
            state["gout_" + nm] = ts

        for jj in range(NSL * NRG):
            prep_tasks.append(lambda jj=jj: task(jj))

    def ffn_stream(pref, WS, WT, xn_tok, w1_scr, w1_tok, w2_scr, w2_tok,
                   b1_cols, b1_tok, act_fn, b2_row, b2_tok, out_scale, Y,
                   y_free0=T0, bar=(), b2_cols=None):
        w1d = Dma(p, "sync", pref + "w1", n_slots=2)
        w2d = Dma(p, "scalar", pref + "w2", n_slots=2)
        ws_free = [list(bar), list(bar)]
        wt_free = [list(bar), list(bar)]
        y_free = y_free0
        lastx = state["x"]
        for g in range(NTG):
            y_toks = []
            for io in range(16):
                pump(1)
                slot = io % 2
                ldt = w1d.load(
                    bass.AP(WS[slot], 0, [[2048, 128], [256, NC], [1, 256]]),
                    bass.AP(w1_scr.tensor, io * 256, [[4096, 128], [128 * 4096, NC], [1, 256]]),
                    waits=list(ws_free[slot]) + [w1_tok], slot=slot)
                tok = None
                for isub in range(2):
                    bi = isub % 2
                    mms = [(WS[slot][:, kc * 256 + isub * 128:kc * 256 + isub * 128 + 128],
                            XN[:, T * kc + TG * g:T * kc + TG * g + TG]) for kc in range(NC)]
                    tok = mm_group(bi, mms, waits=[ldt, xn_tok])
                    ic = io * 2 + isub
                    ytk = S_(lambda e, bi=bi, ic=ic: e.activation(
                        out=Y[:, ic * TG:(ic + 1) * TG], in_=PS[bi][0:128, 0:TG],
                        func=act_fn, bias=b1_cols[:, ic:ic + 1]),
                        waits=[tok, b1_tok, y_free])
                    consume(bi, ytk)
                    y_toks.append(ytk)
                ws_free[slot] = [tok]
            y_tok = y_toks[-1]
            first = [True] * 8
            slot_tok = {}
            for ks in range(8):
                pump(1)
                slot = ks % 2
                ldt = w2d.load(
                    bass.AP(WT[slot], 0, [[4096, 128], [1024, 4], [1, 1024]]),
                    bass.AP(w2_scr.tensor, ks * 512 * D, [[D, 128], [128 * D, 4], [1, 1024]]),
                    waits=list(wt_free[slot]) + [w2_tok], slot=slot)
                for nc_ in range(8):
                    w0 = [ldt, y_tok]
                    if first[nc_]:
                        w0 = w0 + [bank_free[nc_]]
                    for ksub in range(4):
                        ic = ks * 4 + ksub
                        is_last_of_ks = (nc_ == 7 and ksub == 3)
                        lastmm = (b2_cols is not None and ks == 7 and ksub == 3)
                        tokm = p.emit(
                            "tensor",
                            lambda e, nc_=nc_, slot=slot, ksub=ksub, ic=ic,
                            st=(first[nc_] and ksub == 0), sp=lastmm:
                            e.matmul(PS[nc_][0:128, 0:TG],
                                     WT[slot][:, ksub * 1024 + nc_ * 128:ksub * 1024 + nc_ * 128 + 128],
                                     Y[:, ic * TG:(ic + 1) * TG], start=st, stop=sp),
                            waits=(w0 if ksub == 0 else ()),
                            inc=("tE" if (is_last_of_ks or lastmm) else None))
                        if is_last_of_ks:
                            slot_tok[ks] = tokm
                        if lastmm:
                            slot_tok[(7, nc_)] = tokm
                    first[nc_] = False
                wt_free[slot] = [slot_tok[ks]]
            lasttok = None
            for nc_ in range(8):
                if b2_cols is None:
                    tok = p.emit("tensor",
                                 lambda e, nc_=nc_: e.matmul(
                                     PS[nc_][0:128, 0:TG],
                                     b2_row[0:1, nc_ * 128:nc_ * 128 + 128],
                                     C["onesrow"][0:1, 0:TG], start=False, stop=True),
                                 waits=([b2_tok] if nc_ == 0 else ()), inc="tE")
                else:
                    tok = slot_tok[(7, nc_)]
                xt = V_(lambda e, nc_=nc_, g=g: e.scalar_tensor_tensor(
                    X[:, T * nc_ + TG * g:T * nc_ + TG * g + TG],
                    PS[nc_][0:128, 0:TG], out_scale,
                    X[:, T * nc_ + TG * g:T * nc_ + TG * g + TG], OP.mult, OP.add),
                    waits=[tok, lastx])
                if b2_cols is not None:
                    xt = V_(lambda e, nc_=nc_, g=g: e.tensor_scalar(
                        X[:, T * nc_ + TG * g:T * nc_ + TG * g + TG],
                        X[:, T * nc_ + TG * g:T * nc_ + TG * g + TG],
                        b2_cols[1][:, nc_:nc_ + 1], None, OP.add),
                        waits=[xt, b2_cols[0]])
                consume(nc_, xt)
                lastx = xt
                lasttok = tok
            y_free = lasttok
        state["x"] = lastx
        return lastx

    def emit_fc_dequants(bar):
        DQ, DQO, SEL8, SCL = sopen(
            ("dqDQ", [128, 4096], F32), ("dqO", [128, 4096], BF16),
            ("dqSEL", [8, 8 * 128], F32), ("dqSCL", [8, 1024], F32))
        dsel = Dma(p, "sync", "dsel_fc")
        selt = dsel.load(SEL8[:], h_sel8)
        t1 = emit_dequant(DQ, DQO, SEL8, SCL, [selt], h_fc1q, h_fc1s, s_fc1, 8, 4096,
                          lambda kc: (kc, 0), bar)
        t2 = emit_dequant(DQ, DQO, SEL8, SCL, [selt], h_fc2q, h_fc2s, s_fc2, 32, 1024,
                          lambda kc: (kc % 8, kc // 8), bar)
        sclose()
        return t1, t2

    def sub_l0ffn(pref, fc1_tok, fc2_tok):
        bar = barrier()
        st_tok = stats(X, state["x"], eps2=1e-5)
        AUX, = sopen((pref + "AUX", [128, NC * T], F32))
        lnx_tok = materialize_lnx(AUX, st_tok, bar=bar)
        xn_tok = materialize_xn(AUX, lnx_tok, R2C, SM["l0g"], SM["l0b"])
        lastx = state["x"]
        for c in range(NC):
            lastx = V_(lambda e, c=c: e.scalar_tensor_tensor(
                X[:, T * c:T * c + T], AUX[:, T * c:T * c + T], 0.5,
                X[:, T * c:T * c + T], OP.mult, OP.add), waits=[lnx_tok, lastx])
        state["x"] = lastx
        sclose()
        bar2 = barrier()
        Y, WSa, WSb, WTa, WTb = sopen(
            (pref + "Y", [128, 32 * TG], BF16),
            (pref + "WSa", [128, 2048], BF16), (pref + "WSb", [128, 2048], BF16),
            (pref + "WTa", [128, 4096], BF16), (pref + "WTb", [128, 4096], BF16))
        db2 = Dma(p, "sync", pref + "db2")
        b2t = db2.load(ROWS[0:1, 1024:2048], h_fc2brow,
                       waits=bar2 + [rows_free[2], rows_free[3]])
        rows_free[2] = b2t
        rows_free[3] = b2t
        ffn_stream(pref, [WSa, WSb], [WTa, WTb], xn_tok, s_fc1, fc1_tok, s_fc2,
                   fc2_tok, SM["fc1b"], CW, AF.Silu, ROWS[0:1, 1024:2048], b2t,
                   0.25, Y, y_free0=state["x"], bar=bar2)
        sclose()

    if n_sub >= 1:
        t1, t2 = emit_fc_dequants([])
    if n_sub >= 2:
        GSg = sopen(("gGS0", [128, 1024], F32), ("gGS1", [128, 1024], F32),
                    ("gGS2", [128, 1024], F32), ("gGS3", [128, 1024], F32),
                    ("gOUT", [128, 1024], BF16), ("gSCR", [128, 1024], F32))
        gbar = barrier()
        gen_st = {"gfree": [list(gbar)] * 4, "scr": T0}
        emit_gen_tasks("ca", s_qkv, GSg[0:4], GSg[4], GSg[5], gbar, gen_st,
                       scale_head=(8, 0.125))
        emit_gen_tasks("cp", s_cp, GSg[0:4], GSg[4], GSg[5], gbar, gen_st)
        emit_gen_tasks("cf", s_cf, GSg[0:4], GSg[4], GSg[5], gbar, gen_st)
        emit_gen_tasks("cm", s_cm, GSg[0:4], GSg[4], GSg[5], gbar, gen_st)
    if n_sub >= 1:
        sub_l0ffn("f1", t1, t2)


    def materialize_ln_full(SRC, src_tok, gamma, beta, stat_tok):
        last = None
        for g in range(NTG):
            tr, _ = col_to_bcast(stc_col(R1C, g), 4, 5, 4, 0, waits=[stat_tok])
            tm, _ = col_to_bcast(stc_col(MR1C, g), 4, 5, 3, 1, waits=[stat_tok])
            for c in range(NC):
                sl = SCR[:, 512 * (c % 2):512 * (c % 2) + 512]
                t1 = V_(lambda e, c=c, g=g, sl=sl: e.scalar_tensor_tensor(
                    sl, SRC[:, T * c + TG * g:T * c + TG * g + TG], 1.0,
                    PS[4][0:128, 0:TG], OP.mult, OP.mult),
                    waits=[tr, src_tok, state.get("scr_free", T0)]
                          + ([last] if last else []))
                t1 = V_(lambda e, c=c, sl=sl: e.tensor_sub(sl, sl, PS[3][0:128, 0:TG]),
                        waits=[tm, t1])
                t1 = V_(lambda e, c=c, g=g, sl=sl: e.tensor_scalar(
                    XN[:, T * c + TG * g:T * c + TG * g + TG], sl,
                    gamma[:, c:c + 1], beta[:, c:c + 1], OP.mult, OP.add),
                    waits=[t1, CW])
                last = t1
            consume(4, last)
            consume(3, last)
        state["scr_free"] = last
        return last

    def sub_l0attn():
        CH = HEADS // 2
        dLx = Dma(p, "scalar", "dLx")
        bar = barrier()
        st_tok = stats(X, state["x"], eps2=1e-5)
        AUX, = sopen(("a2AUX", [128, NC * T], F32))
        lnx_tok = materialize_lnx(AUX, st_tok, bar=bar)
        h1_tok = materialize_xn(AUX, lnx_tok, R2C, SM["ln1g"], SM["ln1b"])
        slnx = dLx.load(bass.AP(s_lnx.tensor, 0, [[T, 128], [128 * T, NC], [1, T]]),
                        bass.AP(AUX, 0, [[NC * T, 128], [T, NC], [1, T]]),
                        waits=[lnx_tok])
        sclose()
        bar2 = barrier()
        QF, KF, VST0, VST1, VTMh0, VTMh1, PT, QW0, QW1, EPI, NEGBC = sopen(
            ("aQF", [128, NC * T], BF16), ("aKF", [128, NC * T], BF16),
            ("aVST0", [128, 8 * 65], BF16), ("aVST1", [128, 8 * 65], BF16),
            ("aVTMh0", [128, 16 * 65], BF16), ("aVTMh1", [128, 16 * 65], BF16),
            ("aPT", [128, 2 * 512], BF16),
            ("aQW0", [128, 8 * 256], BF16), ("aQW1", [128, 8 * 256], BF16),
            ("aEPI", [128, 512], F32), ("aNEGBC", [128, 4 * 512], BF16))
        QW = [QW0, QW1]
        VTMh = [VTMh0, VTMh1]
        VSTs = [VST0, VST1]
        ones_tok = [V_(lambda e, i=i: e.memset(
            bass.AP(VSTs[i], 64, [[8 * 65, 128], [65, 8], [1, 1]]), 1.0),
            waits=bar2) for i in range(2)]
        vbrow_tok = None
        for grp in range(2):
            wvb = [bank_free[5], gbias["ca"]] + bar2
            tokt = None
            for j in range(4):
                jj = 4 * grp + j
                tokt = p.emit("tensor", lambda e, j=j, jj=jj: e.transpose(
                    PS[5][0:1, 128 * j:128 * (j + 1)], GB["ca"]["bias"][:, 16 + jj:17 + jj],
                    C["ident_f"][:]), waits=(wvb if j == 0 else ()),
                    inc=("tE" if j == 3 else None))
            vbrow_tok = V_(lambda e, grp=grp: e.tensor_copy(
                ROWS[0:1, 1024 + 512 * grp:1536 + 512 * grp], PS[5][0:1, 0:512]),
                waits=[tokt, rows_free[2 + grp]])
            consume(5, vbrow_tok)
            rows_free[2 + grp] = vbrow_tok
        qkvw = Dma(p, "sync", "qkvw", n_slots=2)
        qkv_done = None
        vstore = Dma(p, "scalar", "vst", n_slots=2)
        vst_free = [[ones_tok[0]], [ones_tok[1]]]
        for nsl in range(12):
            slot = nsl % 2
            ldt = qkvw.load(
                bass.AP(QW[slot], 0, [[8 * 256, 128], [256, NC], [1, 256]]),
                bass.AP(s_qkv.tensor, nsl * 256, [[3072, 128], [128 * 3072, NC], [1, 256]]),
                waits=list(state.get("qw%d" % slot, [T0])) + bar2, slot=slot)
            if nsl < 8:
                tok = None
                for nsub in range(2):
                    for g in range(NTG):
                        bi = (nsub * NTG + g) % 2
                        n_global = nsl * 256 + nsub * 128
                        mms = [(QW[slot][:, kc * 256 + nsub * 128:kc * 256 + nsub * 128 + 128],
                                XN[:, T * kc + TG * g:T * kc + TG * g + TG])
                               for kc in range(NC)]
                        tok = mm_group(bi, mms, waits=[ldt, h1_tok])
                        dst = QF if n_global < 1024 else KF
                        colg = n_global if n_global < 1024 else n_global - 1024
                        ch = colg // 128
                        etk = V_(lambda e, bi=bi, dst=dst, ch=ch, g=g,
                                 col=n_global // 128: e.tensor_scalar(
                            dst[:, ch * T + TG * g:ch * T + TG * g + TG],
                            PS[bi][0:128, 0:TG],
                            GB["ca"]["bias"][:, col:col + 1], None, OP.add),
                            waits=[tok, gbias["ca"]])
                        consume(bi, etk)
                qkv_done = tok
            else:
                vsl = nsl - 8
                for tt in range(16):
                    bi = tt % 2
                    w0 = [ldt, h1_tok, bank_free[bi]]
                    tokv = None
                    for kc in range(NC):
                        tokv = p.emit("tensor",
                                      lambda e, kc=kc, tt=tt, slot=slot, bi=bi,
                                      st=(kc == 0): e.matmul(
                                          PS[bi][0:128, 0:256],
                                          XN[:, T * kc + 128 * tt:T * kc + 128 * tt + 128],
                                          QW[slot][:, kc * 256:kc * 256 + 256],
                                          start=st, stop=False),
                                      waits=(w0 if kc == 0 else ()), inc=None)
                    tokv = p.emit("tensor",
                                  lambda e, vsl=vsl, bi=bi: e.matmul(
                                      PS[bi][0:128, 0:256], C["r064f"][0:1, 0:128],
                                      ROWS[0:1, 1024 + 256 * vsl:1024 + 256 * vsl + 256],
                                      start=False, stop=True),
                                  waits=[vbrow_tok], inc="tE")
                    vslot = tt % 2
                    evk = V_(lambda e, bi=bi, vslot=vslot: e.tensor_copy(
                        bass.AP(VSTs[vslot], 0, [[8 * 65, 128], [65, 4], [1, 64]]),
                        bass.AP(PS[bi], 0, [[512, 128], [64, 4], [1, 64]])),
                        waits=[tokv] + list(vst_free[vslot]))
                    consume(bi, evk)
                    tokst = vstore.load(
                        bass.AP(s_v.tensor, tt * 128 * 1040 + (4 * vsl) * 65,
                                [[1040, 128], [1, 4 * 65]]),
                        bass.AP(VSTs[vslot], 0, [[8 * 65, 128], [1, 4 * 65]]),
                        waits=[evk], slot=vslot)
                    vst_free[vslot] = [tokst]
                qkv_done = tokv
            state["qw%d" % slot] = [tok if nsl < 8 else tokv]
        vload = Dma(p, "sync", "vload", n_slots=2)
        vput = [T0, T0]
        negbc_free = T0
        o_first = [qkv_done]
        exp_toks = {}
        NEGH = [144, 160]
        p1_tasks = []

        def emit_p1_head(h):
            ch, b_ = h // 2, 64 * (h % 2)
            base = NEGH[h % 2]

            def tile_task(qt, kg):
                width = 512 if kg < qt // 4 else (qt % 4) * 128 + 128
                bi = (qt + kg) % 2
                mms = [(QF[b_:b_ + 64, ch * T + 128 * qt:ch * T + 128 * qt + 128],
                        KF[b_:b_ + 64, ch * T + 512 * kg:ch * T + 512 * kg + width],
                        PS[bi][0:128, 0:width])]
                if kg == qt // 4:
                    moff = (qt % 4) * 128
                    mms.append((C["itop"][b_:b_ + 64, :], C["m1a"][b_:b_ + 64, :],
                                PS[bi][0:128, moff:moff + 128]))
                    mms.append((C["ibot"][b_:b_ + 64, :], C["m1b"][b_:b_ + 64, :],
                                PS[bi][0:128, moff:moff + 128]))
                tok = mm_group(bi, mms, waits=[qkv_done])
                if kg == 0:
                    nm_tok = V_(lambda e, bi=bi, qt=qt, width=width: e.reduce_max(
                        STC[:, base + qt % 16:base + qt % 16 + 1],
                        PS[bi][0:128, 0:width], axis=AX.X, negate=True),
                        waits=[tok, state.get("negq_%d" % (h % 2), T0)])
                else:
                    t2 = V_(lambda e, bi=bi, width=width: e.reduce_max(
                        STC[:, 176:177], PS[bi][0:128, 0:width], axis=AX.X,
                        negate=True), waits=[tok, state.get("negq_tmp", T0)])
                    nm_tok = V_(lambda e, qt=qt: e.tensor_tensor(
                        STC[:, base + qt % 16:base + qt % 16 + 1],
                        STC[:, base + qt % 16:base + qt % 16 + 1],
                        STC[:, 176:177], OP.min), waits=[t2])
                    state["negq_tmp"] = nm_tok
                consume(bi, nm_tok)
                state["negq_%d" % (h % 2)] = nm_tok

            for qt in range(16):
                for kg in range(qt // 4 + 1):
                    p1_tasks.append(lambda qt=qt, kg=kg: tile_task(qt, kg))

        def p1_pump(k=1):
            for _ in range(min(k, len(p1_tasks))):
                p1_tasks.pop(0)()

        emit_p1_head(0)
        p1_pump(len(p1_tasks))
        for h in range(HEADS):
            ch, b_ = h // 2, 64 * (h % 2)
            vslot = h % 2
            vld = vload.load(
                bass.AP(VTMh[vslot], 0, [[16 * 65, 128], [65, 16], [1, 65]]),
                bass.AP(s_v.tensor, h * 65, [[1040, 128], [128 * 1040, 16], [1, 65]]),
                waits=[vput[vslot], vst_free[0][0], vst_free[1][0]], slot=vslot)
            p1_pump(len(p1_tasks))
            if h + 1 < HEADS:
                emit_p1_head(h + 1)
            p1_done = state["negq_%d" % (h % 2)]
            pv_tok = None
            negbc_toks = []
            for qg in range(4):
                base = NEGH[h % 2]
                tb, nbt = col_to_bcast(
                    lambda j, base=base, qg=qg: STC[:, base + 4 * qg + j:base + 4 * qg + j + 1],
                    4, 4, 5, 2, waits=[p1_done, negbc_free],
                    out_bf16=NEGBC[:, 512 * qg:512 * qg + 512])
                negbc_toks.append(nbt)
            for qg in range(4):
                negbc_tok = negbc_toks[qg]
                nkc = 4 * qg + 4
                pvb = 6 + (qg % 2)
                for kc in range(nkc):
                    p1_pump(1)
                    off = 128 * (kc - 4 * qg) if kc >= 4 * qg else 0
                    bi = 2 + kc % 2
                    mms = [(KF[b_:b_ + 64, ch * T + 128 * kc:ch * T + 128 * kc + 128],
                            QF[b_:b_ + 64, ch * T + 512 * qg + off:ch * T + 512 * qg + 512],
                            PS[bi][0:128, off:512]),
                           (C["r064b"][b_:b_ + 64, 0:128],
                            NEGBC[b_:b_ + 64, 512 * qg + off:512 * qg + 512],
                            PS[bi][0:128, off:512])]
                    if kc >= 4 * qg:
                        mms.append((C["itop"][b_:b_ + 64, :], C["m2a"][b_:b_ + 64, :],
                                    PS[bi][0:128, off:off + 128]))
                        mms.append((C["ibot"][b_:b_ + 64, :], C["m2b"][b_:b_ + 64, :],
                                    PS[bi][0:128, off:off + 128]))
                    tok = mm_group(bi, mms, waits=[negbc_tok, qkv_done])
                    pslot = kc % 2
                    etk = S_(lambda e, bi=bi, off=off, pslot=pslot: e.activation(
                        out=PT[0:128, 512 * pslot + off:512 * pslot + 512],
                        in_=PS[bi][0:128, off:512], func=AF.Exp),
                        waits=[tok, exp_toks.get(pslot, T0)])
                    consume(bi, etk)
                    pv_w = [etk, vld]
                    if kc == 0:
                        pv_w.append(bank_free[pvb])
                    pv_tok = p.emit("tensor",
                                    lambda e, kc=kc, off=off, pslot=pslot, pvb=pvb,
                                    vslot=vslot,
                                    st=(kc == 0), sp=(kc == nkc - 1): e.matmul(
                                        PS[pvb][0:65, off:512],
                                        VTMh[vslot][:, kc * 65:kc * 65 + 65],
                                        PT[0:128, 512 * pslot + off:512 * pslot + 512],
                                        start=st, stop=sp),
                                    waits=pv_w, inc="tE")
                    exp_toks[pslot] = pv_tok
                rc = V_(lambda e, pvb=pvb: e.reciprocal(EPI[64:65, 0:512],
                                                        PS[pvb][64:65, 0:512]),
                        waits=[pv_tok, state.get("epi_free", T0)])
                tb2 = mm_group(5, [(C["r064f"][64:65, 0:128], EPI[64:65, 0:512],
                                    PS[5][0:128, 0:512])], waits=[rc])
                cpy = V_(lambda e: e.tensor_copy(EPI[:, 0:512], PS[5][0:128, 0:512]),
                         waits=[tb2])
                consume(5, cpy)
                ot = V_(lambda e, b_=b_, ch=ch, qg=qg, pvb=pvb: e.tensor_mul(
                    XN[b_:b_ + 64, ch * T + 512 * qg:ch * T + 512 * qg + 512],
                    PS[pvb][0:64, 0:512], EPI[0:64, 0:512]),
                    waits=[cpy] + o_first)
                o_first = []
                consume(pvb, ot)
                state["epi_free"] = ot
            negbc_free = tok
            vput[vslot] = pv_tok
        o_tok = ("vE", p.cnt["vE"])
        sclose()
        bar3 = barrier()
        AUX, CP = sopen(("a3AUX", [128, NC * T], F32), ("aCP", [128, NC * 1024], BF16))
        rl = dLx.load(bass.AP(AUX, 0, [[NC * T, 128], [T, NC], [1, T]]),
                      bass.AP(s_lnx.tensor, 0, [[T, 128], [128 * T, NC], [1, T]]),
                      waits=[slnx] + bar3)
        cpw = Dma(p, "sync", "cpw")
        cpl = cpw.load(bass.AP(CP, 0, [[NC * 1024, 128], [1024, NC], [1, 1024]]),
                       bass.AP(s_cp.tensor, 0, [[1024, 128], [128 * 1024, NC], [1, 1024]]),
                       waits=[state.get("gout_cp", T0)] + bar3)
        lastx = state["x"]
        hm_tok = None
        for nc_ in range(NC):
            for g in range(NTG):
                bi = (nc_ * NTG + g) % 2
                mms = [(CP[:, kc * 1024 + nc_ * 128:kc * 1024 + nc_ * 128 + 128],
                        XN[:, T * kc + TG * g:T * kc + TG * g + TG])
                       for kc in range(NC)]
                tok = mm_group(bi, mms, waits=[cpl, o_tok])
                hm = V_(lambda e, bi=bi, nc_=nc_, g=g: e.scalar_tensor_tensor(
                    AUX[:, T * nc_ + TG * g:T * nc_ + TG * g + TG],
                    PS[bi][0:128, 0:TG], 1.0,
                    AUX[:, T * nc_ + TG * g:T * nc_ + TG * g + TG], OP.mult, OP.add),
                    waits=[tok, rl])
                consume(bi, hm)
                hm_tok = hm
            hm_tok = V_(lambda e, nc_=nc_: e.tensor_scalar(
                AUX[:, T * nc_:T * nc_ + T], AUX[:, T * nc_:T * nc_ + T],
                GB["cp"]["bias"][:, nc_:nc_ + 1], None, OP.add),
                waits=[hm_tok, gbias["cp"]])
            lastx = V_(lambda e, nc_=nc_: e.tensor_add(
                X[:, T * nc_:T * nc_ + T], X[:, T * nc_:T * nc_ + T],
                AUX[:, T * nc_:T * nc_ + T]), waits=[hm_tok, lastx])
        state["x"] = lastx
        st2 = stats(AUX, hm_tok)
        xn2_tok = materialize_ln_full(AUX, hm_tok, SM["ln2g"], SM["ln2b"], st2)
        sclose()
        bar4 = barrier()
        Y, WSa, WSb, WTa, WTb = sopen(
            ("a4Y", [128, 32 * TG], BF16),
            ("a4WSa", [128, 2048], BF16), ("a4WSb", [128, 2048], BF16),
            ("a4WTa", [128, 4096], BF16), ("a4WTb", [128, 4096], BF16))
        ffn_stream("a4", [WSa, WSb], [WTa, WTb], xn2_tok, s_cf,
                   state.get("gout_cf", T0), s_cm, state.get("gout_cm", T0),
                   GB["cf"]["bias"], gbias["cf"], AF.Gelu_apprx_tanh,
                   None, T0, 1.0, Y, y_free0=xn2_tok, bar=bar4,
                   b2_cols=(gbias["cm"], GB["cm"]["bias"]))
        sclose()

    if n_sub >= 2:
        pump_all()
        sclose()
        sub_l0attn()

    h_sel8b = din("sel8b", [8, 8 * 128], BF16)
    GM1, GA1, GM2, GA2, GW1, GW2, GTM = 144, 160, 176, 192, 208, 224, 240

    def sub_l1ffn(pref):
        bar = barrier()
        st_tok = stats(X, state["x"])
        CFM, SEL8B = sopen((pref + "CFM", [8, 2048], BF16),
                           (pref + "SEL8B", [8, 8 * 128], BF16))
        dsel = Dma(p, "sync", pref + "dsel")
        selt = dsel.load(SEL8B[:], h_sel8b, waits=bar)
        AUX, SCO, IT8, FT8, FT8b = sopen(
            (pref + "AUX", [128, NC * T], F32), (pref + "SCO", [128, 128], F32),
            (pref + "IT8", [128, 8], I32), (pref + "FT8", [128, 8], F32),
            (pref + "FT8b", [128, 8], F32))
        lnx_tok = materialize_lnx(AUX, st_tok, bar=bar)
        xn_tok = None
        for c in range(NC):
            xn_tok = V_(lambda e, c=c: e.tensor_copy(
                XN[:, T * c:T * c + T], AUX[:, T * c:T * c + T]), waits=[lnx_tok])
        glast = None
        for tt in range(16):
            mms = [(AUX[:, T * kc + 128 * tt:T * kc + 128 * tt + 128],
                    SM["gw"][:, kc * 8:kc * 8 + 8],
                    PS[4][0:128, 8 * tt:8 * tt + 8]) for kc in range(NC)]
            glast = mm_group(4, mms, waits=[lnx_tok, CW])
        sg_tok = S_(lambda e: e.activation(out=SCO[:], in_=PS[4][0:128, 0:128],
                                           func=AF.Sigmoid), waits=[glast])
        consume(4, sg_tok)
        a = sg_tok
        for tt in range(16):
            sl = SCO[:, 8 * tt:8 * tt + 8]
            a = V_(lambda e, sl=sl, tt=tt: e.reduce_max(
                STC[:, GM1 + tt:GM1 + tt + 1], sl, axis=AX.X), waits=[a])
            a = V_(lambda e, sl=sl, tt=tt: e.tensor_scalar(
                IT8[:], sl, STC[:, GM1 + tt:GM1 + tt + 1], None, OP.is_equal),
                waits=[a])
            a = V_(lambda e: e.select(FT8[:], IT8[:], C["iota8"][:], C["c999"][:],
                                      add_drain=True), waits=[a])
            a = V_(lambda e, tt=tt: e.tensor_reduce(
                STC[:, GA1 + tt:GA1 + tt + 1], FT8[:], AX.X, OP.min), waits=[a])
            a = V_(lambda e, tt=tt: e.tensor_scalar(
                IT8[:], C["iota8"][:], STC[:, GA1 + tt:GA1 + tt + 1], None,
                OP.is_equal), waits=[a])
            a = V_(lambda e, sl=sl: e.select(FT8[:], IT8[:], C["cneg"][:], sl,
                                             add_drain=True), waits=[a])
            a = V_(lambda e, tt=tt: e.reduce_max(
                STC[:, GM2 + tt:GM2 + tt + 1], FT8[:], axis=AX.X), waits=[a])
            a = V_(lambda e, tt=tt: e.tensor_scalar(
                IT8[:], FT8[:], STC[:, GM2 + tt:GM2 + tt + 1], None, OP.is_equal),
                waits=[a])
            a = V_(lambda e: e.select(FT8[:], IT8[:], C["iota8"][:], C["c999"][:],
                                      add_drain=True), waits=[a])
            a = V_(lambda e, tt=tt: e.tensor_reduce(
                STC[:, GA2 + tt:GA2 + tt + 1], FT8[:], AX.X, OP.min), waits=[a])
        a = V_(lambda e: e.tensor_add(STC[:, GTM:GTM + 16], STC[:, GM1:GM1 + 16],
                                      STC[:, GM2:GM2 + 16]), waits=[a])
        a = V_(lambda e: e.tensor_scalar(STC[:, GTM:GTM + 16], STC[:, GTM:GTM + 16],
                                         1e-20, None, OP.add), waits=[a])
        a = V_(lambda e: e.reciprocal(STC[:, GTM:GTM + 16], STC[:, GTM:GTM + 16]),
               waits=[a])
        a = V_(lambda e: e.scalar_tensor_tensor(
            STC[:, GW1:GW1 + 16], STC[:, GM1:GM1 + 16], 2.5, STC[:, GTM:GTM + 16],
            OP.mult, OP.mult), waits=[a])
        a = V_(lambda e: e.scalar_tensor_tensor(
            STC[:, GW2:GW2 + 16], STC[:, GM2:GM2 + 16], 2.5, STC[:, GTM:GTM + 16],
            OP.mult, OP.mult), waits=[a])
        for tt in range(16):
            a = V_(lambda e, tt=tt: e.tensor_scalar(
                FT8[:], C["iota8"][:], STC[:, GA1 + tt:GA1 + tt + 1], None,
                OP.is_equal), waits=[a])
            a = V_(lambda e, tt=tt: e.tensor_scalar(
                FT8[:], FT8[:], STC[:, GW1 + tt:GW1 + tt + 1], None, OP.mult),
                waits=[a])
            a = V_(lambda e, tt=tt: e.tensor_scalar(
                FT8b[:], C["iota8"][:], STC[:, GA2 + tt:GA2 + tt + 1], None,
                OP.is_equal), waits=[a])
            a = V_(lambda e, tt=tt: e.tensor_scalar(
                FT8b[:], FT8b[:], STC[:, GW2 + tt:GW2 + tt + 1], None, OP.mult),
                waits=[a])
            a = V_(lambda e: e.tensor_add(FT8[:], FT8[:], FT8b[:]), waits=[a])
            tokt = TT_(lambda e: e.transpose(PS[5][0:8, 0:128], FT8[:],
                                             C["ident_f"][:]),
                       waits=[a, bank_free[5]])
            a = V_(lambda e, tt=tt: e.tensor_copy(CFM[0:8, 128 * tt:128 * tt + 128],
                                                  PS[5][0:8, 0:128]), waits=[tokt])
            consume(5, a)
        cfm_tok = a
        sclose()
        bar2 = barrier()
        bufs = sopen(
            (pref + "HMS", [128, 8 * 512], BF16),
            (pref + "HM0", [128, 4 * 512], BF16), (pref + "HM1", [128, 4 * 512], BF16),
            (pref + "HM2", [128, 4 * 512], BF16), (pref + "HM3", [128, 4 * 512], BF16),
            (pref + "HM4", [128, 4 * 512], BF16), (pref + "HM5", [128, 4 * 512], BF16),
            (pref + "HM6", [128, 4 * 512], BF16), (pref + "HM7", [128, 4 * 512], BF16),
            (pref + "SGB", [128, 8 * 512], BF16),
            (pref + "GUW0", [128, 8 * 256], BF16), (pref + "GUW1", [128, 8 * 256], BF16),
            (pref + "WD0", [128, 1024], BF16), (pref + "WD1", [128, 1024], BF16))
        HMS = bufs[0]
        HM = bufs[1:9]
        SGB, GUW0, GUW1, WD0, WD1 = bufs[9:]
        GUW = [GUW0, GUW1]
        WD = [WD0, WD1]
        guw = Dma(p, "gpsimd", pref + "guw", n_slots=2)
        wdd = Dma(p, "gpsimd", pref + "wdd", n_slots=2)
        lastx = state["x"]
        guw_free = [list(bar2), list(bar2)]
        wd_free = [list(bar2), list(bar2)]
        hm_free = {}
        for e0 in range(8):
            hm_free[("e%d" % e0, -1)] = T0
        sg_free = {}
        slot_ctr = [0]

        def gu_block(g, wgh, wuh, base_off, NI, HBUF, key, cbc):
            nonlocal lastx
            cbc_tok = None
            if cbc is not None:
                cbc_tok = mm_group(4, [(SEL8B[0:8, cbc * 128:cbc * 128 + 128],
                                        CFM[0:8, TG * g:TG * g + TG],
                                        PS[4][0:128, 0:TG])],
                                   waits=[cfm_tok, selt])
            sg_last = None
            for which, wh in ((0, wgh), (1, wuh)):
                for sl_ in range(NI // 256):
                    slot = slot_ctr[0] % 2
                    slot_ctr[0] += 1
                    ldt = guw.load(
                        bass.AP(GUW[slot], 0, [[8 * 256, 128], [256, NC], [1, 256]]),
                        bass.AP(wh.tensor, base_off + sl_ * 256,
                                [[NI, 128], [128 * NI, NC], [1, 256]]),
                        waits=list(guw_free[slot]), slot=slot)
                    for nsub in range(2):
                        ic = sl_ * 2 + nsub
                        bi = ic % 4 if ic < 8 else ic % 2
                        mms = [(GUW[slot][:, kc * 256 + nsub * 128:kc * 256 + nsub * 128 + 128],
                                XN[:, T * kc + TG * g:T * kc + TG * g + TG])
                               for kc in range(NC)]
                        tok = mm_group(bi, mms, waits=[ldt, xn_tok])
                        if which == 0:
                            sgt = S_(lambda e, bi=bi, ic=ic: e.activation(
                                out=SGB[:, ic * 512:ic * 512 + 512],
                                in_=PS[bi][0:128, 0:TG], func=AF.Silu),
                                waits=[tok, sg_free.get(ic, T0)])
                            consume(bi, sgt)
                            sg_free[ic] = sgt
                            sg_last = sgt
                        else:
                            hmt = V_(lambda e, bi=bi, ic=ic, HBUF=HBUF:
                                     e.scalar_tensor_tensor(
                                         HBUF[:, ic * 512:ic * 512 + 512],
                                         PS[bi][0:128, 0:TG], 1.0,
                                         SGB[:, ic * 512:ic * 512 + 512],
                                         OP.mult, OP.mult),
                                     waits=[tok, sg_free[ic],
                                            hm_free.get((key, ic), T0)])
                            if cbc is not None:
                                hmt = V_(lambda e, ic=ic, HBUF=HBUF: e.tensor_mul(
                                    HBUF[:, ic * 512:ic * 512 + 512],
                                    HBUF[:, ic * 512:ic * 512 + 512],
                                    PS[4][0:128, 0:TG]), waits=[hmt, cbc_tok])
                            consume(bi, hmt)
                            hm_free[(key, ic)] = hmt
                    guw_free[slot] = [tok]
            if cbc is not None:
                consume(4, hm_free[(key, NI // 128 - 1)])
            return

        for g in range(NTG):
            gu_block(g, h_wsg, h_wsu, 0, 1024, HMS, "s", None)
            for e_ in range(8):
                gu_block(g, h_wg, h_wu, e_ * D * 512, 512, HM[e_], "e%d" % e_, e_)
            dn = [("s", HMS, ic, h_wsd.tensor, ic * 128 * 1024) for ic in range(8)]
            for e_ in range(8):
                dn += [("e%d" % e_, HM[e_], ic, h_wd.tensor,
                        e_ * 512 * 1024 + ic * 128 * 1024) for ic in range(4)]
            z_toks = [None] * 8
            for di, (key, HBUF, ic, wt, woff) in enumerate(dn):
                slot = di % 2
                ldt = wdd.load(bass.AP(WD[slot], 0, [[1024, 128], [1, 1024]]),
                               bass.AP(wt, woff, [[1024, 128], [1, 1024]]),
                               waits=list(wd_free[slot]), slot=slot)
                lastmm = None
                for nc_ in range(8):
                    if nc_ == 0:
                        w0 = [ldt, hm_free[(key, ic)]]
                    else:
                        w0 = []
                    if di == 0:
                        w0 = w0 + [bank_free[nc_]]
                    lastmm = p.emit(
                        "tensor",
                        lambda e, nc_=nc_, slot=slot, HBUF=HBUF, ic=ic,
                        st=(di == 0), sp=(di == len(dn) - 1): e.matmul(
                            PS[nc_][0:128, 0:TG],
                            WD[slot][:, nc_ * 128:nc_ * 128 + 128],
                            HBUF[:, ic * 512:ic * 512 + 512], start=st, stop=sp),
                        waits=w0,
                        inc=("tE" if (nc_ == 7 or di == len(dn) - 1) else None))
                    if di == len(dn) - 1:
                        z_toks[nc_] = lastmm
                wd_free[slot] = [lastmm]
            for nc_ in range(8):
                xt = V_(lambda e, nc_=nc_, g=g: e.scalar_tensor_tensor(
                    X[:, T * nc_ + TG * g:T * nc_ + TG * g + TG],
                    PS[nc_][0:128, 0:TG], 0.5,
                    X[:, T * nc_ + TG * g:T * nc_ + TG * g + TG], OP.mult, OP.add),
                    waits=[z_toks[nc_], lastx])
                consume(nc_, xt)
                lastx = xt
            for key, HBUF, ic, wt, woff in dn:
                hm_free[(key, ic)] = z_toks[7]
            for ic in range(8):
                sg_free[ic] = z_toks[7]
        state["x"] = lastx
        sclose()
        sclose()

    def sub_l1attn():
        bar = barrier()
        st_tok = stats(X, state["x"], rms_eps=1e-6)
        AUX, = sopen(("l5AUX", [128, NC * T], F32))
        lnx_tok = materialize_lnx(AUX, st_tok, bar=bar)
        xn_tok = materialize_xn(AUX, lnx_tok, RRC, SM["l1w"], None)
        lastx = state["x"]
        for c in range(NC):
            lastx = V_(lambda e, c=c: e.tensor_add(
                X[:, T * c:T * c + T], X[:, T * c:T * c + T],
                AUX[:, T * c:T * c + T]), waits=[lnx_tok, lastx])
        state["x"] = lastx
        sclose()
        bar2 = barrier()
        FB, OB, GBF, SB_, WSL0, WSL1, WSMB, WSMF = sopen(
            ("l5FB", [128, T], F32), ("l5OB", [128, T], F32),
            ("l5GB", [128, T], F32), ("l5SB", [128, T], F32),
            ("l5W0", [128, 8 * 128], BF16), ("l5W1", [128, 8 * 128], BF16),
            ("l5WSM", [128, 8], BF16), ("l5WSF", [128, 8], F32))
        WSL = [WSL0, WSL1]
        wl = Dma(p, "gpsimd", "l5wl", n_slots=2)
        wl_free = [list(bar2), list(bar2)]
        fb_free = list(bar2)
        ob_free = list(bar2)
        gb_free = list(bar2)
        sb_free = list(bar2)
        wsm_free = T0
        sctr = [0]

        def wload(wh, c):
            slot = sctr[0] % 2
            sctr[0] += 1
            t = wl.load(bass.AP(WSL[slot], 0, [[8 * 128, 128], [128, NC], [1, 128]]),
                        bass.AP(wh.tensor, c * 128, [[D, 128], [128 * D, NC], [1, 128]]),
                        waits=list(wl_free[slot]), slot=slot)
            return slot, t

        for c in range(NC):
            slot, ldt = wload(h_wf, c)
            fts = []
            for g in range(NTG):
                bi = g % 2
                mms = [(WSL[slot][:, kc * 128:kc * 128 + 128],
                        XN[:, T * kc + TG * g:T * kc + TG * g + TG])
                       for kc in range(NC)]
                tok = mm_group(bi, mms, waits=[ldt, xn_tok])
                ft = S_(lambda e, bi=bi, g=g: e.activation(
                    out=FB[:, TG * g:TG * g + TG], in_=PS[bi][0:128, 0:TG],
                    func=AF.Sigmoid), waits=[tok] + (fb_free if g == 0 else []))
                consume(bi, ft)
                fts.append(ft)
            wl_free[slot] = [tok]
            fmax = V_(lambda e: e.tensor_scalar(FB[:], FB[:], 0.9, None, OP.max),
                      waits=[fts[-1]])
            slot, ldt = wload(h_wo, c)
            ots = []
            for g in range(NTG):
                bi = g % 2
                mms = [(WSL[slot][:, kc * 128:kc * 128 + 128],
                        XN[:, T * kc + TG * g:T * kc + TG * g + TG])
                       for kc in range(NC)]
                tok = mm_group(bi, mms, waits=[ldt, xn_tok])
                ot = S_(lambda e, bi=bi, g=g: e.activation(
                    out=OB[:, TG * g:TG * g + TG], in_=PS[bi][0:128, 0:TG],
                    func=AF.Sigmoid), waits=[tok] + (ob_free if g == 0 else []))
                consume(bi, ot)
                ots.append(ot)
            wl_free[slot] = [tok]
            slot, ldt = wload(h_wi, c)
            wst = None
            for kc in range(NC):
                wst = V_(lambda e, kc=kc, slot=slot: e.reduce_sum(
                    WSMF[:, kc:kc + 1], WSL[slot][:, kc * 128:kc * 128 + 128],
                    axis=AX.X), waits=[ldt, wsm_free])
            wst = V_(lambda e: e.tensor_copy(WSMB[:], WSMF[:]), waits=[wst])
            wl_free[slot] = [wst]
            ginit = V_(lambda e: e.tensor_scalar(GBF[:], FB[:], -1.0, 1.0,
                                                 OP.mult, OP.add),
                       waits=[fmax] + gb_free)
            glast = None
            for g in range(NTG):
                tokv = mm_group(6, [(WSMB[:, kc:kc + 1],
                                     XN[:, T * kc + TG * g:T * kc + TG * g + TG],
                                     PS[6][0:1, 0:TG]) for kc in range(NC)],
                                waits=[wst, xn_tok])
                cpv = V_(lambda e: e.tensor_copy(ROWS[0:1, 0:TG], PS[6][0:1, 0:TG]),
                         waits=[tokv, rows_free[0]])
                consume(6, cpv)
                tokb = mm_group(5, [(C["r064f"][0:1, 0:128], ROWS[0:1, 0:TG],
                                     PS[5][0:128, 0:TG])], waits=[cpv])
                rows_free[0] = tokb
                glast = V_(lambda e, g=g: e.tensor_mul(
                    GBF[:, TG * g:TG * g + TG], GBF[:, TG * g:TG * g + TG],
                    PS[5][0:128, 0:TG]), waits=[tokb, ginit])
                consume(5, glast)
            wsm_free = glast
            sc = V_(lambda e: e.tensor_tensor_scan(SB_[:], FB[:], GBF[:], 0.0,
                                                   OP.mult, OP.add),
                    waits=[glast, fmax] + sb_free)
            fb_free = [sc]
            gb_free = [sc]
            yv = V_(lambda e: e.tensor_mul(SB_[:], SB_[:], OB[:]), waits=[sc, ots[-1]])
            ob_free = [yv]
            lastx = V_(lambda e, c=c: e.tensor_add(
                X[:, T * c:T * c + T], X[:, T * c:T * c + T], SB_[:]),
                waits=[yv, lastx])
            sb_free = [lastx]
        state["x"] = lastx
        sclose()

    if n_sub >= 3:
        sub_l0ffn("f3", t1, t2)
    if n_sub >= 4:
        sub_l1ffn("m4")
    if n_sub >= 5:
        sub_l1attn()
    if n_sub >= 6:
        sub_l1ffn("m6")


    st = dX.load(bass.AP(out_h.tensor, 0, [[T, 128], [128 * T, NC], [1, T]]),
                 bass.AP(X, 0, [[NC * T, 128], [T, NC], [1, T]]), waits=[state["x"]],
                 slot=0)
    p.emit("sync", None, waits=[st])

    with nc.Block() as block:
        @block.sync
        def _(e):
            p.replay("sync", e)

        @block.tensor
        def _(e):
            p.replay("tensor", e)

        @block.vector
        def _(e):
            p.replay("vector", e)

        @block.scalar
        def _(e):
            p.replay("scalar", e)

        @block.gpsimd
        def _(e):
            p.replay("gpsimd", e)

    es.close()
    return nc


def kernel(**inputs):
    in_maps = _host_inputs(inputs)
    nc = build()
    if _RUN is not None:
        results = _RUN(nc, in_maps)
    else:
        res = run_bass_kernel_spmd(nc, in_maps, core_ids=list(range(8)))
        results = res.results
    out = np.stack([np.asarray(r["out"]).T for r in results])
    return np.ascontiguousarray(out.astype(np.float32))


# version 05
# speedup vs baseline: 1.1290x; 1.0456x over previous
import numpy as np
import ml_dtypes
import concourse.bass as bass
import concourse.mybir as mybir
from concourse.bass_utils import run_bass_kernel_spmd

F32 = mybir.dt.float32
BF16 = mybir.dt.bfloat16
I32 = mybir.dt.int32
AF = mybir.ActivationFunctionType
OP = mybir.AluOpType
AX = mybir.AxisListType

D = 1024
T = 2048
NC = 8
TG = 512
NTG = T // TG
HEADS = 16
NEGBIG = -1073741824.0

_N_SUB = 6
_RUN = None



def _t(a):
    return np.ascontiguousarray(np.asarray(a, np.float32).T)


def _cols(v, n):
    return np.ascontiguousarray(np.asarray(v, np.float32).reshape(n, 128).T)


def _consts():
    c = {}
    c["ident_f"] = np.eye(128, dtype=np.float32)
    itop = np.zeros((128, 128), dtype=ml_dtypes.bfloat16)
    ibot = np.zeros((128, 128), dtype=ml_dtypes.bfloat16)
    for q in range(64):
        itop[q, q] = 1
        itop[64 + q, q] = 1
        ibot[q, 64 + q] = 1
        ibot[64 + q, 64 + q] = 1
    c["itop"] = itop
    c["ibot"] = ibot
    m1 = np.where(np.arange(128)[None, :] > np.arange(128)[:, None], NEGBIG, 0.0)
    m1 = m1.astype(ml_dtypes.bfloat16)
    c["m1a"] = np.concatenate([m1[0:64], m1[0:64]], axis=0)
    c["m1b"] = np.concatenate([m1[64:128], m1[64:128]], axis=0)
    m2 = np.where(np.arange(128)[None, :] < np.arange(128)[:, None], NEGBIG, 0.0)
    m2 = m2.astype(ml_dtypes.bfloat16)
    c["m2a"] = np.concatenate([m2[0:64], m2[0:64]], axis=0)
    c["m2b"] = np.concatenate([m2[64:128], m2[64:128]], axis=0)
    r064 = np.zeros((128, 128), np.float32)
    r064[0, :] = 1.0
    r064[64, :] = 1.0
    c["r064f"] = r064
    c["r064b"] = r064.astype(ml_dtypes.bfloat16)
    c["invd"] = np.full((128, 1), 1.0 / D, np.float32)
    c["invdb"] = np.full((128, 1), 1.0 / D, ml_dtypes.bfloat16)
    c["eps5"] = np.full((128, 1), 1e-5, np.float32)
    c["eps6"] = np.full((128, 1), 1e-6, np.float32)
    c["onescol"] = np.ones((128, 1), np.float32)
    c["iota8"] = np.broadcast_to(np.arange(8, dtype=np.float32), (128, 8)).copy()
    c["c999"] = np.full((128, 8), 999.0, np.float32)
    c["cneg"] = np.full((128, 8), -1e30, np.float32)
    sel8 = np.zeros((8, 8 * 128), np.float32)
    for e in range(8):
        sel8[e, e * 128:(e + 1) * 128] = 1.0
    c["sel8"] = sel8
    c["sel8b"] = sel8.astype(ml_dtypes.bfloat16)
    return c


CONST_SPECS = [
    ("ident_f", [128, 128], F32), ("itop", [128, 128], BF16), ("ibot", [128, 128], BF16),
    ("m1a", [128, 128], BF16), ("m1b", [128, 128], BF16),
    ("m2a", [128, 128], BF16), ("m2b", [128, 128], BF16),
    ("r064f", [128, 128], F32), ("r064b", [128, 128], BF16),
    ("invd", [128, 1], F32), ("invdb", [128, 1], BF16),
    ("eps5", [128, 1], F32), ("eps6", [128, 1], F32),
    ("onescol", [128, 1], F32),
    ("iota8", [128, 8], F32), ("c999", [128, 8], F32), ("cneg", [128, 8], F32),
]

GEN_MATS = (("ca", 1024, 3072), ("cp", 1024, 1024), ("cf", 1024, 4096), ("cm", 4096, 1024))


def _host_inputs(inp):
    cst = _consts()
    shared = {}
    for nm, I_, O in GEN_MATS:
        key = "L0_attn_" + nm
        sub = np.asarray(inp[key + "_sub_w"], np.float32)
        shared[nm + "_s0"] = _t(sub[0])
        shared[nm + "_s1"] = _t(sub[1])
        shared[nm + "_W"] = _t(inp[key + "_W_w"])
        shared[nm + "_B"] = _t(inp[key + "_B_w"])
        wl = np.asarray(inp[key + "_Wl_w"], np.float32)
        shared[nm + "_wl"] = np.broadcast_to(wl[0][None, :], (128, 2)).copy()
        nco = O // 128
        shared[nm + "_sb"] = np.concatenate(
            [_cols(inp[key + "_sub_b"][0], nco), _cols(inp[key + "_sub_b"][1], nco)], axis=1)
        shared[nm + "_Wb"] = _cols(inp[key + "_W_b"], nco)
        shared[nm + "_Bb"] = _cols(inp[key + "_B_b"], nco)
        wlb = np.asarray(inp[key + "_Wl_b"], np.float32)
        shared[nm + "_wlb"] = np.broadcast_to(wlb[0][None, :], (128, 2)).copy()
    shared["ln1g"] = _cols(inp["L0_attn_ln1_g"], NC)
    shared["ln1b"] = _cols(inp["L0_attn_ln1_b"], NC)
    shared["ln2g"] = _cols(inp["L0_attn_ln2_g"], NC)
    shared["ln2b"] = _cols(inp["L0_attn_ln2_b"], NC)
    shared["l0g"] = _cols(inp["L0_ffn_ln_gamma"], NC)
    shared["l0b"] = _cols(inp["L0_ffn_ln_beta"], NC)
    shared["fc1q"] = np.ascontiguousarray(np.asarray(inp["L0_ffn_fc1_q"], np.int32))
    shared["fc1s"] = np.ascontiguousarray(np.asarray(inp["L0_ffn_fc1_scales"], np.float32))
    shared["fc1b"] = _cols(inp["L0_ffn_fc1_bias"], 32)
    shared["fc2q"] = np.ascontiguousarray(np.asarray(inp["L0_ffn_fc2_q"], np.int32))
    s2 = np.asarray(inp["L0_ffn_fc2_scales"], np.float32)
    shared["fc2s"] = np.ascontiguousarray(
        s2.reshape(4, 8, 1024).transpose(1, 0, 2).reshape(8, 4096))
    shared["fc2bcol"] = _cols(inp["L0_ffn_fc2_bias"], NC)
    shared["l1w"] = _cols(inp["L1_attn_norm_w"], NC)
    shared["wf"] = _t(inp["L1_attn_w_forget"])
    shared["wi"] = _t(inp["L1_attn_w_input"])
    shared["wo"] = _t(inp["L1_attn_w_output"])
    shared["gw"] = np.ascontiguousarray(
        _t(inp["L1_ffn_gate_weight"]).reshape(8, 128, 8).transpose(1, 0, 2).reshape(128, 64))
    shared["wg"] = np.ascontiguousarray(np.asarray(inp["L1_ffn_w_gate"], np.float32).transpose(0, 2, 1))
    shared["wu"] = np.ascontiguousarray(np.asarray(inp["L1_ffn_w_up"], np.float32).transpose(0, 2, 1))
    shared["wd"] = np.ascontiguousarray(np.asarray(inp["L1_ffn_w_down"], np.float32).transpose(0, 2, 1))
    shared["wsg"] = _t(inp["L1_ffn_ws_gate"])
    shared["wsu"] = _t(inp["L1_ffn_ws_up"])
    shared["wsd"] = _t(inp["L1_ffn_ws_down"])
    shared.update(cst)

    x = np.asarray(inp["L0_attn_hs"], np.float32)
    in_maps = []
    for c in range(8):
        m = dict(shared)
        m["x"] = _t(x[c])
        in_maps.append(m)
    return in_maps



ENGS = ("sync", "tensor", "vector", "scalar", "gpsimd")
T0 = ("tE", 0)


class P:
    def __init__(self, nc, es):
        self.nc = nc
        self.es = es
        self.q = {e: [] for e in ENGS}
        self.cnt = {}
        self.sems = {}

    def sem(self, name):
        if name not in self.sems:
            self.sems[name] = self.es.enter_context(self.nc.semaphore(name))
            self.cnt[name] = 0
        return name

    def emit(self, eng, fn, waits=(), inc=None, n=1):
        ws = []
        for s, v in waits:
            if v <= 0:
                continue
            assert v <= self.cnt.get(s, 0), f"wait on future {s}:{v}>{self.cnt.get(s, 0)}"
            ws.append((s, v))
        self.q[eng].append((ws, fn, inc, n))
        if inc is not None:
            self.sem(inc)
            self.cnt[inc] += n
            return (inc, self.cnt[inc])
        return None

    def replay(self, ename, engine):
        maxw = {}
        for ws, fn, inc, n in self.q[ename]:
            for s, v in ws:
                if maxw.get(s, 0) < v:
                    engine.wait_ge(self.sems[s], v)
                    maxw[s] = v
            if fn is None:
                continue
            inst = fn(engine)
            if inc is not None:
                inst.then_inc(self.sems[inc], n)


class Dma:

    def __init__(self, p, eng, name, n_slots=1):
        self.p = p
        self.eng = eng
        self.sems = [p.sem(f"{name}{i}") for i in range(n_slots)]

    def load(self, dst, src, waits=(), slot=0):
        return self.p.emit(self.eng, lambda e, d=dst, s=src: e.dma_start(out=d, in_=s),
                           waits=waits, inc=self.sems[slot], n=16)



def build(n_sub=None):
    if n_sub is None:
        n_sub = _N_SUB
    nc = bass.Bass("TRN2", target_bir_lowering=False)
    from contextlib import ExitStack
    es = ExitStack()
    p = P(nc, es)

    def din(name, shape, dt=F32):
        return nc.dram_tensor(name, shape, dt, kind="ExternalInput").ap()

    x_h = din("x", [D, T])
    out_h = nc.dram_tensor("out", [D, T], F32, kind="ExternalOutput").ap()
    cst_h = {n_: din(n_, sh, dt) for n_, sh, dt in CONST_SPECS}
    h_sel8 = din("sel8", [8, 8 * 128])
    genh = {}
    for nm, I_, O in GEN_MATS:
        nco = O // 128
        genh[nm] = dict(
            s0=din(nm + "_s0", [I_, O]), s1=din(nm + "_s1", [I_, O]),
            W=din(nm + "_W", [I_, O]), B=din(nm + "_B", [I_, O]),
            wl=din(nm + "_wl", [128, 2]), sb=din(nm + "_sb", [128, 2 * nco]),
            Wb=din(nm + "_Wb", [128, nco]), Bb=din(nm + "_Bb", [128, nco]),
            wlb=din(nm + "_wlb", [128, 2]), O=O, I=I_, nco=nco)
    smh = {n_: din(n_, sh) for n_, sh in [
        ("ln1g", [128, NC]), ("ln1b", [128, NC]), ("ln2g", [128, NC]), ("ln2b", [128, NC]),
        ("l0g", [128, NC]), ("l0b", [128, NC]), ("fc1b", [128, 32]),
        ("l1w", [128, NC]), ("gw", [128, 64]), ("fc2bcol", [128, NC])]}
    h_fc1s = din("fc1s", [8, 4096])
    h_fc2s = din("fc2s", [8, 4096])
    h_fc1q = din("fc1q", [D, 4096], I32)
    h_fc2q = din("fc2q", [4096, D], I32)
    h_wf = din("wf", [D, D]); h_wi = din("wi", [D, D]); h_wo = din("wo", [D, D])
    h_wg = din("wg", [8, D, 512]); h_wu = din("wu", [8, D, 512]); h_wd = din("wd", [8, 512, D])
    h_wsg = din("wsg", [D, D]); h_wsu = din("wsu", [D, D]); h_wsd = din("wsd", [D, D])

    s_fc1 = nc.dram_tensor("s_fc1", [D, 4096], BF16, kind="Internal").ap()
    s_fc2 = nc.dram_tensor("s_fc2", [4096, D], BF16, kind="Internal").ap()
    s_cf = nc.dram_tensor("s_cf", [D, 4096], BF16, kind="Internal").ap()
    s_cm = nc.dram_tensor("s_cm", [4096, D], BF16, kind="Internal").ap()
    s_qkv = nc.dram_tensor("s_qkv", [D, 3072], BF16, kind="Internal").ap()
    s_cp = nc.dram_tensor("s_cp", [D, D], BF16, kind="Internal").ap()
    s_lnx = nc.dram_tensor("s_lnx", [D, T], F32, kind="Internal").ap()
    s_wg = nc.dram_tensor("s_wg", [8, D, 512], BF16, kind="Internal").ap()
    s_wu = nc.dram_tensor("s_wu", [8, D, 512], BF16, kind="Internal").ap()
    s_wd = nc.dram_tensor("s_wd", [8, 512, D], BF16, kind="Internal").ap()
    s_wsg = nc.dram_tensor("s_wsg", [D, D], BF16, kind="Internal").ap()
    s_wsu = nc.dram_tensor("s_wsu", [D, D], BF16, kind="Internal").ap()
    s_wsd = nc.dram_tensor("s_wsd", [D, D], BF16, kind="Internal").ap()
    s_v = nc.dram_tensor("s_v", [T, HEADS * 65], BF16, kind="Internal").ap()

    def sb(name, shape, dt):
        return es.enter_context(nc.sbuf_tensor(name, shape, dt))

    X = sb("X", [128, NC * T], F32)
    XN = sb("XN", [128, NC * T], BF16)
    C = {n_: sb("c_" + n_, sh, dt) for n_, sh, dt in CONST_SPECS}
    SM = {n_: sb("p_" + n_, sh, F32) for n_, sh in [
        ("ln1g", [128, NC]), ("ln1b", [128, NC]), ("ln2g", [128, NC]), ("ln2b", [128, NC]),
        ("l0g", [128, NC]), ("l0b", [128, NC]), ("fc1b", [128, 32]),
        ("l1w", [128, NC]), ("gw", [128, 64]), ("fc2bcol", [128, NC])]}
    GB = {}
    for nm, I_, O in GEN_MATS:
        nco = O // 128
        GB[nm] = dict(sb=sb(nm + "sb", [128, 2 * nco], F32), Wb=sb(nm + "Wb", [128, nco], F32),
                      Bb=sb(nm + "Bb", [128, nco], F32), wl=sb(nm + "wl", [128, 2], F32),
                      wlb=sb(nm + "wlb", [128, 2], F32), bias=sb(nm + "bias", [128, nco], F32))
    ROWS = sb("ROWS", [1, 2048], F32)
    STC = sb("STC", [128, 256], F32)
    SCR = sb("SCR", [128, 1024], F32)
    SCRB = sb("SCRB", [128, 1024], BF16)
    PS = [es.enter_context(nc.psum_tensor(f"PS{i}", [128, 512], F32)) for i in range(8)]

    scope_stack = []

    def sopen(*specs):
        cms = []
        ts = []
        for nm_, sh_, dt_ in specs:
            cm = nc.sbuf_tensor(nm_, sh_, dt_)
            ts.append(cm.__enter__())
            cms.append(cm)
        scope_stack.append(cms)
        return ts

    def sclose():
        cms = scope_stack.pop()
        for cm in reversed(cms):
            cm.__exit__(None, None, None)

    def barrier():
        return [(sname, p.cnt[sname]) for sname in list(p.cnt)]

    dW = Dma(p, "sync", "dW")
    dX = Dma(p, "sync", "dX")
    dS = Dma(p, "scalar", "dS")
    dG = Dma(p, "gpsimd", "dG")
    for s in ("tE", "vE", "sE"):
        p.sem(s)

    def TT_(fn, waits=(), inc="tE"):
        return p.emit("tensor", fn, waits=waits, inc=inc)

    def V_(fn, waits=(), inc="vE"):
        return p.emit("vector", fn, waits=waits, inc=inc)

    def S_(fn, waits=(), inc="sE"):
        return p.emit("scalar", fn, waits=waits, inc=inc)

    bank_free = [T0] * 8
    prep_tasks = []

    def pump(k=1):
        for _ in range(min(k, len(prep_tasks))):
            prep_tasks.pop(0)()

    def pump_all():
        while prep_tasks:
            prep_tasks.pop(0)()

    def mm_group(bi, mms, waits=(), out=None):
        w = list(waits) + [bank_free[bi]]
        n = len(mms)
        tok = None
        for i, mm in enumerate(mms):
            lh, rh = mm[0], mm[1]
            oa = mm[2] if len(mm) > 2 else PS[bi][:]
            tok = p.emit("tensor",
                         lambda e, oa=oa, lh=lh, rh=rh, st=(i == 0), sp=(i == n - 1):
                         e.matmul(oa, lh, rh, start=st, stop=sp),
                         waits=(w if i == 0 else ()), inc=("tE" if i == n - 1 else None))
        return tok

    def consume(bi, tok):
        bank_free[bi] = tok

    x_tok = dX.load(bass.AP(X, 0, [[NC * T, 128], [T, NC], [1, T]]),
                    bass.AP(x_h.tensor, 0, [[T, 128], [128 * T, NC], [1, T]]))
    ctok = None
    for n_ in C:
        ctok = dW.load(C[n_][:], cst_h[n_])
    for n_ in SM:
        ctok = dW.load(SM[n_][:], smh[n_])
    for nm, I_, O in GEN_MATS:
        for k2 in ("sb", "Wb", "Bb", "wl", "wlb"):
            ctok = dW.load(GB[nm][k2][:], genh[nm][k2])
    CW = ctok

    gbias = {}
    for nm, I_, O in GEN_MATS:
        b = GB[nm]
        nco = O // 128
        t1 = S_(lambda e, b=b, nco=nco: e.activation(
            out=b["bias"][:, 0:nco], in_=b["sb"][:, nco:2 * nco], func=AF.Copy,
            scale=b["wlb"][:, 1:2]), waits=[CW])
        t2 = V_(lambda e, b=b, nco=nco: e.scalar_tensor_tensor(
            b["bias"][:, 0:nco], b["sb"][:, 0:nco], b["wlb"][:, 0:1],
            b["bias"][:, 0:nco], OP.mult, OP.add), waits=[t1, CW])
        t3 = S_(lambda e, b=b, nco=nco: e.activation(
            out=b["bias"][:, 0:nco], in_=b["bias"][:, 0:nco], func=AF.Tanh), waits=[t2])
        t4 = V_(lambda e, b=b, nco=nco: e.scalar_tensor_tensor(
            b["bias"][:, 0:nco], b["bias"][:, 0:nco], 1.0, b["Wb"][:, 0:nco],
            OP.mult, OP.mult), waits=[t3])
        t5 = V_(lambda e, b=b, nco=nco: e.tensor_add(
            b["bias"][:, 0:nco], b["bias"][:, 0:nco], b["Bb"][:, 0:nco]), waits=[t4])
        gbias[nm] = t5
    gbias["ca"] = V_(lambda e: e.tensor_scalar_mul(
        GB["ca"]["bias"][:, 0:8], GB["ca"]["bias"][:, 0:8], 0.125), waits=[gbias["ca"]])
    FC2BQ = sb("FC2BQ", [128, NC], F32)
    fc2bq_tok = V_(lambda e: e.tensor_scalar_mul(FC2BQ[:], SM["fc2bcol"][:], 0.25),
                   waits=[CW])

    rows_free = [T0] * 4

    def col_to_bcast(col_of, n_cols, bi_t, bi_bc, rslot, waits=(), out_bf16=None):
        w = list(waits) + [bank_free[bi_t]]
        tok = None
        for j in range(n_cols):
            tok = p.emit("tensor",
                         lambda e, j=j: e.transpose(
                             PS[bi_t][0:1, 128 * j:128 * (j + 1)], col_of(j), C["ident_f"][:]),
                         waits=(w if j == 0 else ()), inc=("tE" if j == n_cols - 1 else None))
        tokc = V_(lambda e: e.tensor_copy(
            ROWS[0:1, 512 * rslot:512 * rslot + 128 * n_cols],
            PS[bi_t][0:1, 0:128 * n_cols]), waits=[tok, rows_free[rslot]])
        consume(bi_t, tokc)
        tokb = mm_group(bi_bc, [(C["r064f"][0:1, 0:128],
                                 ROWS[0:1, 512 * rslot:512 * rslot + 128 * n_cols],
                                 PS[bi_bc][0:128, 0:128 * n_cols])], waits=[tokc])
        rows_free[rslot] = tokb
        if out_bf16 is not None:
            toko = V_(lambda e: e.tensor_copy(out_bf16, PS[bi_bc][0:128, 0:128 * n_cols]),
                      waits=[tokb])
            consume(bi_bc, toko)
            return tokb, toko
        return tokb, None

    SUMC, SQC, VARC, R1C, TMPC, R2C, RHC, MR1C, RRC = (0, 16, 32, 48, 64, 80, 96, 112, 128)

    def stats(buf, buf_tok, eps2=None, rms_eps=None):
        last_st = None
        for g in range(NTG):
            pump(1)
            casts = []
            for c in range(NC):
                casts.append(V_(lambda e, c=c, g=g: e.tensor_copy(
                    XN[:, T * c + TG * g:T * c + TG * g + TG],
                    buf[:, T * c + TG * g:T * c + TG * g + TG]),
                    waits=[buf_tok]))
            tok_s = mm_group(6, [(C["invdb"][:, 0:1],
                                  XN[:, T * c + TG * g:T * c + TG * g + TG],
                                  PS[6][0:1, 0:TG]) for c in range(NC)],
                             waits=[casts[-1]])
            tok_q = None
            for wave in range(4):
                sq_tok = None
                for cc in range(2):
                    c = 2 * wave + cc
                    sq_tok = S_(lambda e, c=c, cc=cc, g=g: e.activation(
                        out=SCRB[:, 512 * cc:512 * cc + 512],
                        in_=XN[:, T * c + TG * g:T * c + TG * g + TG], func=AF.Square),
                        waits=[casts[c], tok_q or state.get("scr_free", T0)]
                              + ([last_st] if last_st else []))
                w0 = [sq_tok] + ([bank_free[7]] if wave == 0 else [])
                for cc in range(2):
                    tok_q = p.emit("tensor",
                                   lambda e, cc=cc, st=(wave == 0 and cc == 0),
                                   sp=(wave == 3 and cc == 1): e.matmul(
                                       PS[7][0:1, 0:TG], C["invdb"][:, 0:1],
                                       SCRB[:, 512 * cc:512 * cc + 512],
                                       start=st, stop=sp),
                                   waits=(w0 if cc == 0 else ()),
                                   inc=("tE" if (cc == 1) else None))
            state["scr_free"] = tok_q
            cm1 = V_(lambda e: e.tensor_copy(ROWS[0:1, 0:TG], PS[6][0:1, 0:TG]),
                     waits=[tok_s, rows_free[0]])
            consume(6, cm1)
            cm2 = V_(lambda e: e.tensor_copy(ROWS[0:1, 512:512 + TG], PS[7][0:1, 0:TG]),
                     waits=[tok_q, rows_free[1]])
            consume(7, cm2)
            w1 = [bank_free[5], cm2]
            tokt = None
            for j in range(4):
                tokt = p.emit("tensor", lambda e, j=j: e.transpose(
                    PS[5][0:128, j:j + 1], ROWS[0:1, 128 * j:128 * (j + 1)],
                    C["ident_f"][0:1, 0:1]),
                    waits=(w1 if j == 0 else ()), inc=None)
            for j in range(4):
                tokt = p.emit("tensor", lambda e, j=j: e.transpose(
                    PS[5][0:128, 4 + j:5 + j], ROWS[0:1, 512 + 128 * j:512 + 128 * (j + 1)],
                    C["ident_f"][0:1, 0:1]), waits=(), inc=("tE" if j == 3 else None))
            st1 = V_(lambda e, g=g: e.tensor_copy(STC[:, SUMC + 4 * g:SUMC + 4 * g + 4],
                                                  PS[5][0:128, 0:4]), waits=[tokt])
            last_st = V_(lambda e, g=g: e.tensor_copy(STC[:, SQC + 4 * g:SQC + 4 * g + 4],
                                                      PS[5][0:128, 4:8]), waits=[tokt])
            consume(5, last_st)
            rows_free[0] = tokt
            rows_free[1] = tokt
        a = V_(lambda e: e.scalar_tensor_tensor(
            STC[:, TMPC:TMPC + 16], STC[:, SUMC:SUMC + 16], -1.0, STC[:, SUMC:SUMC + 16],
            OP.mult, OP.mult), waits=[last_st])
        a = V_(lambda e: e.tensor_add(STC[:, VARC:VARC + 16], STC[:, TMPC:TMPC + 16],
                                      STC[:, SQC:SQC + 16]), waits=[a])
        a = S_(lambda e: e.activation(out=STC[:, TMPC:TMPC + 16], in_=STC[:, VARC:VARC + 16],
                                      func=AF.Sqrt, bias=C["eps5"][:, 0:1]), waits=[a])
        a = V_(lambda e: e.reciprocal(STC[:, R1C:R1C + 16], STC[:, TMPC:TMPC + 16]), waits=[a])
        a = V_(lambda e: e.tensor_mul(STC[:, MR1C:MR1C + 16], STC[:, SUMC:SUMC + 16],
                                      STC[:, R1C:R1C + 16]), waits=[a])
        if eps2 is not None or rms_eps is not None:
            a = V_(lambda e: e.tensor_mul(STC[:, TMPC:TMPC + 16], STC[:, R1C:R1C + 16],
                                          STC[:, R1C:R1C + 16]), waits=[a])
            a = V_(lambda e: e.tensor_mul(STC[:, TMPC:TMPC + 16], STC[:, TMPC:TMPC + 16],
                                          STC[:, VARC:VARC + 16]), waits=[a])
            ee = eps2 if eps2 is not None else rms_eps
            tgt = R2C if eps2 is not None else RRC
            eps_ap = C["eps5"] if ee == 1e-5 else C["eps6"]
            a = S_(lambda e, eps_ap=eps_ap: e.activation(out=STC[:, TMPC:TMPC + 16],
                                                 in_=STC[:, TMPC:TMPC + 16],
                                                 func=AF.Sqrt, bias=eps_ap[:, 0:1]), waits=[a])
            a = V_(lambda e, tgt=tgt: e.reciprocal(STC[:, tgt:tgt + 16],
                                                   STC[:, TMPC:TMPC + 16]), waits=[a])
            if eps2 is not None:
                a = V_(lambda e: e.tensor_mul(STC[:, RHC:RHC + 16], STC[:, R1C:R1C + 16],
                                              STC[:, R2C:R2C + 16]), waits=[a])
        return a

    def stc_col(slot, g):
        return lambda j: STC[:, slot + 4 * g + j:slot + 4 * g + j + 1]

    state = {"x": x_tok}

    def materialize_lnx(AUX, stat_tok, bar=()):
        last = None
        for g in range(NTG):
            tr, _ = col_to_bcast(stc_col(R1C, g), 4, 5, 4, 0, waits=[stat_tok])
            tm, _ = col_to_bcast(stc_col(MR1C, g), 4, 5, 3, 1, waits=[stat_tok])
            for c in range(NC):
                tt = V_(lambda e, c=c, g=g: e.scalar_tensor_tensor(
                    AUX[:, T * c + TG * g:T * c + TG * g + TG],
                    X[:, T * c + TG * g:T * c + TG * g + TG], 1.0,
                    PS[4][0:128, 0:TG], OP.mult, OP.mult),
                    waits=[tr, state["x"], stat_tok] + list(bar))
                tt = V_(lambda e, c=c, g=g: e.tensor_sub(
                    AUX[:, T * c + TG * g:T * c + TG * g + TG],
                    AUX[:, T * c + TG * g:T * c + TG * g + TG],
                    PS[3][0:128, 0:TG]), waits=[tm, tt])
                last = tt
            consume(4, last)
            consume(3, last)
        return last

    def materialize_xn(SRC, src_tok, colslot, gamma, beta):
        last = None
        for g in range(NTG):
            tb, _ = col_to_bcast(stc_col(colslot, g), 4, 5, 4, 0, waits=[src_tok])
            for c in range(NC):
                tt = V_(lambda e, c=c, g=g: e.scalar_tensor_tensor(
                    XN[:, T * c + TG * g:T * c + TG * g + TG],
                    SRC[:, T * c + TG * g:T * c + TG * g + TG], 1.0,
                    PS[4][0:128, 0:TG], OP.mult, OP.mult), waits=[tb, src_tok])
                if gamma is not None:
                    if beta is not None:
                        tt = V_(lambda e, c=c, g=g: e.tensor_scalar(
                            XN[:, T * c + TG * g:T * c + TG * g + TG],
                            XN[:, T * c + TG * g:T * c + TG * g + TG],
                            gamma[:, c:c + 1], beta[:, c:c + 1], OP.mult, OP.add),
                            waits=[tt, CW])
                    else:
                        tt = V_(lambda e, c=c, g=g: e.tensor_scalar(
                            XN[:, T * c + TG * g:T * c + TG * g + TG],
                            XN[:, T * c + TG * g:T * c + TG * g + TG],
                            gamma[:, c:c + 1], None, OP.mult),
                            waits=[tt, CW])
                last = tt
            consume(4, last)
        return last

    dq = Dma(p, "gpsimd", "dq")
    dqs = Dma(p, "scalar", "dqs")
    dqsc = Dma(p, "sync", "dqsc", n_slots=2)

    def emit_dequant(DQ, DQO, SEL8, SCL, selw, qh, scales_h, scratch, n_k, n_n, blkmap, bar):
        dq_free = state.get("dq", T0)
        dqo_free = state.get("dqo", T0)
        scl_free = state.get("scl", T0)
        last_store = None
        for kc in range(n_k):
            tq = dq.load(DQ[:, 0:n_n],
                         bass.AP(qh.tensor, kc * 128 * n_n, [[n_n, 128], [1, n_n]]),
                         waits=[dq_free] + bar)
            selp, colb = blkmap(kc)
            td = None
            for j in range(n_n // 512):
                jj = j % 2
                tsc = dqsc.load(SCL[0:8, jj * 512:jj * 512 + 512],
                                bass.AP(scales_h.tensor, colb * n_n + j * 512,
                                        [[4096, 8], [1, 512]]),
                                waits=[state.get("sclb%d" % jj, T0)] + bar, slot=jj)
                tokb = mm_group(2, [(SEL8[0:8, selp * 128:selp * 128 + 128],
                                     SCL[0:8, jj * 512:jj * 512 + 512],
                                     PS[2][0:128, 0:512])], waits=[CW, tsc] + selw)
                state["sclb%d" % jj] = tokb
                td = V_(lambda e, j=j: e.scalar_tensor_tensor(
                    DQO[:, j * 512:(j + 1) * 512], DQ[:, j * 512:(j + 1) * 512],
                    -128.0, PS[2][0:128, 0:512], OP.add, OP.mult),
                    waits=[tokb, tq, dqo_free])
                consume(2, td)
            dq_free = td
            ts = dqs.load(bass.AP(scratch.tensor, kc * 128 * n_n, [[n_n, 128], [1, n_n]]),
                          DQO[:, 0:n_n], waits=[td])
            dqo_free = ts
            last_store = ts
        state["dq"] = dq_free
        state["dqo"] = dqo_free
        return last_store

    def emit_gen_tasks(nm, scratch, GS, GOUT, GSCR, bar, st_, scale_head=None):
        g = genh[nm]
        b = GB[nm]
        O = g["O"]
        I_ = g["I"]
        NRG = I_ // 1024
        genw = Dma(p, "sync", "g_" + nm, n_slots=2)
        genw2 = Dma(p, "scalar", "g2_" + nm, n_slots=2)
        gens_ = Dma(p, "scalar", "gs_" + nm)
        NSL = O // 128

        def task(jj):
            j, rg = jj // NRG, jj % NRG
            tks = []
            for i, key in enumerate(("s0", "s1", "W", "B")):
                dq_ = genw if i < 2 else genw2
                tks.append(dq_.load(
                    bass.AP(GS[i], 0, [[1024, 128], [128, NC], [1, 128]]),
                    bass.AP(g[key].tensor, rg * 1024 * O + j * 128,
                            [[O, 128], [128 * O, NC], [1, 128]]),
                    waits=list(st_["gfree"][i]), slot=i % 2))
            t1 = S_(lambda e: e.activation(out=GSCR[:, 0:1024], in_=GS[1][:, 0:1024],
                                           func=AF.Copy, scale=b["wl"][:, 1:2]),
                    waits=[tks[1], CW, st_["scr"]])
            t2 = V_(lambda e: e.scalar_tensor_tensor(
                GSCR[:, 0:1024], GS[0][:, 0:1024], b["wl"][:, 0:1], GSCR[:, 0:1024],
                OP.mult, OP.add), waits=[t1, tks[0]])
            t3 = S_(lambda e: e.activation(out=GSCR[:, 0:1024], in_=GSCR[:, 0:1024],
                                           func=AF.Tanh), waits=[t2])
            t4 = V_(lambda e: e.scalar_tensor_tensor(
                GSCR[:, 0:1024], GSCR[:, 0:1024], 1.0, GS[2][:, 0:1024],
                OP.mult, OP.mult), waits=[t3, tks[2]])
            gout_free = st_.get("gout", T0)
            if scale_head is not None and j < scale_head[0]:
                t5 = V_(lambda e: e.tensor_add(GSCR[:, 0:1024], GSCR[:, 0:1024],
                                               GS[3][:, 0:1024]), waits=[t4, tks[3]])
                t5 = V_(lambda e, f=scale_head[1]: e.tensor_scalar(
                    GOUT[:, 0:1024], GSCR[:, 0:1024], f, None, OP.mult),
                    waits=[t5, gout_free])
            else:
                t5 = V_(lambda e: e.tensor_add(
                    GOUT[:, 0:1024], GSCR[:, 0:1024], GS[3][:, 0:1024]),
                    waits=[t4, tks[3], gout_free])
            st_["scr"] = t5
            st_["gfree"] = [[t5]] * 4
            ts = gens_.load(bass.AP(scratch.tensor, rg * 1024 * O + j * 128,
                                    [[O, 128], [128 * O, NC], [1, 128]]),
                            bass.AP(GOUT, 0, [[1024, 128], [128, NC], [1, 128]]),
                            waits=[t5])
            st_["gout"] = ts
            state["gout_" + nm] = ts

        for jj in range(NSL * NRG):
            prep_tasks.append(lambda jj=jj: task(jj))

    def emit_cast_tasks(CASTB, bar):
        cst_dma = Dma(p, "gpsimd", "l1cast", n_slots=2)
        cst_st = Dma(p, "scalar", "l1casts", n_slots=2)
        st_ = {"free": [list(bar), list(bar)], "ctr": 0}

        def task(src, dst, off, nchunk, width):
            sl = st_["ctr"] % 2
            st_["ctr"] += 1
            ld = cst_dma.load(
                bass.AP(CASTB[sl], 0, [[1024, 128], [width, nchunk], [1, width]]),
                bass.AP(src, off, [[width, 128], [128 * width, nchunk], [1, width]]),
                waits=list(st_["free"][sl]), slot=sl)
            ts = cst_st.load(
                bass.AP(dst, off, [[width, 128], [128 * width, nchunk], [1, width]]),
                bass.AP(CASTB[sl], 0, [[1024, 128], [width, nchunk], [1, width]]),
                waits=[ld], slot=sl)
            st_["free"][sl] = [ts]
            state["l1cast"] = ts

        for e_ in range(8):
            for hf in range(4):
                prep_tasks.append(lambda e_=e_, hf=hf: task(
                    h_wg.tensor, s_wg.tensor, e_ * D * 512 + hf * 256 * 512, 2, 512))
                prep_tasks.append(lambda e_=e_, hf=hf: task(
                    h_wu.tensor, s_wu.tensor, e_ * D * 512 + hf * 256 * 512, 2, 512))
            for hf in range(4):
                prep_tasks.append(lambda e_=e_, hf=hf: task(
                    h_wd.tensor, s_wd.tensor, e_ * 512 * D + hf * 128 * D, 1, 1024))
        for src, dst in ((h_wsg.tensor, s_wsg.tensor), (h_wsu.tensor, s_wsu.tensor),
                         (h_wsd.tensor, s_wsd.tensor)):
            for q in range(8):
                prep_tasks.append(lambda src=src, dst=dst, q=q:
                                  task(src, dst, q * 128 * D, 1, 1024))

    def ffn_stream(pref, WS, WT, xn_tok, w1_scr, w1_tok, w2_scr, w2_tok,
                   b1_cols, b1_tok, act_fn, b2_row, b2_tok, out_scale, Y,
                   y_free0=T0, bar=(), b2_cols=None):
        w1d = Dma(p, "sync", pref + "w1", n_slots=2)
        w2d = Dma(p, "scalar", pref + "w2", n_slots=2)
        ws_free = [list(bar), list(bar)]
        wt_free = [list(bar), list(bar)]
        y_free = y_free0
        lastx = state["x"]
        for g in range(NTG):
            y_toks = []
            for io in range(16):
                pump(1)
                slot = io % 2
                ldt = w1d.load(
                    bass.AP(WS[slot], 0, [[2048, 128], [256, NC], [1, 256]]),
                    bass.AP(w1_scr.tensor, io * 256, [[4096, 128], [128 * 4096, NC], [1, 256]]),
                    waits=list(ws_free[slot]) + [w1_tok], slot=slot)
                tok = None
                for isub in range(2):
                    bi = isub % 2
                    mms = [(WS[slot][:, kc * 256 + isub * 128:kc * 256 + isub * 128 + 128],
                            XN[:, T * kc + TG * g:T * kc + TG * g + TG]) for kc in range(NC)]
                    tok = mm_group(bi, mms, waits=[ldt, xn_tok])
                    ic = io * 2 + isub
                    ytk = S_(lambda e, bi=bi, ic=ic: e.activation(
                        out=Y[:, ic * TG:(ic + 1) * TG], in_=PS[bi][0:128, 0:TG],
                        func=act_fn, bias=b1_cols[:, ic:ic + 1]),
                        waits=[tok, b1_tok, y_free])
                    consume(bi, ytk)
                    y_toks.append(ytk)
                ws_free[slot] = [tok]
            y_tok = y_toks[-1]
            first = [True] * 8
            slot_tok = {}
            for ks in range(8):
                pump(1)
                slot = ks % 2
                ldt = w2d.load(
                    bass.AP(WT[slot], 0, [[4096, 128], [1024, 4], [1, 1024]]),
                    bass.AP(w2_scr.tensor, ks * 512 * D, [[D, 128], [128 * D, 4], [1, 1024]]),
                    waits=list(wt_free[slot]) + [w2_tok], slot=slot)
                for nc_ in range(8):
                    w0 = [ldt, y_tok]
                    if first[nc_]:
                        w0 = w0 + [bank_free[nc_]]
                    for ksub in range(4):
                        ic = ks * 4 + ksub
                        is_last_of_ks = (nc_ == 7 and ksub == 3)
                        lastmm = (b2_cols is not None and ks == 7 and ksub == 3)
                        tokm = p.emit(
                            "tensor",
                            lambda e, nc_=nc_, slot=slot, ksub=ksub, ic=ic,
                            st=(first[nc_] and ksub == 0), sp=lastmm:
                            e.matmul(PS[nc_][0:128, 0:TG],
                                     WT[slot][:, ksub * 1024 + nc_ * 128:ksub * 1024 + nc_ * 128 + 128],
                                     Y[:, ic * TG:(ic + 1) * TG], start=st, stop=sp),
                            waits=(w0 if ksub == 0 else ()),
                            inc=("tE" if (is_last_of_ks or lastmm) else None))
                        if is_last_of_ks:
                            slot_tok[ks] = tokm
                        if lastmm:
                            slot_tok[(7, nc_)] = tokm
                    first[nc_] = False
                wt_free[slot] = [slot_tok[ks]]
            lasttok = None
            for nc_ in range(8):
                if b2_cols is None:
                    tok = p.emit("tensor",
                                 lambda e, nc_=nc_: e.matmul(
                                     PS[nc_][0:128, 0:TG],
                                     b2_row[0:1, nc_ * 128:nc_ * 128 + 128],
                                     C["onesrow"][0:1, 0:TG], start=False, stop=True),
                                 waits=([b2_tok] if nc_ == 0 else ()), inc="tE")
                else:
                    tok = slot_tok[(7, nc_)]
                xt = V_(lambda e, nc_=nc_, g=g: e.scalar_tensor_tensor(
                    X[:, T * nc_ + TG * g:T * nc_ + TG * g + TG],
                    PS[nc_][0:128, 0:TG], out_scale,
                    X[:, T * nc_ + TG * g:T * nc_ + TG * g + TG], OP.mult, OP.add),
                    waits=[tok, lastx])
                if b2_cols is not None:
                    xt = V_(lambda e, nc_=nc_, g=g: e.tensor_scalar(
                        X[:, T * nc_ + TG * g:T * nc_ + TG * g + TG],
                        X[:, T * nc_ + TG * g:T * nc_ + TG * g + TG],
                        b2_cols[1][:, nc_:nc_ + 1], None, OP.add),
                        waits=[xt, b2_cols[0]])
                consume(nc_, xt)
                lastx = xt
                lasttok = tok
            y_free = lasttok
        state["x"] = lastx
        return lastx

    def emit_fc_dequants(bar):
        DQ, DQO, SEL8, SCL = sopen(
            ("dqDQ", [128, 4096], F32), ("dqO", [128, 4096], BF16),
            ("dqSEL", [8, 8 * 128], F32), ("dqSCL", [8, 1024], F32))
        dsel = Dma(p, "sync", "dsel_fc")
        selt = dsel.load(SEL8[:], h_sel8)
        t1 = emit_dequant(DQ, DQO, SEL8, SCL, [selt], h_fc1q, h_fc1s, s_fc1, 8, 4096,
                          lambda kc: (kc, 0), bar)
        t2 = emit_dequant(DQ, DQO, SEL8, SCL, [selt], h_fc2q, h_fc2s, s_fc2, 32, 1024,
                          lambda kc: (kc % 8, kc // 8), bar)
        sclose()
        return t1, t2

    def sub_l0ffn(pref, fc1_tok, fc2_tok):
        bar = barrier()
        st_tok = stats(X, state["x"], eps2=1e-5)
        AUX, = sopen((pref + "AUX", [128, NC * T], F32))
        lnx_tok = materialize_lnx(AUX, st_tok, bar=bar)
        xn_tok = materialize_xn(AUX, lnx_tok, R2C, SM["l0g"], SM["l0b"])
        lastx = state["x"]
        for c in range(NC):
            lastx = V_(lambda e, c=c: e.scalar_tensor_tensor(
                X[:, T * c:T * c + T], AUX[:, T * c:T * c + T], 0.5,
                X[:, T * c:T * c + T], OP.mult, OP.add), waits=[lnx_tok, lastx])
        state["x"] = lastx
        sclose()
        bar2 = barrier()
        Y, WSa, WSb, WTa, WTb = sopen(
            (pref + "Y", [128, 32 * TG], BF16),
            (pref + "WSa", [128, 2048], BF16), (pref + "WSb", [128, 2048], BF16),
            (pref + "WTa", [128, 4096], BF16), (pref + "WTb", [128, 4096], BF16))
        ffn_stream(pref, [WSa, WSb], [WTa, WTb], xn_tok, s_fc1, fc1_tok, s_fc2,
                   fc2_tok, SM["fc1b"], CW, AF.Silu, None, T0,
                   0.25, Y, y_free0=state["x"], bar=bar2,
                   b2_cols=(fc2bq_tok, FC2BQ))
        sclose()

    if n_sub >= 1:
        t1, t2 = emit_fc_dequants([])
    if n_sub >= 2:
        GSg = sopen(("gGS0", [128, 1024], F32), ("gGS1", [128, 1024], F32),
                    ("gGS2", [128, 1024], F32), ("gGS3", [128, 1024], F32),
                    ("gOUT", [128, 1024], BF16), ("gSCR", [128, 1024], F32),
                    ("gCB0", [128, 1024], BF16), ("gCB1", [128, 1024], BF16))
        gbar = barrier()
        gen_st = {"gfree": [list(gbar)] * 4, "scr": T0}
        emit_gen_tasks("ca", s_qkv, GSg[0:4], GSg[4], GSg[5], gbar, gen_st,
                       scale_head=(8, 0.125))
        emit_gen_tasks("cp", s_cp, GSg[0:4], GSg[4], GSg[5], gbar, gen_st)
        emit_gen_tasks("cf", s_cf, GSg[0:4], GSg[4], GSg[5], gbar, gen_st)
        emit_gen_tasks("cm", s_cm, GSg[0:4], GSg[4], GSg[5], gbar, gen_st)
        emit_cast_tasks([GSg[6], GSg[7]], gbar)
    if n_sub >= 1:
        sub_l0ffn("f1", t1, t2)


    def materialize_ln_full(SRC, src_tok, gamma, beta, stat_tok):
        last = None
        for g in range(NTG):
            tr, _ = col_to_bcast(stc_col(R1C, g), 4, 5, 4, 0, waits=[stat_tok])
            tm, _ = col_to_bcast(stc_col(MR1C, g), 4, 5, 3, 1, waits=[stat_tok])
            for c in range(NC):
                sl = SCR[:, 512 * (c % 2):512 * (c % 2) + 512]
                t1 = V_(lambda e, c=c, g=g, sl=sl: e.scalar_tensor_tensor(
                    sl, SRC[:, T * c + TG * g:T * c + TG * g + TG], 1.0,
                    PS[4][0:128, 0:TG], OP.mult, OP.mult),
                    waits=[tr, src_tok, state.get("scr_free", T0)]
                          + ([last] if last else []))
                t1 = V_(lambda e, c=c, sl=sl: e.tensor_sub(sl, sl, PS[3][0:128, 0:TG]),
                        waits=[tm, t1])
                t1 = V_(lambda e, c=c, g=g, sl=sl: e.tensor_scalar(
                    XN[:, T * c + TG * g:T * c + TG * g + TG], sl,
                    gamma[:, c:c + 1], beta[:, c:c + 1], OP.mult, OP.add),
                    waits=[t1, CW])
                last = t1
            consume(4, last)
            consume(3, last)
        state["scr_free"] = last
        return last

    def sub_l0attn():
        CH = HEADS // 2
        dLx = Dma(p, "scalar", "dLx")
        bar = barrier()
        st_tok = stats(X, state["x"], eps2=1e-5)
        AUX, = sopen(("a2AUX", [128, NC * T], F32))
        lnx_tok = materialize_lnx(AUX, st_tok, bar=bar)
        h1_tok = materialize_xn(AUX, lnx_tok, R2C, SM["ln1g"], SM["ln1b"])
        slnx = dLx.load(bass.AP(s_lnx.tensor, 0, [[T, 128], [128 * T, NC], [1, T]]),
                        bass.AP(AUX, 0, [[NC * T, 128], [T, NC], [1, T]]),
                        waits=[lnx_tok])
        sclose()
        bar2 = barrier()
        QF, KF, VST0, VST1, VTMh0, VTMh1, PT, QW0, QW1, EPI, NEGBC = sopen(
            ("aQF", [128, NC * T], BF16), ("aKF", [128, NC * T], BF16),
            ("aVST0", [128, 8 * 65], BF16), ("aVST1", [128, 8 * 65], BF16),
            ("aVTMh0", [128, 16 * 65], BF16), ("aVTMh1", [128, 16 * 65], BF16),
            ("aPT", [128, 2 * 512], BF16),
            ("aQW0", [128, 8 * 256], BF16), ("aQW1", [128, 8 * 256], BF16),
            ("aEPI", [128, 512], F32), ("aNEGBC", [128, 4 * 512], BF16))
        QW = [QW0, QW1]
        VTMh = [VTMh0, VTMh1]
        VSTs = [VST0, VST1]
        ones_tok = [V_(lambda e, i=i: e.memset(
            bass.AP(VSTs[i], 64, [[8 * 65, 128], [65, 8], [1, 1]]), 1.0),
            waits=bar2) for i in range(2)]
        vbrow_tok = None
        for grp in range(2):
            wvb = [bank_free[5], gbias["ca"]] + bar2
            tokt = None
            for j in range(4):
                jj = 4 * grp + j
                tokt = p.emit("tensor", lambda e, j=j, jj=jj: e.transpose(
                    PS[5][0:1, 128 * j:128 * (j + 1)], GB["ca"]["bias"][:, 16 + jj:17 + jj],
                    C["ident_f"][:]), waits=(wvb if j == 0 else ()),
                    inc=("tE" if j == 3 else None))
            vbrow_tok = V_(lambda e, grp=grp: e.tensor_copy(
                ROWS[0:1, 1024 + 512 * grp:1536 + 512 * grp], PS[5][0:1, 0:512]),
                waits=[tokt, rows_free[2 + grp]])
            consume(5, vbrow_tok)
            rows_free[2 + grp] = vbrow_tok
        qkvw = Dma(p, "sync", "qkvw", n_slots=2)
        qkv_done = None
        vstore = Dma(p, "scalar", "vst", n_slots=2)
        vst_free = [[ones_tok[0]], [ones_tok[1]]]
        for nsl in range(12):
            slot = nsl % 2
            ldt = qkvw.load(
                bass.AP(QW[slot], 0, [[8 * 256, 128], [256, NC], [1, 256]]),
                bass.AP(s_qkv.tensor, nsl * 256, [[3072, 128], [128 * 3072, NC], [1, 256]]),
                waits=list(state.get("qw%d" % slot, [T0])) + bar2, slot=slot)
            if nsl < 8:
                tok = None
                for nsub in range(2):
                    for g in range(NTG):
                        bi = (nsub * NTG + g) % 2
                        n_global = nsl * 256 + nsub * 128
                        mms = [(QW[slot][:, kc * 256 + nsub * 128:kc * 256 + nsub * 128 + 128],
                                XN[:, T * kc + TG * g:T * kc + TG * g + TG])
                               for kc in range(NC)]
                        tok = mm_group(bi, mms, waits=[ldt, h1_tok])
                        dst = QF if n_global < 1024 else KF
                        colg = n_global if n_global < 1024 else n_global - 1024
                        ch = colg // 128
                        etk = V_(lambda e, bi=bi, dst=dst, ch=ch, g=g,
                                 col=n_global // 128: e.tensor_scalar(
                            dst[:, ch * T + TG * g:ch * T + TG * g + TG],
                            PS[bi][0:128, 0:TG],
                            GB["ca"]["bias"][:, col:col + 1], None, OP.add),
                            waits=[tok, gbias["ca"]])
                        consume(bi, etk)
                qkv_done = tok
            else:
                vsl = nsl - 8
                for tt in range(16):
                    bi = tt % 2
                    w0 = [ldt, h1_tok, bank_free[bi]]
                    tokv = None
                    for kc in range(NC):
                        tokv = p.emit("tensor",
                                      lambda e, kc=kc, tt=tt, slot=slot, bi=bi,
                                      st=(kc == 0): e.matmul(
                                          PS[bi][0:128, 0:256],
                                          XN[:, T * kc + 128 * tt:T * kc + 128 * tt + 128],
                                          QW[slot][:, kc * 256:kc * 256 + 256],
                                          start=st, stop=False),
                                      waits=(w0 if kc == 0 else ()), inc=None)
                    tokv = p.emit("tensor",
                                  lambda e, vsl=vsl, bi=bi: e.matmul(
                                      PS[bi][0:128, 0:256], C["r064f"][0:1, 0:128],
                                      ROWS[0:1, 1024 + 256 * vsl:1024 + 256 * vsl + 256],
                                      start=False, stop=True),
                                  waits=[vbrow_tok], inc="tE")
                    vslot = tt % 2
                    evk = V_(lambda e, bi=bi, vslot=vslot: e.tensor_copy(
                        bass.AP(VSTs[vslot], 0, [[8 * 65, 128], [65, 4], [1, 64]]),
                        bass.AP(PS[bi], 0, [[512, 128], [64, 4], [1, 64]])),
                        waits=[tokv] + list(vst_free[vslot]))
                    consume(bi, evk)
                    tokst = vstore.load(
                        bass.AP(s_v.tensor, tt * 128 * 1040 + (4 * vsl) * 65,
                                [[1040, 128], [1, 4 * 65]]),
                        bass.AP(VSTs[vslot], 0, [[8 * 65, 128], [1, 4 * 65]]),
                        waits=[evk], slot=vslot)
                    vst_free[vslot] = [tokst]
                qkv_done = tokv
            state["qw%d" % slot] = [tok if nsl < 8 else tokv]
        vload = Dma(p, "sync", "vload", n_slots=2)
        vput = [T0, T0]
        negbc_free = T0
        o_first = [qkv_done]
        exp_toks = {}
        NEGH = [144, 160]
        p1_tasks = []

        def emit_p1_head(h):
            ch, b_ = h // 2, 64 * (h % 2)
            base = NEGH[h % 2]

            def tile_task(qt, kg):
                width = 512 if kg < qt // 4 else (qt % 4) * 128 + 128
                state["p1ctr"] = state.get("p1ctr", 0) + 1
                bi = (0, 1, 4, 5)[state["p1ctr"] % 4]
                mms = [(QF[b_:b_ + 64, ch * T + 128 * qt:ch * T + 128 * qt + 128],
                        KF[b_:b_ + 64, ch * T + 512 * kg:ch * T + 512 * kg + width],
                        PS[bi][0:128, 0:width])]
                if kg == qt // 4:
                    moff = (qt % 4) * 128
                    mms.append((C["itop"][b_:b_ + 64, :], C["m1a"][b_:b_ + 64, :],
                                PS[bi][0:128, moff:moff + 128]))
                    mms.append((C["ibot"][b_:b_ + 64, :], C["m1b"][b_:b_ + 64, :],
                                PS[bi][0:128, moff:moff + 128]))
                tok = mm_group(bi, mms, waits=[qkv_done])
                if kg == 0:
                    nm_tok = V_(lambda e, bi=bi, qt=qt, width=width: e.reduce_max(
                        STC[:, base + qt % 16:base + qt % 16 + 1],
                        PS[bi][0:128, 0:width], axis=AX.X, negate=True),
                        waits=[tok, state.get("negq_%d" % (h % 2), T0)])
                else:
                    t2 = V_(lambda e, bi=bi, width=width: e.reduce_max(
                        STC[:, 176:177], PS[bi][0:128, 0:width], axis=AX.X,
                        negate=True), waits=[tok, state.get("negq_tmp", T0)])
                    nm_tok = V_(lambda e, qt=qt: e.tensor_tensor(
                        STC[:, base + qt % 16:base + qt % 16 + 1],
                        STC[:, base + qt % 16:base + qt % 16 + 1],
                        STC[:, 176:177], OP.min), waits=[t2])
                    state["negq_tmp"] = nm_tok
                consume(bi, nm_tok)
                state["negq_%d" % (h % 2)] = nm_tok

            for qt in range(16):
                for kg in range(qt // 4 + 1):
                    p1_tasks.append(lambda qt=qt, kg=kg: tile_task(qt, kg))

        def p1_pump(k=1):
            for _ in range(min(k, len(p1_tasks))):
                p1_tasks.pop(0)()

        emit_p1_head(0)
        p1_pump(len(p1_tasks))
        for h in range(HEADS):
            ch, b_ = h // 2, 64 * (h % 2)
            vslot = h % 2
            vld = vload.load(
                bass.AP(VTMh[vslot], 0, [[16 * 65, 128], [65, 16], [1, 65]]),
                bass.AP(s_v.tensor, h * 65, [[1040, 128], [128 * 1040, 16], [1, 65]]),
                waits=[vput[vslot], vst_free[0][0], vst_free[1][0]], slot=vslot)
            p1_pump(len(p1_tasks))
            if h + 1 < HEADS:
                emit_p1_head(h + 1)
            p1_done = state["negq_%d" % (h % 2)]
            pv_tok = None
            negbc_toks = []
            for qg in range(4):
                base = NEGH[h % 2]
                tb, nbt = col_to_bcast(
                    lambda j, base=base, qg=qg: STC[:, base + 4 * qg + j:base + 4 * qg + j + 1],
                    4, 2, 3, 2, waits=[p1_done, negbc_free],
                    out_bf16=NEGBC[:, 512 * qg:512 * qg + 512])
                negbc_toks.append(nbt)
            for qg in range(4):
                negbc_tok = negbc_toks[qg]
                nkc = 4 * qg + 4
                pvb = 6 + (qg % 2)
                for kc in range(nkc):
                    p1_pump(1)
                    off = 128 * (kc - 4 * qg) if kc >= 4 * qg else 0
                    bi = 2 + kc % 2
                    mms = [(KF[b_:b_ + 64, ch * T + 128 * kc:ch * T + 128 * kc + 128],
                            QF[b_:b_ + 64, ch * T + 512 * qg + off:ch * T + 512 * qg + 512],
                            PS[bi][0:128, off:512]),
                           (C["r064b"][b_:b_ + 64, 0:128],
                            NEGBC[b_:b_ + 64, 512 * qg + off:512 * qg + 512],
                            PS[bi][0:128, off:512])]
                    if kc >= 4 * qg:
                        mms.append((C["itop"][b_:b_ + 64, :], C["m2a"][b_:b_ + 64, :],
                                    PS[bi][0:128, off:off + 128]))
                        mms.append((C["ibot"][b_:b_ + 64, :], C["m2b"][b_:b_ + 64, :],
                                    PS[bi][0:128, off:off + 128]))
                    tok = mm_group(bi, mms, waits=[negbc_tok, qkv_done])
                    pslot = kc % 2
                    etk = S_(lambda e, bi=bi, off=off, pslot=pslot: e.activation(
                        out=PT[0:128, 512 * pslot + off:512 * pslot + 512],
                        in_=PS[bi][0:128, off:512], func=AF.Exp),
                        waits=[tok, exp_toks.get(pslot, T0)])
                    consume(bi, etk)
                    pv_w = [etk, vld]
                    if kc == 0:
                        pv_w.append(bank_free[pvb])
                    pv_tok = p.emit("tensor",
                                    lambda e, kc=kc, off=off, pslot=pslot, pvb=pvb,
                                    vslot=vslot,
                                    st=(kc == 0), sp=(kc == nkc - 1): e.matmul(
                                        PS[pvb][0:65, off:512],
                                        VTMh[vslot][:, kc * 65:kc * 65 + 65],
                                        PT[0:128, 512 * pslot + off:512 * pslot + 512],
                                        start=st, stop=sp),
                                    waits=pv_w, inc="tE")
                    exp_toks[pslot] = pv_tok
                rc = V_(lambda e, pvb=pvb: e.reciprocal(EPI[64:65, 0:512],
                                                        PS[pvb][64:65, 0:512]),
                        waits=[pv_tok, state.get("epi_free", T0)])
                tb2 = mm_group(2, [(C["r064f"][64:65, 0:128], EPI[64:65, 0:512],
                                    PS[2][0:128, 0:512])], waits=[rc])
                cpy = V_(lambda e: e.tensor_copy(EPI[:, 0:512], PS[2][0:128, 0:512]),
                         waits=[tb2])
                consume(2, cpy)
                ot = V_(lambda e, b_=b_, ch=ch, qg=qg, pvb=pvb: e.tensor_mul(
                    XN[b_:b_ + 64, ch * T + 512 * qg:ch * T + 512 * qg + 512],
                    PS[pvb][0:64, 0:512], EPI[0:64, 0:512]),
                    waits=[cpy] + o_first)
                o_first = []
                consume(pvb, ot)
                state["epi_free"] = ot
            negbc_free = tok
            vput[vslot] = pv_tok
        o_tok = ("vE", p.cnt["vE"])
        sclose()
        bar3 = barrier()
        AUX, CP = sopen(("a3AUX", [128, NC * T], F32), ("aCP", [128, NC * 1024], BF16))
        rl = dLx.load(bass.AP(AUX, 0, [[NC * T, 128], [T, NC], [1, T]]),
                      bass.AP(s_lnx.tensor, 0, [[T, 128], [128 * T, NC], [1, T]]),
                      waits=[slnx] + bar3)
        cpw = Dma(p, "sync", "cpw")
        cpl = cpw.load(bass.AP(CP, 0, [[NC * 1024, 128], [1024, NC], [1, 1024]]),
                       bass.AP(s_cp.tensor, 0, [[1024, 128], [128 * 1024, NC], [1, 1024]]),
                       waits=[state.get("gout_cp", T0)] + bar3)
        lastx = state["x"]
        hm_tok = None
        for nc_ in range(NC):
            for g in range(NTG):
                bi = (nc_ * NTG + g) % 2
                mms = [(CP[:, kc * 1024 + nc_ * 128:kc * 1024 + nc_ * 128 + 128],
                        XN[:, T * kc + TG * g:T * kc + TG * g + TG])
                       for kc in range(NC)]
                tok = mm_group(bi, mms, waits=[cpl, o_tok])
                hm = V_(lambda e, bi=bi, nc_=nc_, g=g: e.scalar_tensor_tensor(
                    AUX[:, T * nc_ + TG * g:T * nc_ + TG * g + TG],
                    PS[bi][0:128, 0:TG], 1.0,
                    AUX[:, T * nc_ + TG * g:T * nc_ + TG * g + TG], OP.mult, OP.add),
                    waits=[tok, rl])
                consume(bi, hm)
                hm_tok = hm
            hm_tok = V_(lambda e, nc_=nc_: e.tensor_scalar(
                AUX[:, T * nc_:T * nc_ + T], AUX[:, T * nc_:T * nc_ + T],
                GB["cp"]["bias"][:, nc_:nc_ + 1], None, OP.add),
                waits=[hm_tok, gbias["cp"]])
            lastx = V_(lambda e, nc_=nc_: e.tensor_add(
                X[:, T * nc_:T * nc_ + T], X[:, T * nc_:T * nc_ + T],
                AUX[:, T * nc_:T * nc_ + T]), waits=[hm_tok, lastx])
        state["x"] = lastx
        st2 = stats(AUX, hm_tok)
        xn2_tok = materialize_ln_full(AUX, hm_tok, SM["ln2g"], SM["ln2b"], st2)
        sclose()
        bar4 = barrier()
        Y, WSa, WSb, WTa, WTb = sopen(
            ("a4Y", [128, 32 * TG], BF16),
            ("a4WSa", [128, 2048], BF16), ("a4WSb", [128, 2048], BF16),
            ("a4WTa", [128, 4096], BF16), ("a4WTb", [128, 4096], BF16))
        ffn_stream("a4", [WSa, WSb], [WTa, WTb], xn2_tok, s_cf,
                   state.get("gout_cf", T0), s_cm, state.get("gout_cm", T0),
                   GB["cf"]["bias"], gbias["cf"], AF.Gelu_apprx_tanh,
                   None, T0, 1.0, Y, y_free0=xn2_tok, bar=bar4,
                   b2_cols=(gbias["cm"], GB["cm"]["bias"]))
        sclose()

    if n_sub >= 2:
        pump_all()
        sclose()
        sub_l0attn()

    h_sel8b = din("sel8b", [8, 8 * 128], BF16)
    GM1, GA1, GM2, GA2, GW1, GW2, GTM = 144, 160, 176, 192, 208, 224, 240

    def sub_l1ffn(pref):
        bar = barrier()
        st_tok = stats(X, state["x"])
        CFM, SEL8B = sopen((pref + "CFM", [8, 2048], BF16),
                           (pref + "SEL8B", [8, 8 * 128], BF16))
        dsel = Dma(p, "sync", pref + "dsel")
        selt = dsel.load(SEL8B[:], h_sel8b, waits=bar)
        AUX, SCO, IT8, FT8, FT8b = sopen(
            (pref + "AUX", [128, NC * T], F32), (pref + "SCO", [128, 128], F32),
            (pref + "IT8", [128, 8], I32), (pref + "FT8", [128, 8], F32),
            (pref + "FT8b", [128, 8], F32))
        lnx_tok = materialize_lnx(AUX, st_tok, bar=bar)
        xn_tok = None
        for c in range(NC):
            xn_tok = V_(lambda e, c=c: e.tensor_copy(
                XN[:, T * c:T * c + T], AUX[:, T * c:T * c + T]), waits=[lnx_tok])
        glast = None
        for tt in range(16):
            mms = [(AUX[:, T * kc + 128 * tt:T * kc + 128 * tt + 128],
                    SM["gw"][:, kc * 8:kc * 8 + 8],
                    PS[4][0:128, 8 * tt:8 * tt + 8]) for kc in range(NC)]
            glast = mm_group(4, mms, waits=[lnx_tok, CW])
        sg_tok = S_(lambda e: e.activation(out=SCO[:], in_=PS[4][0:128, 0:128],
                                           func=AF.Sigmoid), waits=[glast])
        consume(4, sg_tok)
        a = sg_tok
        for tt in range(16):
            sl = SCO[:, 8 * tt:8 * tt + 8]
            a = V_(lambda e, sl=sl, tt=tt: e.reduce_max(
                STC[:, GM1 + tt:GM1 + tt + 1], sl, axis=AX.X), waits=[a])
            a = V_(lambda e, sl=sl, tt=tt: e.tensor_scalar(
                IT8[:], sl, STC[:, GM1 + tt:GM1 + tt + 1], None, OP.is_equal),
                waits=[a])
            a = V_(lambda e: e.select(FT8[:], IT8[:], C["iota8"][:], C["c999"][:],
                                      add_drain=True), waits=[a])
            a = V_(lambda e, tt=tt: e.tensor_reduce(
                STC[:, GA1 + tt:GA1 + tt + 1], FT8[:], AX.X, OP.min), waits=[a])
            a = V_(lambda e, tt=tt: e.tensor_scalar(
                IT8[:], C["iota8"][:], STC[:, GA1 + tt:GA1 + tt + 1], None,
                OP.is_equal), waits=[a])
            a = V_(lambda e, sl=sl: e.select(FT8[:], IT8[:], C["cneg"][:], sl,
                                             add_drain=True), waits=[a])
            a = V_(lambda e, tt=tt: e.reduce_max(
                STC[:, GM2 + tt:GM2 + tt + 1], FT8[:], axis=AX.X), waits=[a])
            a = V_(lambda e, tt=tt: e.tensor_scalar(
                IT8[:], FT8[:], STC[:, GM2 + tt:GM2 + tt + 1], None, OP.is_equal),
                waits=[a])
            a = V_(lambda e: e.select(FT8[:], IT8[:], C["iota8"][:], C["c999"][:],
                                      add_drain=True), waits=[a])
            a = V_(lambda e, tt=tt: e.tensor_reduce(
                STC[:, GA2 + tt:GA2 + tt + 1], FT8[:], AX.X, OP.min), waits=[a])
        a = V_(lambda e: e.tensor_add(STC[:, GTM:GTM + 16], STC[:, GM1:GM1 + 16],
                                      STC[:, GM2:GM2 + 16]), waits=[a])
        a = V_(lambda e: e.tensor_scalar(STC[:, GTM:GTM + 16], STC[:, GTM:GTM + 16],
                                         1e-20, None, OP.add), waits=[a])
        a = V_(lambda e: e.reciprocal(STC[:, GTM:GTM + 16], STC[:, GTM:GTM + 16]),
               waits=[a])
        a = V_(lambda e: e.scalar_tensor_tensor(
            STC[:, GW1:GW1 + 16], STC[:, GM1:GM1 + 16], 2.5, STC[:, GTM:GTM + 16],
            OP.mult, OP.mult), waits=[a])
        a = V_(lambda e: e.scalar_tensor_tensor(
            STC[:, GW2:GW2 + 16], STC[:, GM2:GM2 + 16], 2.5, STC[:, GTM:GTM + 16],
            OP.mult, OP.mult), waits=[a])
        for tt in range(16):
            a = V_(lambda e, tt=tt: e.tensor_scalar(
                FT8[:], C["iota8"][:], STC[:, GA1 + tt:GA1 + tt + 1], None,
                OP.is_equal), waits=[a])
            a = V_(lambda e, tt=tt: e.tensor_scalar(
                FT8[:], FT8[:], STC[:, GW1 + tt:GW1 + tt + 1], None, OP.mult),
                waits=[a])
            a = V_(lambda e, tt=tt: e.tensor_scalar(
                FT8b[:], C["iota8"][:], STC[:, GA2 + tt:GA2 + tt + 1], None,
                OP.is_equal), waits=[a])
            a = V_(lambda e, tt=tt: e.tensor_scalar(
                FT8b[:], FT8b[:], STC[:, GW2 + tt:GW2 + tt + 1], None, OP.mult),
                waits=[a])
            a = V_(lambda e: e.tensor_add(FT8[:], FT8[:], FT8b[:]), waits=[a])
            tokt = TT_(lambda e: e.transpose(PS[5][0:8, 0:128], FT8[:],
                                             C["ident_f"][:]),
                       waits=[a, bank_free[5]])
            a = V_(lambda e, tt=tt: e.tensor_copy(CFM[0:8, 128 * tt:128 * tt + 128],
                                                  PS[5][0:8, 0:128]), waits=[tokt])
            consume(5, a)
        cfm_tok = a
        sclose()
        bar2 = barrier()
        bufs = sopen(
            (pref + "HMS", [128, 8 * 512], BF16),
            (pref + "HM0", [128, 4 * 512], BF16), (pref + "HM1", [128, 4 * 512], BF16),
            (pref + "HM2", [128, 4 * 512], BF16), (pref + "HM3", [128, 4 * 512], BF16),
            (pref + "HM4", [128, 4 * 512], BF16), (pref + "HM5", [128, 4 * 512], BF16),
            (pref + "HM6", [128, 4 * 512], BF16), (pref + "HM7", [128, 4 * 512], BF16),
            (pref + "SGB", [128, 2 * 512], BF16),
            (pref + "GUW0", [128, 8 * 256], BF16), (pref + "GUW1", [128, 8 * 256], BF16),
            (pref + "GUW2", [128, 8 * 256], BF16), (pref + "GUW3", [128, 8 * 256], BF16),
            (pref + "WD0", [128, 1024], BF16), (pref + "WD1", [128, 1024], BF16))
        HMS = bufs[0]
        HM = bufs[1:9]
        SGB, GUW0, GUW1, GUW2, GUW3, WD0, WD1 = bufs[9:]
        GUW = [GUW0, GUW1, GUW2, GUW3]
        WD = [WD0, WD1]
        guw = Dma(p, "sync", pref + "guw", n_slots=4)
        wdd = Dma(p, "scalar", pref + "wdd", n_slots=2)
        lastx = state["x"]
        guw_free = [list(bar2) for _ in range(4)]
        wd_free = [list(bar2), list(bar2)]
        hm_free = {}
        sg_free = {}
        slot_ctr = [0]
        cast_tok = state.get("l1cast", T0)

        def gu_block(g, gscr, uscr, base_off, NI, HBUF, key, cbc):
            cbc_tok = None
            if cbc is not None:
                cbc_tok = mm_group(4, [(SEL8B[0:8, cbc * 128:cbc * 128 + 128],
                                        CFM[0:8, TG * g:TG * g + TG],
                                        PS[4][0:128, 0:TG])],
                                   waits=[cfm_tok, selt])
            for sl_ in range(NI // 256):
                gslot = slot_ctr[0] % 4
                uslot = (slot_ctr[0] + 1) % 4
                slot_ctr[0] += 2
                ldg = guw.load(
                    bass.AP(GUW[gslot], 0, [[8 * 256, 128], [256, NC], [1, 256]]),
                    bass.AP(gscr, base_off + sl_ * 256,
                            [[NI, 128], [128 * NI, NC], [1, 256]]),
                    waits=list(guw_free[gslot]) + [cast_tok], slot=gslot)
                ldu = guw.load(
                    bass.AP(GUW[uslot], 0, [[8 * 256, 128], [256, NC], [1, 256]]),
                    bass.AP(uscr, base_off + sl_ * 256,
                            [[NI, 128], [128 * NI, NC], [1, 256]]),
                    waits=list(guw_free[uslot]) + [cast_tok], slot=uslot)
                gtoks = []
                for nsub in range(2):
                    ic = sl_ * 2 + nsub
                    bi = ic % 2
                    mms = [(GUW[gslot][:, kc * 256 + nsub * 128:kc * 256 + nsub * 128 + 128],
                            XN[:, T * kc + TG * g:T * kc + TG * g + TG])
                           for kc in range(NC)]
                    gtok = mm_group(bi, mms, waits=[ldg, xn_tok])
                    sgt = S_(lambda e, bi=bi, nsub=nsub: e.activation(
                        out=SGB[:, nsub * 512:nsub * 512 + 512],
                        in_=PS[bi][0:128, 0:TG], func=AF.Silu),
                        waits=[gtok, sg_free.get(nsub, T0)])
                    consume(bi, sgt)
                    sg_free[nsub] = sgt
                    gtoks.append((gtok, sgt))
                for nsub in range(2):
                    ic = sl_ * 2 + nsub
                    bi = 2 + ic % 2
                    mms = [(GUW[uslot][:, kc * 256 + nsub * 128:kc * 256 + nsub * 128 + 128],
                            XN[:, T * kc + TG * g:T * kc + TG * g + TG])
                           for kc in range(NC)]
                    utok = mm_group(bi, mms, waits=[ldu, xn_tok])
                    hmt = V_(lambda e, bi=bi, ic=ic, nsub=nsub, HBUF=HBUF:
                             e.scalar_tensor_tensor(
                                 HBUF[:, ic * 512:ic * 512 + 512],
                                 PS[bi][0:128, 0:TG], 1.0,
                                 SGB[:, nsub * 512:nsub * 512 + 512],
                                 OP.mult, OP.mult),
                             waits=[utok, gtoks[nsub][1],
                                    hm_free.get((key, ic), T0)])
                    if cbc is not None:
                        hmt = V_(lambda e, ic=ic, HBUF=HBUF: e.tensor_mul(
                            HBUF[:, ic * 512:ic * 512 + 512],
                            HBUF[:, ic * 512:ic * 512 + 512],
                            PS[4][0:128, 0:TG]), waits=[hmt, cbc_tok])
                    consume(bi, hmt)
                    hm_free[(key, ic)] = hmt
                    sg_free[nsub] = hmt
                guw_free[gslot] = [gtoks[-1][0]]
                guw_free[uslot] = [utok]
            if cbc is not None:
                consume(4, hm_free[(key, NI // 128 - 1)])

        for g in range(NTG):
            gu_block(g, s_wsg.tensor, s_wsu.tensor, 0, 1024, HMS, "s", None)
            for e_ in range(8):
                gu_block(g, s_wg.tensor, s_wu.tensor, e_ * D * 512, 512,
                         HM[e_], "e%d" % e_, e_)
            dn = [("s", HMS, ic, s_wsd.tensor, ic * 128 * 1024) for ic in range(8)]
            for e_ in range(8):
                dn += [("e%d" % e_, HM[e_], ic, s_wd.tensor,
                        e_ * 512 * 1024 + ic * 128 * 1024) for ic in range(4)]
            z_toks = [None] * 8
            for di, (key, HBUF, ic, wt, woff) in enumerate(dn):
                slot = di % 2
                ldt = wdd.load(bass.AP(WD[slot], 0, [[1024, 128], [1, 1024]]),
                               bass.AP(wt, woff, [[1024, 128], [1, 1024]]),
                               waits=list(wd_free[slot]) + [cast_tok], slot=slot)
                lastmm = None
                for nc_ in range(8):
                    if nc_ == 0:
                        w0 = [ldt, hm_free[(key, ic)]]
                    else:
                        w0 = []
                    if di == 0:
                        w0 = w0 + [bank_free[nc_]]
                    lastmm = p.emit(
                        "tensor",
                        lambda e, nc_=nc_, slot=slot, HBUF=HBUF, ic=ic,
                        st=(di == 0), sp=(di == len(dn) - 1): e.matmul(
                            PS[nc_][0:128, 0:TG],
                            WD[slot][:, nc_ * 128:nc_ * 128 + 128],
                            HBUF[:, ic * 512:ic * 512 + 512], start=st, stop=sp),
                        waits=w0,
                        inc=("tE" if (nc_ == 7 or di == len(dn) - 1) else None))
                    if di == len(dn) - 1:
                        z_toks[nc_] = lastmm
                wd_free[slot] = [lastmm]
            for nc_ in range(8):
                xt = V_(lambda e, nc_=nc_, g=g: e.scalar_tensor_tensor(
                    X[:, T * nc_ + TG * g:T * nc_ + TG * g + TG],
                    PS[nc_][0:128, 0:TG], 0.5,
                    X[:, T * nc_ + TG * g:T * nc_ + TG * g + TG], OP.mult, OP.add),
                    waits=[z_toks[nc_], lastx])
                consume(nc_, xt)
                lastx = xt
            for key, HBUF, ic, wt, woff in dn:
                hm_free[(key, ic)] = z_toks[7]
            for ic in range(2):
                sg_free[ic] = z_toks[7]
        state["x"] = lastx
        sclose()
        sclose()

    def sub_l1attn():
        bar = barrier()
        st_tok = stats(X, state["x"], rms_eps=1e-6)
        AUX, = sopen(("l5AUX", [128, NC * T], F32))
        lnx_tok = materialize_lnx(AUX, st_tok, bar=bar)
        xn_tok = materialize_xn(AUX, lnx_tok, RRC, SM["l1w"], None)
        lastx = state["x"]
        for c in range(NC):
            lastx = V_(lambda e, c=c: e.tensor_add(
                X[:, T * c:T * c + T], X[:, T * c:T * c + T],
                AUX[:, T * c:T * c + T]), waits=[lnx_tok, lastx])
        state["x"] = lastx
        sclose()
        bar2 = barrier()
        FB, OB, GBF, SB_, WSL0, WSL1, WSMB, WSMF = sopen(
            ("l5FB", [128, T], F32), ("l5OB", [128, T], F32),
            ("l5GB", [128, T], F32), ("l5SB", [128, T], F32),
            ("l5W0", [128, 8 * 128], BF16), ("l5W1", [128, 8 * 128], BF16),
            ("l5WSM", [128, 8], BF16), ("l5WSF", [128, 8], F32))
        WSL = [WSL0, WSL1]
        wl = Dma(p, "gpsimd", "l5wl", n_slots=2)
        wl_free = [list(bar2), list(bar2)]
        fb_free = list(bar2)
        ob_free = list(bar2)
        gb_free = list(bar2)
        sb_free = list(bar2)
        wsm_free = T0
        sctr = [0]

        def wload(wh, c):
            slot = sctr[0] % 2
            sctr[0] += 1
            t = wl.load(bass.AP(WSL[slot], 0, [[8 * 128, 128], [128, NC], [1, 128]]),
                        bass.AP(wh.tensor, c * 128, [[D, 128], [128 * D, NC], [1, 128]]),
                        waits=list(wl_free[slot]), slot=slot)
            return slot, t

        for c in range(NC):
            slot, ldt = wload(h_wf, c)
            fts = []
            for g in range(NTG):
                bi = g % 2
                mms = [(WSL[slot][:, kc * 128:kc * 128 + 128],
                        XN[:, T * kc + TG * g:T * kc + TG * g + TG])
                       for kc in range(NC)]
                tok = mm_group(bi, mms, waits=[ldt, xn_tok])
                ft = S_(lambda e, bi=bi, g=g: e.activation(
                    out=FB[:, TG * g:TG * g + TG], in_=PS[bi][0:128, 0:TG],
                    func=AF.Sigmoid), waits=[tok] + (fb_free if g == 0 else []))
                consume(bi, ft)
                fts.append(ft)
            wl_free[slot] = [tok]
            fmax = V_(lambda e: e.tensor_scalar(FB[:], FB[:], 0.9, None, OP.max),
                      waits=[fts[-1]])
            slot, ldt = wload(h_wo, c)
            ots = []
            for g in range(NTG):
                bi = g % 2
                mms = [(WSL[slot][:, kc * 128:kc * 128 + 128],
                        XN[:, T * kc + TG * g:T * kc + TG * g + TG])
                       for kc in range(NC)]
                tok = mm_group(bi, mms, waits=[ldt, xn_tok])
                ot = S_(lambda e, bi=bi, g=g: e.activation(
                    out=OB[:, TG * g:TG * g + TG], in_=PS[bi][0:128, 0:TG],
                    func=AF.Sigmoid), waits=[tok] + (ob_free if g == 0 else []))
                consume(bi, ot)
                ots.append(ot)
            wl_free[slot] = [tok]
            slot, ldt = wload(h_wi, c)
            wst = None
            for kc in range(NC):
                wst = V_(lambda e, kc=kc, slot=slot: e.reduce_sum(
                    WSMF[:, kc:kc + 1], WSL[slot][:, kc * 128:kc * 128 + 128],
                    axis=AX.X), waits=[ldt, wsm_free])
            wst = V_(lambda e: e.tensor_copy(WSMB[:], WSMF[:]), waits=[wst])
            wl_free[slot] = [wst]
            ginit = V_(lambda e: e.tensor_scalar(GBF[:], FB[:], -1.0, 1.0,
                                                 OP.mult, OP.add),
                       waits=[fmax] + gb_free)
            glast = None
            for g in range(NTG):
                tokv = mm_group(6, [(WSMB[:, kc:kc + 1],
                                     XN[:, T * kc + TG * g:T * kc + TG * g + TG],
                                     PS[6][0:1, 0:TG]) for kc in range(NC)],
                                waits=[wst, xn_tok])
                cpv = V_(lambda e: e.tensor_copy(ROWS[0:1, 0:TG], PS[6][0:1, 0:TG]),
                         waits=[tokv, rows_free[0]])
                consume(6, cpv)
                tokb = mm_group(5, [(C["r064f"][0:1, 0:128], ROWS[0:1, 0:TG],
                                     PS[5][0:128, 0:TG])], waits=[cpv])
                rows_free[0] = tokb
                glast = V_(lambda e, g=g: e.tensor_mul(
                    GBF[:, TG * g:TG * g + TG], GBF[:, TG * g:TG * g + TG],
                    PS[5][0:128, 0:TG]), waits=[tokb, ginit])
                consume(5, glast)
            wsm_free = glast
            sc = V_(lambda e: e.tensor_tensor_scan(SB_[:], FB[:], GBF[:], 0.0,
                                                   OP.mult, OP.add),
                    waits=[glast, fmax] + sb_free)
            fb_free = [sc]
            gb_free = [sc]
            yv = V_(lambda e: e.tensor_mul(SB_[:], SB_[:], OB[:]), waits=[sc, ots[-1]])
            ob_free = [yv]
            lastx = V_(lambda e, c=c: e.tensor_add(
                X[:, T * c:T * c + T], X[:, T * c:T * c + T], SB_[:]),
                waits=[yv, lastx])
            sb_free = [lastx]
        state["x"] = lastx
        sclose()

    if n_sub >= 3:
        sub_l0ffn("f3", t1, t2)
    if n_sub >= 4:
        sub_l1ffn("m4")
    if n_sub >= 5:
        sub_l1attn()
    if n_sub >= 6:
        sub_l1ffn("m6")


    st = dX.load(bass.AP(out_h.tensor, 0, [[T, 128], [128 * T, NC], [1, T]]),
                 bass.AP(X, 0, [[NC * T, 128], [T, NC], [1, T]]), waits=[state["x"]],
                 slot=0)
    p.emit("sync", None, waits=[st])

    with nc.Block() as block:
        @block.sync
        def _(e):
            p.replay("sync", e)

        @block.tensor
        def _(e):
            p.replay("tensor", e)

        @block.vector
        def _(e):
            p.replay("vector", e)

        @block.scalar
        def _(e):
            p.replay("scalar", e)

        @block.gpsimd
        def _(e):
            p.replay("gpsimd", e)

    es.close()
    return nc


def kernel(**inputs):
    in_maps = _host_inputs(inputs)
    nc = build()
    if _RUN is not None:
        results = _RUN(nc, in_maps)
    else:
        res = run_bass_kernel_spmd(nc, in_maps, core_ids=list(range(8)))
        results = res.results
    out = np.stack([np.asarray(r["out"]).T for r in results])
    return np.ascontiguousarray(out.astype(np.float32))


# version 06
# speedup vs baseline: 1.1415x; 1.0111x over previous
import numpy as np
import ml_dtypes
import concourse.bass as bass
import concourse.mybir as mybir
from concourse.bass_utils import run_bass_kernel_spmd

F32 = mybir.dt.float32
BF16 = mybir.dt.bfloat16
I32 = mybir.dt.int32
AF = mybir.ActivationFunctionType
OP = mybir.AluOpType
AX = mybir.AxisListType

D = 1024
T = 2048
NC = 8
TG = 512
NTG = T // TG
HEADS = 16
NEGBIG = -1073741824.0

_N_SUB = 6
_RUN = None



def _t(a):
    return np.ascontiguousarray(np.asarray(a, np.float32).T)


def _cols(v, n):
    return np.ascontiguousarray(np.asarray(v, np.float32).reshape(n, 128).T)


def _consts():
    c = {}
    c["ident_f"] = np.eye(128, dtype=np.float32)
    itop = np.zeros((128, 128), dtype=ml_dtypes.bfloat16)
    ibot = np.zeros((128, 128), dtype=ml_dtypes.bfloat16)
    for q in range(64):
        itop[q, q] = 1
        itop[64 + q, q] = 1
        ibot[q, 64 + q] = 1
        ibot[64 + q, 64 + q] = 1
    c["itop"] = itop
    c["ibot"] = ibot
    m1 = np.where(np.arange(128)[None, :] > np.arange(128)[:, None], NEGBIG, 0.0)
    m1 = m1.astype(ml_dtypes.bfloat16)
    c["m1a"] = np.concatenate([m1[0:64], m1[0:64]], axis=0)
    c["m1b"] = np.concatenate([m1[64:128], m1[64:128]], axis=0)
    m2 = np.where(np.arange(128)[None, :] < np.arange(128)[:, None], NEGBIG, 0.0)
    m2 = m2.astype(ml_dtypes.bfloat16)
    c["m2a"] = np.concatenate([m2[0:64], m2[0:64]], axis=0)
    c["m2b"] = np.concatenate([m2[64:128], m2[64:128]], axis=0)
    r064 = np.zeros((128, 128), np.float32)
    r064[0, :] = 1.0
    r064[64, :] = 1.0
    c["r064f"] = r064
    c["r064b"] = r064.astype(ml_dtypes.bfloat16)
    c["invd"] = np.full((128, 1), 1.0 / D, np.float32)
    c["invdb"] = np.full((128, 1), 1.0 / D, ml_dtypes.bfloat16)
    c["eps5"] = np.full((128, 1), 1e-5, np.float32)
    c["eps6"] = np.full((128, 1), 1e-6, np.float32)
    c["onescol"] = np.ones((128, 1), np.float32)
    c["iota8"] = np.broadcast_to(np.arange(8, dtype=np.float32), (128, 8)).copy()
    c["c999"] = np.full((128, 8), 999.0, np.float32)
    c["cneg"] = np.full((128, 8), -1e30, np.float32)
    sel8 = np.zeros((8, 8 * 128), np.float32)
    for e in range(8):
        sel8[e, e * 128:(e + 1) * 128] = 1.0
    c["sel8"] = sel8
    c["sel8b"] = sel8.astype(ml_dtypes.bfloat16)
    return c


CONST_SPECS = [
    ("ident_f", [128, 128], F32), ("itop", [128, 128], BF16), ("ibot", [128, 128], BF16),
    ("m1a", [128, 128], BF16), ("m1b", [128, 128], BF16),
    ("m2a", [128, 128], BF16), ("m2b", [128, 128], BF16),
    ("r064f", [128, 128], F32), ("r064b", [128, 128], BF16),
    ("invd", [128, 1], F32), ("invdb", [128, 1], BF16),
    ("eps5", [128, 1], F32), ("eps6", [128, 1], F32),
    ("onescol", [128, 1], F32),
    ("iota8", [128, 8], F32), ("c999", [128, 8], F32), ("cneg", [128, 8], F32),
]

GEN_MATS = (("ca", 1024, 3072), ("cp", 1024, 1024), ("cf", 1024, 4096), ("cm", 4096, 1024))


def _host_inputs(inp):
    cst = _consts()
    shared = {}
    for nm, I_, O in GEN_MATS:
        key = "L0_attn_" + nm
        sub = np.asarray(inp[key + "_sub_w"], np.float32)
        shared[nm + "_s0"] = _t(sub[0])
        shared[nm + "_s1"] = _t(sub[1])
        shared[nm + "_W"] = _t(inp[key + "_W_w"])
        shared[nm + "_B"] = _t(inp[key + "_B_w"])
        wl = np.asarray(inp[key + "_Wl_w"], np.float32)
        shared[nm + "_wl"] = np.broadcast_to(wl[0][None, :], (128, 2)).copy()
        nco = O // 128
        shared[nm + "_sb"] = np.concatenate(
            [_cols(inp[key + "_sub_b"][0], nco), _cols(inp[key + "_sub_b"][1], nco)], axis=1)
        shared[nm + "_Wb"] = _cols(inp[key + "_W_b"], nco)
        shared[nm + "_Bb"] = _cols(inp[key + "_B_b"], nco)
        wlb = np.asarray(inp[key + "_Wl_b"], np.float32)
        shared[nm + "_wlb"] = np.broadcast_to(wlb[0][None, :], (128, 2)).copy()
    shared["ln1g"] = _cols(inp["L0_attn_ln1_g"], NC)
    shared["ln1b"] = _cols(inp["L0_attn_ln1_b"], NC)
    shared["ln2g"] = _cols(inp["L0_attn_ln2_g"], NC)
    shared["ln2b"] = _cols(inp["L0_attn_ln2_b"], NC)
    shared["l0g"] = _cols(inp["L0_ffn_ln_gamma"], NC)
    shared["l0b"] = _cols(inp["L0_ffn_ln_beta"], NC)
    shared["fc1q"] = np.ascontiguousarray(np.asarray(inp["L0_ffn_fc1_q"], np.int32))
    shared["fc1s"] = np.ascontiguousarray(np.asarray(inp["L0_ffn_fc1_scales"], np.float32))
    shared["fc1b"] = _cols(inp["L0_ffn_fc1_bias"], 32)
    shared["fc2q"] = np.ascontiguousarray(np.asarray(inp["L0_ffn_fc2_q"], np.int32))
    s2 = np.asarray(inp["L0_ffn_fc2_scales"], np.float32)
    shared["fc2s"] = np.ascontiguousarray(
        s2.reshape(4, 8, 1024).transpose(1, 0, 2).reshape(8, 4096))
    shared["fc2bcol"] = _cols(inp["L0_ffn_fc2_bias"], NC)
    shared["l1w"] = _cols(inp["L1_attn_norm_w"], NC)
    shared["wf"] = _t(inp["L1_attn_w_forget"])
    shared["wi"] = _t(inp["L1_attn_w_input"])
    shared["wo"] = _t(inp["L1_attn_w_output"])
    shared["gw"] = np.ascontiguousarray(
        _t(inp["L1_ffn_gate_weight"]).reshape(8, 128, 8).transpose(1, 0, 2).reshape(128, 64))
    shared["wg"] = np.ascontiguousarray(np.asarray(inp["L1_ffn_w_gate"], np.float32).transpose(0, 2, 1))
    shared["wu"] = np.ascontiguousarray(np.asarray(inp["L1_ffn_w_up"], np.float32).transpose(0, 2, 1))
    shared["wd"] = np.ascontiguousarray(np.asarray(inp["L1_ffn_w_down"], np.float32).transpose(0, 2, 1))
    shared["wsg"] = _t(inp["L1_ffn_ws_gate"])
    shared["wsu"] = _t(inp["L1_ffn_ws_up"])
    shared["wsd"] = _t(inp["L1_ffn_ws_down"])
    shared.update(cst)

    x = np.asarray(inp["L0_attn_hs"], np.float32)
    in_maps = []
    for c in range(8):
        m = dict(shared)
        m["x"] = _t(x[c])
        in_maps.append(m)
    return in_maps



ENGS = ("sync", "tensor", "vector", "scalar", "gpsimd")
T0 = ("tE", 0)


class P:
    def __init__(self, nc, es):
        self.nc = nc
        self.es = es
        self.q = {e: [] for e in ENGS}
        self.cnt = {}
        self.sems = {}

    def sem(self, name):
        if name not in self.sems:
            self.sems[name] = self.es.enter_context(self.nc.semaphore(name))
            self.cnt[name] = 0
        return name

    def emit(self, eng, fn, waits=(), inc=None, n=1):
        ws = []
        for s, v in waits:
            if v <= 0:
                continue
            assert v <= self.cnt.get(s, 0), f"wait on future {s}:{v}>{self.cnt.get(s, 0)}"
            ws.append((s, v))
        self.q[eng].append((ws, fn, inc, n))
        if inc is not None:
            self.sem(inc)
            self.cnt[inc] += n
            return (inc, self.cnt[inc])
        return None

    def replay(self, ename, engine):
        maxw = {}
        for ws, fn, inc, n in self.q[ename]:
            for s, v in ws:
                if maxw.get(s, 0) < v:
                    engine.wait_ge(self.sems[s], v)
                    maxw[s] = v
            if fn is None:
                continue
            inst = fn(engine)
            if inc is not None:
                inst.then_inc(self.sems[inc], n)


class Dma:

    def __init__(self, p, eng, name, n_slots=1):
        self.p = p
        self.eng = eng
        self.sems = [p.sem(f"{name}{i}") for i in range(n_slots)]

    def load(self, dst, src, waits=(), slot=0):
        return self.p.emit(self.eng, lambda e, d=dst, s=src: e.dma_start(out=d, in_=s),
                           waits=waits, inc=self.sems[slot], n=16)



def build(n_sub=None):
    if n_sub is None:
        n_sub = _N_SUB
    nc = bass.Bass("TRN2", target_bir_lowering=False)
    from contextlib import ExitStack
    es = ExitStack()
    p = P(nc, es)

    def din(name, shape, dt=F32):
        return nc.dram_tensor(name, shape, dt, kind="ExternalInput").ap()

    x_h = din("x", [D, T])
    out_h = nc.dram_tensor("out", [D, T], F32, kind="ExternalOutput").ap()
    cst_h = {n_: din(n_, sh, dt) for n_, sh, dt in CONST_SPECS}
    h_sel8 = din("sel8", [8, 8 * 128])
    genh = {}
    for nm, I_, O in GEN_MATS:
        nco = O // 128
        genh[nm] = dict(
            s0=din(nm + "_s0", [I_, O]), s1=din(nm + "_s1", [I_, O]),
            W=din(nm + "_W", [I_, O]), B=din(nm + "_B", [I_, O]),
            wl=din(nm + "_wl", [128, 2]), sb=din(nm + "_sb", [128, 2 * nco]),
            Wb=din(nm + "_Wb", [128, nco]), Bb=din(nm + "_Bb", [128, nco]),
            wlb=din(nm + "_wlb", [128, 2]), O=O, I=I_, nco=nco)
    smh = {n_: din(n_, sh) for n_, sh in [
        ("ln1g", [128, NC]), ("ln1b", [128, NC]), ("ln2g", [128, NC]), ("ln2b", [128, NC]),
        ("l0g", [128, NC]), ("l0b", [128, NC]), ("fc1b", [128, 32]),
        ("l1w", [128, NC]), ("gw", [128, 64]), ("fc2bcol", [128, NC])]}
    h_fc1s = din("fc1s", [8, 4096])
    h_fc2s = din("fc2s", [8, 4096])
    h_fc1q = din("fc1q", [D, 4096], I32)
    h_fc2q = din("fc2q", [4096, D], I32)
    h_wf = din("wf", [D, D]); h_wi = din("wi", [D, D]); h_wo = din("wo", [D, D])
    h_wg = din("wg", [8, D, 512]); h_wu = din("wu", [8, D, 512]); h_wd = din("wd", [8, 512, D])
    h_wsg = din("wsg", [D, D]); h_wsu = din("wsu", [D, D]); h_wsd = din("wsd", [D, D])

    s_fc1 = nc.dram_tensor("s_fc1", [D, 4096], BF16, kind="Internal").ap()
    s_fc2 = nc.dram_tensor("s_fc2", [4096, D], BF16, kind="Internal").ap()
    s_cf = nc.dram_tensor("s_cf", [D, 4096], BF16, kind="Internal").ap()
    s_cm = nc.dram_tensor("s_cm", [4096, D], BF16, kind="Internal").ap()
    s_qkv = nc.dram_tensor("s_qkv", [D, 3072], BF16, kind="Internal").ap()
    s_cp = nc.dram_tensor("s_cp", [D, D], BF16, kind="Internal").ap()
    s_lnx = nc.dram_tensor("s_lnx", [D, T], F32, kind="Internal").ap()
    s_wg = nc.dram_tensor("s_wg", [8, D, 512], BF16, kind="Internal").ap()
    s_wu = nc.dram_tensor("s_wu", [8, D, 512], BF16, kind="Internal").ap()
    s_wd = nc.dram_tensor("s_wd", [8, 512, D], BF16, kind="Internal").ap()
    s_wsg = nc.dram_tensor("s_wsg", [D, D], BF16, kind="Internal").ap()
    s_wsu = nc.dram_tensor("s_wsu", [D, D], BF16, kind="Internal").ap()
    s_wsd = nc.dram_tensor("s_wsd", [D, D], BF16, kind="Internal").ap()
    s_v = nc.dram_tensor("s_v", [T, HEADS * 65], BF16, kind="Internal").ap()

    def sb(name, shape, dt):
        return es.enter_context(nc.sbuf_tensor(name, shape, dt))

    X = sb("X", [128, NC * T], F32)
    XN = sb("XN", [128, NC * T], BF16)
    C = {n_: sb("c_" + n_, sh, dt) for n_, sh, dt in CONST_SPECS}
    SM = {n_: sb("p_" + n_, sh, F32) for n_, sh in [
        ("ln1g", [128, NC]), ("ln1b", [128, NC]), ("ln2g", [128, NC]), ("ln2b", [128, NC]),
        ("l0g", [128, NC]), ("l0b", [128, NC]), ("fc1b", [128, 32]),
        ("l1w", [128, NC]), ("gw", [128, 64]), ("fc2bcol", [128, NC])]}
    GB = {}
    for nm, I_, O in GEN_MATS:
        nco = O // 128
        GB[nm] = dict(sb=sb(nm + "sb", [128, 2 * nco], F32), Wb=sb(nm + "Wb", [128, nco], F32),
                      Bb=sb(nm + "Bb", [128, nco], F32), wl=sb(nm + "wl", [128, 2], F32),
                      wlb=sb(nm + "wlb", [128, 2], F32), bias=sb(nm + "bias", [128, nco], F32))
    ROWS = sb("ROWS", [1, 2048], F32)
    STC = sb("STC", [128, 256], F32)
    SCR = sb("SCR", [128, 1024], F32)
    SCRB = sb("SCRB", [128, 1024], BF16)
    PS = [es.enter_context(nc.psum_tensor(f"PS{i}", [128, 512], F32)) for i in range(8)]

    scope_stack = []

    def sopen(*specs):
        cms = []
        ts = []
        for nm_, sh_, dt_ in specs:
            cm = nc.sbuf_tensor(nm_, sh_, dt_)
            ts.append(cm.__enter__())
            cms.append(cm)
        scope_stack.append(cms)
        return ts

    def sclose():
        cms = scope_stack.pop()
        for cm in reversed(cms):
            cm.__exit__(None, None, None)

    def barrier():
        return [(sname, p.cnt[sname]) for sname in list(p.cnt)]

    dW = Dma(p, "sync", "dW")
    dX = Dma(p, "sync", "dX")
    dS = Dma(p, "scalar", "dS")
    dG = Dma(p, "gpsimd", "dG")
    for s in ("tE", "vE", "sE"):
        p.sem(s)

    def TT_(fn, waits=(), inc="tE"):
        return p.emit("tensor", fn, waits=waits, inc=inc)

    def V_(fn, waits=(), inc="vE"):
        return p.emit("vector", fn, waits=waits, inc=inc)

    def S_(fn, waits=(), inc="sE"):
        return p.emit("scalar", fn, waits=waits, inc=inc)

    bank_free = [T0] * 8
    prep_tasks = []

    def pump(k=1):
        for _ in range(min(k, len(prep_tasks))):
            prep_tasks.pop(0)()

    def pump_all():
        while prep_tasks:
            prep_tasks.pop(0)()

    def mm_group(bi, mms, waits=(), out=None):
        w = list(waits) + [bank_free[bi]]
        n = len(mms)
        tok = None
        for i, mm in enumerate(mms):
            lh, rh = mm[0], mm[1]
            oa = mm[2] if len(mm) > 2 else PS[bi][:]
            tok = p.emit("tensor",
                         lambda e, oa=oa, lh=lh, rh=rh, st=(i == 0), sp=(i == n - 1):
                         e.matmul(oa, lh, rh, start=st, stop=sp),
                         waits=(w if i == 0 else ()), inc=("tE" if i == n - 1 else None))
        return tok

    def consume(bi, tok):
        bank_free[bi] = tok

    x_tok = dX.load(bass.AP(X, 0, [[NC * T, 128], [T, NC], [1, T]]),
                    bass.AP(x_h.tensor, 0, [[T, 128], [128 * T, NC], [1, T]]))
    ctok = None
    for n_ in C:
        ctok = dW.load(C[n_][:], cst_h[n_])
    for n_ in SM:
        ctok = dW.load(SM[n_][:], smh[n_])
    for nm, I_, O in GEN_MATS:
        for k2 in ("sb", "Wb", "Bb", "wl", "wlb"):
            ctok = dW.load(GB[nm][k2][:], genh[nm][k2])
    CW = ctok

    gbias = {}
    for nm, I_, O in GEN_MATS:
        b = GB[nm]
        nco = O // 128
        t1 = S_(lambda e, b=b, nco=nco: e.activation(
            out=b["bias"][:, 0:nco], in_=b["sb"][:, nco:2 * nco], func=AF.Copy,
            scale=b["wlb"][:, 1:2]), waits=[CW])
        t2 = V_(lambda e, b=b, nco=nco: e.scalar_tensor_tensor(
            b["bias"][:, 0:nco], b["sb"][:, 0:nco], b["wlb"][:, 0:1],
            b["bias"][:, 0:nco], OP.mult, OP.add), waits=[t1, CW])
        t3 = S_(lambda e, b=b, nco=nco: e.activation(
            out=b["bias"][:, 0:nco], in_=b["bias"][:, 0:nco], func=AF.Tanh), waits=[t2])
        t4 = V_(lambda e, b=b, nco=nco: e.scalar_tensor_tensor(
            b["bias"][:, 0:nco], b["bias"][:, 0:nco], 1.0, b["Wb"][:, 0:nco],
            OP.mult, OP.mult), waits=[t3])
        t5 = V_(lambda e, b=b, nco=nco: e.tensor_add(
            b["bias"][:, 0:nco], b["bias"][:, 0:nco], b["Bb"][:, 0:nco]), waits=[t4])
        gbias[nm] = t5
    gbias["ca"] = V_(lambda e: e.tensor_scalar_mul(
        GB["ca"]["bias"][:, 0:8], GB["ca"]["bias"][:, 0:8], 0.125), waits=[gbias["ca"]])
    FC2BQ = sb("FC2BQ", [128, NC], F32)
    fc2bq_tok = V_(lambda e: e.tensor_scalar_mul(FC2BQ[:], SM["fc2bcol"][:], 0.25),
                   waits=[CW])

    rows_free = [T0] * 4

    def col_to_bcast(col_of, n_cols, bi_t, bi_bc, rslot, waits=(), out_bf16=None):
        w = list(waits) + [bank_free[bi_t]]
        tok = None
        for j in range(n_cols):
            tok = p.emit("tensor",
                         lambda e, j=j: e.transpose(
                             PS[bi_t][0:1, 128 * j:128 * (j + 1)], col_of(j), C["ident_f"][:]),
                         waits=(w if j == 0 else ()), inc=("tE" if j == n_cols - 1 else None))
        tokc = V_(lambda e: e.tensor_copy(
            ROWS[0:1, 512 * rslot:512 * rslot + 128 * n_cols],
            PS[bi_t][0:1, 0:128 * n_cols]), waits=[tok, rows_free[rslot]])
        consume(bi_t, tokc)
        tokb = mm_group(bi_bc, [(C["r064f"][0:1, 0:128],
                                 ROWS[0:1, 512 * rslot:512 * rslot + 128 * n_cols],
                                 PS[bi_bc][0:128, 0:128 * n_cols])], waits=[tokc])
        rows_free[rslot] = tokb
        if out_bf16 is not None:
            toko = V_(lambda e: e.tensor_copy(out_bf16, PS[bi_bc][0:128, 0:128 * n_cols]),
                      waits=[tokb])
            consume(bi_bc, toko)
            return tokb, toko
        return tokb, None

    SUMC, SQC, VARC, R1C, TMPC, R2C, RHC, MR1C, RRC = (0, 16, 32, 48, 64, 80, 96, 112, 128)

    def stats(buf, buf_tok, eps2=None, rms_eps=None):
        last_st = None
        for g in range(NTG):
            pump(1)
            casts = []
            for c in range(NC):
                casts.append(V_(lambda e, c=c, g=g: e.tensor_copy(
                    XN[:, T * c + TG * g:T * c + TG * g + TG],
                    buf[:, T * c + TG * g:T * c + TG * g + TG]),
                    waits=[buf_tok]))
            tok_s = mm_group(6, [(C["invdb"][:, 0:1],
                                  XN[:, T * c + TG * g:T * c + TG * g + TG],
                                  PS[6][0:1, 0:TG]) for c in range(NC)],
                             waits=[casts[-1]])
            tok_q = None
            for wave in range(4):
                sq_tok = None
                for cc in range(2):
                    c = 2 * wave + cc
                    sq_tok = S_(lambda e, c=c, cc=cc, g=g: e.activation(
                        out=SCRB[:, 512 * cc:512 * cc + 512],
                        in_=XN[:, T * c + TG * g:T * c + TG * g + TG], func=AF.Square),
                        waits=[casts[c], tok_q or state.get("scr_free", T0)]
                              + ([last_st] if last_st else []))
                w0 = [sq_tok] + ([bank_free[7]] if wave == 0 else [])
                for cc in range(2):
                    tok_q = p.emit("tensor",
                                   lambda e, cc=cc, st=(wave == 0 and cc == 0),
                                   sp=(wave == 3 and cc == 1): e.matmul(
                                       PS[7][0:1, 0:TG], C["invdb"][:, 0:1],
                                       SCRB[:, 512 * cc:512 * cc + 512],
                                       start=st, stop=sp),
                                   waits=(w0 if cc == 0 else ()),
                                   inc=("tE" if (cc == 1) else None))
            state["scr_free"] = tok_q
            cm1 = V_(lambda e: e.tensor_copy(ROWS[0:1, 0:TG], PS[6][0:1, 0:TG]),
                     waits=[tok_s, rows_free[0]])
            consume(6, cm1)
            cm2 = V_(lambda e: e.tensor_copy(ROWS[0:1, 512:512 + TG], PS[7][0:1, 0:TG]),
                     waits=[tok_q, rows_free[1]])
            consume(7, cm2)
            w1 = [bank_free[5], cm2]
            tokt = None
            for j in range(4):
                tokt = p.emit("tensor", lambda e, j=j: e.transpose(
                    PS[5][0:128, j:j + 1], ROWS[0:1, 128 * j:128 * (j + 1)],
                    C["ident_f"][0:1, 0:1]),
                    waits=(w1 if j == 0 else ()), inc=None)
            for j in range(4):
                tokt = p.emit("tensor", lambda e, j=j: e.transpose(
                    PS[5][0:128, 4 + j:5 + j], ROWS[0:1, 512 + 128 * j:512 + 128 * (j + 1)],
                    C["ident_f"][0:1, 0:1]), waits=(), inc=("tE" if j == 3 else None))
            st1 = V_(lambda e, g=g: e.tensor_copy(STC[:, SUMC + 4 * g:SUMC + 4 * g + 4],
                                                  PS[5][0:128, 0:4]), waits=[tokt])
            last_st = V_(lambda e, g=g: e.tensor_copy(STC[:, SQC + 4 * g:SQC + 4 * g + 4],
                                                      PS[5][0:128, 4:8]), waits=[tokt])
            consume(5, last_st)
            rows_free[0] = tokt
            rows_free[1] = tokt
        a = V_(lambda e: e.scalar_tensor_tensor(
            STC[:, TMPC:TMPC + 16], STC[:, SUMC:SUMC + 16], -1.0, STC[:, SUMC:SUMC + 16],
            OP.mult, OP.mult), waits=[last_st])
        a = V_(lambda e: e.tensor_add(STC[:, VARC:VARC + 16], STC[:, TMPC:TMPC + 16],
                                      STC[:, SQC:SQC + 16]), waits=[a])
        a = S_(lambda e: e.activation(out=STC[:, TMPC:TMPC + 16], in_=STC[:, VARC:VARC + 16],
                                      func=AF.Sqrt, bias=C["eps5"][:, 0:1]), waits=[a])
        a = V_(lambda e: e.reciprocal(STC[:, R1C:R1C + 16], STC[:, TMPC:TMPC + 16]), waits=[a])
        a = V_(lambda e: e.tensor_mul(STC[:, MR1C:MR1C + 16], STC[:, SUMC:SUMC + 16],
                                      STC[:, R1C:R1C + 16]), waits=[a])
        if eps2 is not None or rms_eps is not None:
            a = V_(lambda e: e.tensor_mul(STC[:, TMPC:TMPC + 16], STC[:, R1C:R1C + 16],
                                          STC[:, R1C:R1C + 16]), waits=[a])
            a = V_(lambda e: e.tensor_mul(STC[:, TMPC:TMPC + 16], STC[:, TMPC:TMPC + 16],
                                          STC[:, VARC:VARC + 16]), waits=[a])
            ee = eps2 if eps2 is not None else rms_eps
            tgt = R2C if eps2 is not None else RRC
            eps_ap = C["eps5"] if ee == 1e-5 else C["eps6"]
            a = S_(lambda e, eps_ap=eps_ap: e.activation(out=STC[:, TMPC:TMPC + 16],
                                                 in_=STC[:, TMPC:TMPC + 16],
                                                 func=AF.Sqrt, bias=eps_ap[:, 0:1]), waits=[a])
            a = V_(lambda e, tgt=tgt: e.reciprocal(STC[:, tgt:tgt + 16],
                                                   STC[:, TMPC:TMPC + 16]), waits=[a])
            if eps2 is not None:
                a = V_(lambda e: e.tensor_mul(STC[:, RHC:RHC + 16], STC[:, R1C:R1C + 16],
                                              STC[:, R2C:R2C + 16]), waits=[a])
        return a

    def stc_col(slot, g):
        return lambda j: STC[:, slot + 4 * g + j:slot + 4 * g + j + 1]

    state = {"x": x_tok}

    def materialize_lnx(AUX, stat_tok, bar=()):
        last = None
        for g in range(NTG):
            tr, _ = col_to_bcast(stc_col(R1C, g), 4, 5, 4, 0, waits=[stat_tok])
            tm, _ = col_to_bcast(stc_col(MR1C, g), 4, 5, 3, 1, waits=[stat_tok])
            for c in range(NC):
                tt = V_(lambda e, c=c, g=g: e.scalar_tensor_tensor(
                    AUX[:, T * c + TG * g:T * c + TG * g + TG],
                    X[:, T * c + TG * g:T * c + TG * g + TG], 1.0,
                    PS[4][0:128, 0:TG], OP.mult, OP.mult),
                    waits=[tr, state["x"], stat_tok] + list(bar))
                tt = V_(lambda e, c=c, g=g: e.tensor_sub(
                    AUX[:, T * c + TG * g:T * c + TG * g + TG],
                    AUX[:, T * c + TG * g:T * c + TG * g + TG],
                    PS[3][0:128, 0:TG]), waits=[tm, tt])
                last = tt
            consume(4, last)
            consume(3, last)
        return last

    def materialize_xn(SRC, src_tok, colslot, gamma, beta):
        last = None
        for g in range(NTG):
            tb, _ = col_to_bcast(stc_col(colslot, g), 4, 5, 4, 0, waits=[src_tok])
            for c in range(NC):
                tt = V_(lambda e, c=c, g=g: e.scalar_tensor_tensor(
                    XN[:, T * c + TG * g:T * c + TG * g + TG],
                    SRC[:, T * c + TG * g:T * c + TG * g + TG], 1.0,
                    PS[4][0:128, 0:TG], OP.mult, OP.mult), waits=[tb, src_tok])
                if gamma is not None:
                    if beta is not None:
                        tt = V_(lambda e, c=c, g=g: e.tensor_scalar(
                            XN[:, T * c + TG * g:T * c + TG * g + TG],
                            XN[:, T * c + TG * g:T * c + TG * g + TG],
                            gamma[:, c:c + 1], beta[:, c:c + 1], OP.mult, OP.add),
                            waits=[tt, CW])
                    else:
                        tt = V_(lambda e, c=c, g=g: e.tensor_scalar(
                            XN[:, T * c + TG * g:T * c + TG * g + TG],
                            XN[:, T * c + TG * g:T * c + TG * g + TG],
                            gamma[:, c:c + 1], None, OP.mult),
                            waits=[tt, CW])
                last = tt
            consume(4, last)
        return last

    dq = Dma(p, "gpsimd", "dq")
    dqs = Dma(p, "scalar", "dqs")
    dqsc = Dma(p, "sync", "dqsc", n_slots=2)

    def emit_dequant(DQ, DQO, SEL8, SCL, selw, qh, scales_h, scratch, n_k, n_n, blkmap, bar):
        dq_free = state.get("dq", T0)
        dqo_free = state.get("dqo", T0)
        scl_free = state.get("scl", T0)
        last_store = None
        for kc in range(n_k):
            tq = dq.load(DQ[:, 0:n_n],
                         bass.AP(qh.tensor, kc * 128 * n_n, [[n_n, 128], [1, n_n]]),
                         waits=[dq_free] + bar)
            selp, colb = blkmap(kc)
            td = None
            for j in range(n_n // 512):
                jj = j % 2
                tsc = dqsc.load(SCL[0:8, jj * 512:jj * 512 + 512],
                                bass.AP(scales_h.tensor, colb * n_n + j * 512,
                                        [[4096, 8], [1, 512]]),
                                waits=[state.get("sclb%d" % jj, T0)] + bar, slot=jj)
                tokb = mm_group(2, [(SEL8[0:8, selp * 128:selp * 128 + 128],
                                     SCL[0:8, jj * 512:jj * 512 + 512],
                                     PS[2][0:128, 0:512])], waits=[CW, tsc] + selw)
                state["sclb%d" % jj] = tokb
                td = V_(lambda e, j=j: e.scalar_tensor_tensor(
                    DQO[:, j * 512:(j + 1) * 512], DQ[:, j * 512:(j + 1) * 512],
                    -128.0, PS[2][0:128, 0:512], OP.add, OP.mult),
                    waits=[tokb, tq, dqo_free])
                consume(2, td)
            dq_free = td
            ts = dqs.load(bass.AP(scratch.tensor, kc * 128 * n_n, [[n_n, 128], [1, n_n]]),
                          DQO[:, 0:n_n], waits=[td])
            dqo_free = ts
            last_store = ts
        state["dq"] = dq_free
        state["dqo"] = dqo_free
        return last_store

    def emit_gen_tasks(nm, scratch, GS, GOUT, GSCR, bar, st_, scale_head=None):
        g = genh[nm]
        b = GB[nm]
        O = g["O"]
        I_ = g["I"]
        NRG = I_ // 1024
        genw = Dma(p, "sync", "g_" + nm, n_slots=2)
        genw2 = Dma(p, "scalar", "g2_" + nm, n_slots=2)
        gens_ = Dma(p, "scalar", "gs_" + nm)
        NSL = O // 128

        def task(jj):
            j, rg = jj // NRG, jj % NRG
            tks = []
            for i, key in enumerate(("s0", "s1", "W", "B")):
                dq_ = genw if i < 2 else genw2
                tks.append(dq_.load(
                    bass.AP(GS[i], 0, [[1024, 128], [128, NC], [1, 128]]),
                    bass.AP(g[key].tensor, rg * 1024 * O + j * 128,
                            [[O, 128], [128 * O, NC], [1, 128]]),
                    waits=list(st_["gfree"][i]), slot=i % 2))
            t1 = S_(lambda e: e.activation(out=GSCR[:, 0:1024], in_=GS[1][:, 0:1024],
                                           func=AF.Copy, scale=b["wl"][:, 1:2]),
                    waits=[tks[1], CW, st_["scr"]])
            t2 = V_(lambda e: e.scalar_tensor_tensor(
                GSCR[:, 0:1024], GS[0][:, 0:1024], b["wl"][:, 0:1], GSCR[:, 0:1024],
                OP.mult, OP.add), waits=[t1, tks[0]])
            t3 = S_(lambda e: e.activation(out=GSCR[:, 0:1024], in_=GSCR[:, 0:1024],
                                           func=AF.Tanh), waits=[t2])
            t4 = V_(lambda e: e.scalar_tensor_tensor(
                GSCR[:, 0:1024], GSCR[:, 0:1024], 1.0, GS[2][:, 0:1024],
                OP.mult, OP.mult), waits=[t3, tks[2]])
            gout_free = st_.get("gout", T0)
            if scale_head is not None and j < scale_head[0]:
                t5 = V_(lambda e: e.tensor_add(GSCR[:, 0:1024], GSCR[:, 0:1024],
                                               GS[3][:, 0:1024]), waits=[t4, tks[3]])
                t5 = V_(lambda e, f=scale_head[1]: e.tensor_scalar(
                    GOUT[:, 0:1024], GSCR[:, 0:1024], f, None, OP.mult),
                    waits=[t5, gout_free])
            else:
                t5 = V_(lambda e: e.tensor_add(
                    GOUT[:, 0:1024], GSCR[:, 0:1024], GS[3][:, 0:1024]),
                    waits=[t4, tks[3], gout_free])
            st_["scr"] = t5
            st_["gfree"] = [[t5]] * 4
            ts = gens_.load(bass.AP(scratch.tensor, rg * 1024 * O + j * 128,
                                    [[O, 128], [128 * O, NC], [1, 128]]),
                            bass.AP(GOUT, 0, [[1024, 128], [128, NC], [1, 128]]),
                            waits=[t5])
            st_["gout"] = ts
            state["gout_" + nm] = ts

        for jj in range(NSL * NRG):
            prep_tasks.append(lambda jj=jj: task(jj))

    def emit_cast_tasks(CASTB, bar):
        cst_dma = Dma(p, "gpsimd", "l1cast", n_slots=2)
        cst_st = Dma(p, "scalar", "l1casts", n_slots=2)
        st_ = {"free": [list(bar), list(bar)], "ctr": 0}

        def task(src, dst, off, nchunk, width):
            sl = st_["ctr"] % 2
            st_["ctr"] += 1
            ld = cst_dma.load(
                bass.AP(CASTB[sl], 0, [[1024, 128], [width, nchunk], [1, width]]),
                bass.AP(src, off, [[width, 128], [128 * width, nchunk], [1, width]]),
                waits=list(st_["free"][sl]), slot=sl)
            ts = cst_st.load(
                bass.AP(dst, off, [[width, 128], [128 * width, nchunk], [1, width]]),
                bass.AP(CASTB[sl], 0, [[1024, 128], [width, nchunk], [1, width]]),
                waits=[ld], slot=sl)
            st_["free"][sl] = [ts]
            state["l1cast"] = ts

        for e_ in range(8):
            for hf in range(4):
                prep_tasks.append(lambda e_=e_, hf=hf: task(
                    h_wg.tensor, s_wg.tensor, e_ * D * 512 + hf * 256 * 512, 2, 512))
                prep_tasks.append(lambda e_=e_, hf=hf: task(
                    h_wu.tensor, s_wu.tensor, e_ * D * 512 + hf * 256 * 512, 2, 512))
            for hf in range(4):
                prep_tasks.append(lambda e_=e_, hf=hf: task(
                    h_wd.tensor, s_wd.tensor, e_ * 512 * D + hf * 128 * D, 1, 1024))
        for src, dst in ((h_wsg.tensor, s_wsg.tensor), (h_wsu.tensor, s_wsu.tensor),
                         (h_wsd.tensor, s_wsd.tensor)):
            for q in range(8):
                prep_tasks.append(lambda src=src, dst=dst, q=q:
                                  task(src, dst, q * 128 * D, 1, 1024))

    def ffn_stream(pref, WS, WT, xn_tok, w1_scr, w1_tok, w2_scr, w2_tok,
                   b1_cols, b1_tok, act_fn, b2_row, b2_tok, out_scale, Y,
                   y_free0=T0, bar=(), b2_cols=None):
        w1d = Dma(p, "sync", pref + "w1", n_slots=2)
        w2d = Dma(p, "scalar", pref + "w2", n_slots=2)
        ws_free = [list(bar), list(bar)]
        wt_free = [list(bar), list(bar)]
        y_free = y_free0
        lastx = state["x"]
        for g in range(NTG):
            y_toks = []
            for io in range(16):
                pump(1)
                slot = io % 2
                ldt = w1d.load(
                    bass.AP(WS[slot], 0, [[2048, 128], [256, NC], [1, 256]]),
                    bass.AP(w1_scr.tensor, io * 256, [[4096, 128], [128 * 4096, NC], [1, 256]]),
                    waits=list(ws_free[slot]) + [w1_tok], slot=slot)
                tok = None
                for isub in range(2):
                    bi = isub % 2
                    mms = [(WS[slot][:, kc * 256 + isub * 128:kc * 256 + isub * 128 + 128],
                            XN[:, T * kc + TG * g:T * kc + TG * g + TG]) for kc in range(NC)]
                    tok = mm_group(bi, mms, waits=[ldt, xn_tok])
                    ic = io * 2 + isub
                    ytk = S_(lambda e, bi=bi, ic=ic: e.activation(
                        out=Y[:, ic * TG:(ic + 1) * TG], in_=PS[bi][0:128, 0:TG],
                        func=act_fn, bias=b1_cols[:, ic:ic + 1]),
                        waits=[tok, b1_tok, y_free])
                    consume(bi, ytk)
                    y_toks.append(ytk)
                ws_free[slot] = [tok]
            y_tok = y_toks[-1]
            first = [True] * 8
            slot_tok = {}
            for ks in range(8):
                pump(1)
                slot = ks % 2
                ldt = w2d.load(
                    bass.AP(WT[slot], 0, [[4096, 128], [1024, 4], [1, 1024]]),
                    bass.AP(w2_scr.tensor, ks * 512 * D, [[D, 128], [128 * D, 4], [1, 1024]]),
                    waits=list(wt_free[slot]) + [w2_tok], slot=slot)
                for nc_ in range(8):
                    w0 = [ldt, y_tok]
                    if first[nc_]:
                        w0 = w0 + [bank_free[nc_]]
                    for ksub in range(4):
                        ic = ks * 4 + ksub
                        is_last_of_ks = (nc_ == 7 and ksub == 3)
                        lastmm = (b2_cols is not None and ks == 7 and ksub == 3)
                        tokm = p.emit(
                            "tensor",
                            lambda e, nc_=nc_, slot=slot, ksub=ksub, ic=ic,
                            st=(first[nc_] and ksub == 0), sp=lastmm:
                            e.matmul(PS[nc_][0:128, 0:TG],
                                     WT[slot][:, ksub * 1024 + nc_ * 128:ksub * 1024 + nc_ * 128 + 128],
                                     Y[:, ic * TG:(ic + 1) * TG], start=st, stop=sp),
                            waits=(w0 if ksub == 0 else ()),
                            inc=("tE" if (is_last_of_ks or lastmm) else None))
                        if is_last_of_ks:
                            slot_tok[ks] = tokm
                        if lastmm:
                            slot_tok[(7, nc_)] = tokm
                    first[nc_] = False
                wt_free[slot] = [slot_tok[ks]]
            lasttok = None
            for nc_ in range(8):
                if b2_cols is None:
                    tok = p.emit("tensor",
                                 lambda e, nc_=nc_: e.matmul(
                                     PS[nc_][0:128, 0:TG],
                                     b2_row[0:1, nc_ * 128:nc_ * 128 + 128],
                                     C["onesrow"][0:1, 0:TG], start=False, stop=True),
                                 waits=([b2_tok] if nc_ == 0 else ()), inc="tE")
                else:
                    tok = slot_tok[(7, nc_)]
                xt = V_(lambda e, nc_=nc_, g=g: e.scalar_tensor_tensor(
                    X[:, T * nc_ + TG * g:T * nc_ + TG * g + TG],
                    PS[nc_][0:128, 0:TG], out_scale,
                    X[:, T * nc_ + TG * g:T * nc_ + TG * g + TG], OP.mult, OP.add),
                    waits=[tok, lastx])
                if b2_cols is not None:
                    xt = V_(lambda e, nc_=nc_, g=g: e.tensor_scalar(
                        X[:, T * nc_ + TG * g:T * nc_ + TG * g + TG],
                        X[:, T * nc_ + TG * g:T * nc_ + TG * g + TG],
                        b2_cols[1][:, nc_:nc_ + 1], None, OP.add),
                        waits=[xt, b2_cols[0]])
                consume(nc_, xt)
                lastx = xt
                lasttok = tok
            y_free = lasttok
        state["x"] = lastx
        return lastx

    def emit_fc_dequants(bar):
        DQ, DQO, SEL8, SCL = sopen(
            ("dqDQ", [128, 4096], F32), ("dqO", [128, 4096], BF16),
            ("dqSEL", [8, 8 * 128], F32), ("dqSCL", [8, 1024], F32))
        dsel = Dma(p, "sync", "dsel_fc")
        selt = dsel.load(SEL8[:], h_sel8)
        t1 = emit_dequant(DQ, DQO, SEL8, SCL, [selt], h_fc1q, h_fc1s, s_fc1, 8, 4096,
                          lambda kc: (kc, 0), bar)
        t2 = emit_dequant(DQ, DQO, SEL8, SCL, [selt], h_fc2q, h_fc2s, s_fc2, 32, 1024,
                          lambda kc: (kc % 8, kc // 8), bar)
        sclose()
        return t1, t2

    def sub_l0ffn(pref, fc1_tok, fc2_tok):
        bar = barrier()
        st_tok = stats(X, state["x"], eps2=1e-5)
        AUX, = sopen((pref + "AUX", [128, NC * T], F32))
        lnx_tok = materialize_lnx(AUX, st_tok, bar=bar)
        xn_tok = materialize_xn(AUX, lnx_tok, R2C, SM["l0g"], SM["l0b"])
        lastx = state["x"]
        for c in range(NC):
            lastx = V_(lambda e, c=c: e.scalar_tensor_tensor(
                X[:, T * c:T * c + T], AUX[:, T * c:T * c + T], 0.5,
                X[:, T * c:T * c + T], OP.mult, OP.add), waits=[lnx_tok, lastx])
        state["x"] = lastx
        sclose()
        bar2 = barrier()
        Y, WSa, WSb, WTa, WTb = sopen(
            (pref + "Y", [128, 32 * TG], BF16),
            (pref + "WSa", [128, 2048], BF16), (pref + "WSb", [128, 2048], BF16),
            (pref + "WTa", [128, 4096], BF16), (pref + "WTb", [128, 4096], BF16))
        ffn_stream(pref, [WSa, WSb], [WTa, WTb], xn_tok, s_fc1, fc1_tok, s_fc2,
                   fc2_tok, SM["fc1b"], CW, AF.Silu, None, T0,
                   0.25, Y, y_free0=state["x"], bar=bar2,
                   b2_cols=(fc2bq_tok, FC2BQ))
        sclose()

    if n_sub >= 1:
        t1, t2 = emit_fc_dequants([])
    if n_sub >= 2:
        GSg = sopen(("gGS0", [128, 1024], F32), ("gGS1", [128, 1024], F32),
                    ("gGS2", [128, 1024], F32), ("gGS3", [128, 1024], F32),
                    ("gOUT", [128, 1024], BF16), ("gSCR", [128, 1024], F32),
                    ("gCB0", [128, 1024], BF16), ("gCB1", [128, 1024], BF16))
        gbar = barrier()
        gen_st = {"gfree": [list(gbar)] * 4, "scr": T0}
        emit_gen_tasks("ca", s_qkv, GSg[0:4], GSg[4], GSg[5], gbar, gen_st,
                       scale_head=(8, 0.125))
        emit_gen_tasks("cp", s_cp, GSg[0:4], GSg[4], GSg[5], gbar, gen_st)
        emit_gen_tasks("cf", s_cf, GSg[0:4], GSg[4], GSg[5], gbar, gen_st)
        emit_gen_tasks("cm", s_cm, GSg[0:4], GSg[4], GSg[5], gbar, gen_st)
        emit_cast_tasks([GSg[6], GSg[7]], gbar)
    if n_sub >= 1:
        sub_l0ffn("f1", t1, t2)


    def materialize_ln_full(SRC, src_tok, gamma, beta, stat_tok):
        last = None
        for g in range(NTG):
            tr, _ = col_to_bcast(stc_col(R1C, g), 4, 5, 4, 0, waits=[stat_tok])
            tm, _ = col_to_bcast(stc_col(MR1C, g), 4, 5, 3, 1, waits=[stat_tok])
            for c in range(NC):
                sl = SCR[:, 512 * (c % 2):512 * (c % 2) + 512]
                t1 = V_(lambda e, c=c, g=g, sl=sl: e.scalar_tensor_tensor(
                    sl, SRC[:, T * c + TG * g:T * c + TG * g + TG], 1.0,
                    PS[4][0:128, 0:TG], OP.mult, OP.mult),
                    waits=[tr, src_tok, state.get("scr_free", T0)]
                          + ([last] if last else []))
                t1 = V_(lambda e, c=c, sl=sl: e.tensor_sub(sl, sl, PS[3][0:128, 0:TG]),
                        waits=[tm, t1])
                t1 = V_(lambda e, c=c, g=g, sl=sl: e.tensor_scalar(
                    XN[:, T * c + TG * g:T * c + TG * g + TG], sl,
                    gamma[:, c:c + 1], beta[:, c:c + 1], OP.mult, OP.add),
                    waits=[t1, CW])
                last = t1
            consume(4, last)
            consume(3, last)
        state["scr_free"] = last
        return last

    def sub_l0attn():
        CH = HEADS // 2
        dLx = Dma(p, "scalar", "dLx")
        bar = barrier()
        st_tok = stats(X, state["x"], eps2=1e-5)
        AUX, = sopen(("a2AUX", [128, NC * T], F32))
        lnx_tok = materialize_lnx(AUX, st_tok, bar=bar)
        h1_tok = materialize_xn(AUX, lnx_tok, R2C, SM["ln1g"], SM["ln1b"])
        slnx = dLx.load(bass.AP(s_lnx.tensor, 0, [[T, 128], [128 * T, NC], [1, T]]),
                        bass.AP(AUX, 0, [[NC * T, 128], [T, NC], [1, T]]),
                        waits=[lnx_tok])
        sclose()
        bar2 = barrier()
        QF, KF, VST0, VST1, VTMh0, VTMh1, PT, QW0, QW1, EPI, NEGBC = sopen(
            ("aQF", [128, NC * T], BF16), ("aKF", [128, NC * T], BF16),
            ("aVST0", [128, 16 * 65], BF16), ("aVST1", [128, 16 * 65], BF16),
            ("aVTMh0", [128, 16 * 65], BF16), ("aVTMh1", [128, 16 * 65], BF16),
            ("aPT", [128, 4 * 512], BF16),
            ("aQW0", [128, 8 * 256], BF16), ("aQW1", [128, 8 * 256], BF16),
            ("aEPI", [128, 512], F32), ("aNEGBC", [128, 4 * 512], BF16))
        QW = [QW0, QW1]
        VTMh = [VTMh0, VTMh1]
        VSTs = [VST0, VST1]
        ones_tok = [V_(lambda e, i=i: e.memset(
            bass.AP(VSTs[i], 64, [[16 * 65, 128], [65, 16], [1, 1]]), 1.0),
            waits=bar2) for i in range(2)]
        vbrow_tok = None
        for grp in range(2):
            wvb = [bank_free[5], gbias["ca"]] + bar2
            tokt = None
            for j in range(4):
                jj = 4 * grp + j
                tokt = p.emit("tensor", lambda e, j=j, jj=jj: e.transpose(
                    PS[5][0:1, 128 * j:128 * (j + 1)], GB["ca"]["bias"][:, 16 + jj:17 + jj],
                    C["ident_f"][:]), waits=(wvb if j == 0 else ()),
                    inc=("tE" if j == 3 else None))
            vbrow_tok = V_(lambda e, grp=grp: e.tensor_copy(
                ROWS[0:1, 1024 + 512 * grp:1536 + 512 * grp], PS[5][0:1, 0:512]),
                waits=[tokt, rows_free[2 + grp]])
            consume(5, vbrow_tok)
            rows_free[2 + grp] = vbrow_tok
        qkvw = Dma(p, "sync", "qkvw", n_slots=2)
        qkv_done = None
        vstore = Dma(p, "scalar", "vst", n_slots=2)
        vst_free = [[ones_tok[0]], [ones_tok[1]]]
        for nsl in range(12):
            slot = nsl % 2
            ldt = qkvw.load(
                bass.AP(QW[slot], 0, [[8 * 256, 128], [256, NC], [1, 256]]),
                bass.AP(s_qkv.tensor, nsl * 256, [[3072, 128], [128 * 3072, NC], [1, 256]]),
                waits=list(state.get("qw%d" % slot, [T0])) + bar2, slot=slot)
            if nsl < 8:
                tok = None
                for nsub in range(2):
                    for g in range(NTG):
                        bi = (nsub * NTG + g) % 2
                        n_global = nsl * 256 + nsub * 128
                        mms = [(QW[slot][:, kc * 256 + nsub * 128:kc * 256 + nsub * 128 + 128],
                                XN[:, T * kc + TG * g:T * kc + TG * g + TG])
                               for kc in range(NC)]
                        tok = mm_group(bi, mms, waits=[ldt, h1_tok])
                        dst = QF if n_global < 1024 else KF
                        colg = n_global if n_global < 1024 else n_global - 1024
                        ch = colg // 128
                        etk = V_(lambda e, bi=bi, dst=dst, ch=ch, g=g,
                                 col=n_global // 128: e.tensor_scalar(
                            dst[:, ch * T + TG * g:ch * T + TG * g + TG],
                            PS[bi][0:128, 0:TG],
                            GB["ca"]["bias"][:, col:col + 1], None, OP.add),
                            waits=[tok, gbias["ca"]])
                        consume(bi, etk)
                qkv_done = tok
            else:
                vsl = nsl - 8
                for tt in range(16):
                    bi = tt % 2
                    w0 = [ldt, h1_tok, bank_free[bi]]
                    tokv = None
                    for kc in range(NC):
                        tokv = p.emit("tensor",
                                      lambda e, kc=kc, tt=tt, slot=slot, bi=bi,
                                      st=(kc == 0): e.matmul(
                                          PS[bi][0:128, 0:256],
                                          XN[:, T * kc + 128 * tt:T * kc + 128 * tt + 128],
                                          QW[slot][:, kc * 256:kc * 256 + 256],
                                          start=st, stop=False),
                                      waits=(w0 if kc == 0 else ()), inc=None)
                    tokv = p.emit("tensor",
                                  lambda e, vsl=vsl, bi=bi: e.matmul(
                                      PS[bi][0:128, 0:256], C["r064f"][0:1, 0:128],
                                      ROWS[0:1, 1024 + 256 * vsl:1024 + 256 * vsl + 256],
                                      start=False, stop=True),
                                  waits=[vbrow_tok], inc="tE")
                    vslot = (tt // 4) % 2
                    qoff = (tt % 4) * 260
                    evk = V_(lambda e, bi=bi, vslot=vslot, qoff=qoff: e.tensor_copy(
                        bass.AP(VSTs[vslot], qoff, [[16 * 65, 128], [65, 4], [1, 64]]),
                        bass.AP(PS[bi], 0, [[512, 128], [64, 4], [1, 64]])),
                        waits=[tokv] + (list(vst_free[vslot]) if tt % 4 == 0 else [state["vevk"]]))
                    state["vevk"] = evk
                    consume(bi, evk)
                    if tt % 4 == 3:
                        tokst = vstore.load(
                            bass.AP(s_v.tensor, (tt - 3) * 128 * 1040 + (4 * vsl) * 65,
                                    [[1040, 128], [128 * 1040, 4], [1, 4 * 65]]),
                            bass.AP(VSTs[vslot], 0, [[16 * 65, 128], [260, 4], [1, 4 * 65]]),
                            waits=[evk], slot=vslot)
                        vst_free[vslot] = [tokst]
                qkv_done = tokv
            state["qw%d" % slot] = [tok if nsl < 8 else tokv]
        vload = Dma(p, "sync", "vload", n_slots=2)
        vput = [T0, T0]
        negbc_free = T0
        o_first = [qkv_done]
        exp_toks = {}
        NEGH = [144, 160]
        p1_tasks = []

        def emit_p1_head(h):
            ch, b_ = h // 2, 64 * (h % 2)
            base = NEGH[h % 2]

            def tile_task(qt, kg):
                width = 512 if kg < qt // 4 else (qt % 4) * 128 + 128
                state["p1ctr"] = state.get("p1ctr", 0) + 1
                bi = (0, 1)[state["p1ctr"] % 2]
                mms = [(QF[b_:b_ + 64, ch * T + 128 * qt:ch * T + 128 * qt + 128],
                        KF[b_:b_ + 64, ch * T + 512 * kg:ch * T + 512 * kg + width],
                        PS[bi][0:128, 0:width])]
                if kg == qt // 4:
                    moff = (qt % 4) * 128
                    mms.append((C["itop"][b_:b_ + 64, :], C["m1a"][b_:b_ + 64, :],
                                PS[bi][0:128, moff:moff + 128]))
                    mms.append((C["ibot"][b_:b_ + 64, :], C["m1b"][b_:b_ + 64, :],
                                PS[bi][0:128, moff:moff + 128]))
                tok = mm_group(bi, mms, waits=[qkv_done])
                if kg == 0:
                    nm_tok = V_(lambda e, bi=bi, qt=qt, width=width: e.reduce_max(
                        STC[:, base + qt % 16:base + qt % 16 + 1],
                        PS[bi][0:128, 0:width], axis=AX.X, negate=True),
                        waits=[tok, state.get("negq_%d" % (h % 2), T0)])
                else:
                    t2 = V_(lambda e, bi=bi, width=width: e.reduce_max(
                        STC[:, 176:177], PS[bi][0:128, 0:width], axis=AX.X,
                        negate=True), waits=[tok, state.get("negq_tmp", T0)])
                    nm_tok = V_(lambda e, qt=qt: e.tensor_tensor(
                        STC[:, base + qt % 16:base + qt % 16 + 1],
                        STC[:, base + qt % 16:base + qt % 16 + 1],
                        STC[:, 176:177], OP.min), waits=[t2])
                    state["negq_tmp"] = nm_tok
                consume(bi, nm_tok)
                state["negq_%d" % (h % 2)] = nm_tok

            for qt in range(16):
                for kg in range(qt // 4 + 1):
                    p1_tasks.append(lambda qt=qt, kg=kg: tile_task(qt, kg))

        def p1_pump(k=1):
            for _ in range(min(k, len(p1_tasks))):
                p1_tasks.pop(0)()

        emit_p1_head(0)
        p1_pump(len(p1_tasks))
        for h in range(HEADS):
            ch, b_ = h // 2, 64 * (h % 2)
            vslot = h % 2
            vld = vload.load(
                bass.AP(VTMh[vslot], 0, [[16 * 65, 128], [65, 16], [1, 65]]),
                bass.AP(s_v.tensor, h * 65, [[1040, 128], [128 * 1040, 16], [1, 65]]),
                waits=[vput[vslot], vst_free[0][0], vst_free[1][0]], slot=vslot)
            p1_pump(len(p1_tasks))
            if h + 1 < HEADS:
                emit_p1_head(h + 1)
            p1_done = state["negq_%d" % (h % 2)]
            pv_tok = None
            negbc_toks = []
            for qg in range(4):
                base = NEGH[h % 2]
                tb, nbt = col_to_bcast(
                    lambda j, base=base, qg=qg: STC[:, base + 4 * qg + j:base + 4 * qg + j + 1],
                    4, 2, 3, 2, waits=[p1_done, negbc_free],
                    out_bf16=NEGBC[:, 512 * qg:512 * qg + 512])
                negbc_toks.append(nbt)
            for qg in range(4):
                negbc_tok = negbc_toks[qg]
                nkc = 4 * qg + 4
                pvb = 6 + (qg % 2)
                for kc in range(nkc):
                    p1_pump(1)
                    off = 128 * (kc - 4 * qg) if kc >= 4 * qg else 0
                    bi = 2 + kc % 4
                    mms = [(KF[b_:b_ + 64, ch * T + 128 * kc:ch * T + 128 * kc + 128],
                            QF[b_:b_ + 64, ch * T + 512 * qg + off:ch * T + 512 * qg + 512],
                            PS[bi][0:128, off:512]),
                           (C["r064b"][b_:b_ + 64, 0:128],
                            NEGBC[b_:b_ + 64, 512 * qg + off:512 * qg + 512],
                            PS[bi][0:128, off:512])]
                    if kc >= 4 * qg:
                        mms.append((C["itop"][b_:b_ + 64, :], C["m2a"][b_:b_ + 64, :],
                                    PS[bi][0:128, off:off + 128]))
                        mms.append((C["ibot"][b_:b_ + 64, :], C["m2b"][b_:b_ + 64, :],
                                    PS[bi][0:128, off:off + 128]))
                    tok = mm_group(bi, mms, waits=[negbc_tok, qkv_done])
                    pslot = kc % 4
                    etk = S_(lambda e, bi=bi, off=off, pslot=pslot: e.activation(
                        out=PT[0:128, 512 * pslot + off:512 * pslot + 512],
                        in_=PS[bi][0:128, off:512], func=AF.Exp),
                        waits=[tok, exp_toks.get(pslot, T0)])
                    consume(bi, etk)
                    pv_w = [etk, vld]
                    if kc == 0:
                        pv_w.append(bank_free[pvb])
                    pv_tok = p.emit("tensor",
                                    lambda e, kc=kc, off=off, pslot=pslot, pvb=pvb,
                                    vslot=vslot,
                                    st=(kc == 0), sp=(kc == nkc - 1): e.matmul(
                                        PS[pvb][0:65, off:512],
                                        VTMh[vslot][:, kc * 65:kc * 65 + 65],
                                        PT[0:128, 512 * pslot + off:512 * pslot + 512],
                                        start=st, stop=sp),
                                    waits=pv_w, inc="tE")
                    exp_toks[pslot] = pv_tok
                rc = V_(lambda e, pvb=pvb: e.reciprocal(EPI[64:65, 0:512],
                                                        PS[pvb][64:65, 0:512]),
                        waits=[pv_tok, state.get("epi_free", T0)])
                tb2 = mm_group(2, [(C["r064f"][64:65, 0:128], EPI[64:65, 0:512],
                                    PS[2][0:128, 0:512])], waits=[rc])
                cpy = V_(lambda e: e.tensor_copy(EPI[:, 0:512], PS[2][0:128, 0:512]),
                         waits=[tb2])
                consume(2, cpy)
                ot = V_(lambda e, b_=b_, ch=ch, qg=qg, pvb=pvb: e.tensor_mul(
                    XN[b_:b_ + 64, ch * T + 512 * qg:ch * T + 512 * qg + 512],
                    PS[pvb][0:64, 0:512], EPI[0:64, 0:512]),
                    waits=[cpy] + o_first)
                o_first = []
                consume(pvb, ot)
                state["epi_free"] = ot
            negbc_free = tok
            vput[vslot] = pv_tok
        o_tok = ("vE", p.cnt["vE"])
        sclose()
        bar3 = barrier()
        AUX, CP = sopen(("a3AUX", [128, NC * T], F32), ("aCP", [128, NC * 1024], BF16))
        rl = dLx.load(bass.AP(AUX, 0, [[NC * T, 128], [T, NC], [1, T]]),
                      bass.AP(s_lnx.tensor, 0, [[T, 128], [128 * T, NC], [1, T]]),
                      waits=[slnx] + bar3)
        cpw = Dma(p, "sync", "cpw")
        cpl = cpw.load(bass.AP(CP, 0, [[NC * 1024, 128], [1024, NC], [1, 1024]]),
                       bass.AP(s_cp.tensor, 0, [[1024, 128], [128 * 1024, NC], [1, 1024]]),
                       waits=[state.get("gout_cp", T0)] + bar3)
        lastx = state["x"]
        hm_tok = None
        for nc_ in range(NC):
            for g in range(NTG):
                bi = (nc_ * NTG + g) % 2
                mms = [(CP[:, kc * 1024 + nc_ * 128:kc * 1024 + nc_ * 128 + 128],
                        XN[:, T * kc + TG * g:T * kc + TG * g + TG])
                       for kc in range(NC)]
                tok = mm_group(bi, mms, waits=[cpl, o_tok])
                hm = V_(lambda e, bi=bi, nc_=nc_, g=g: e.scalar_tensor_tensor(
                    AUX[:, T * nc_ + TG * g:T * nc_ + TG * g + TG],
                    PS[bi][0:128, 0:TG], 1.0,
                    AUX[:, T * nc_ + TG * g:T * nc_ + TG * g + TG], OP.mult, OP.add),
                    waits=[tok, rl])
                consume(bi, hm)
                hm_tok = hm
            hm_tok = V_(lambda e, nc_=nc_: e.tensor_scalar(
                AUX[:, T * nc_:T * nc_ + T], AUX[:, T * nc_:T * nc_ + T],
                GB["cp"]["bias"][:, nc_:nc_ + 1], None, OP.add),
                waits=[hm_tok, gbias["cp"]])
            lastx = V_(lambda e, nc_=nc_: e.tensor_add(
                X[:, T * nc_:T * nc_ + T], X[:, T * nc_:T * nc_ + T],
                AUX[:, T * nc_:T * nc_ + T]), waits=[hm_tok, lastx])
        state["x"] = lastx
        st2 = stats(AUX, hm_tok)
        xn2_tok = materialize_ln_full(AUX, hm_tok, SM["ln2g"], SM["ln2b"], st2)
        sclose()
        bar4 = barrier()
        Y, WSa, WSb, WTa, WTb = sopen(
            ("a4Y", [128, 32 * TG], BF16),
            ("a4WSa", [128, 2048], BF16), ("a4WSb", [128, 2048], BF16),
            ("a4WTa", [128, 4096], BF16), ("a4WTb", [128, 4096], BF16))
        ffn_stream("a4", [WSa, WSb], [WTa, WTb], xn2_tok, s_cf,
                   state.get("gout_cf", T0), s_cm, state.get("gout_cm", T0),
                   GB["cf"]["bias"], gbias["cf"], AF.Gelu_apprx_tanh,
                   None, T0, 1.0, Y, y_free0=xn2_tok, bar=bar4,
                   b2_cols=(gbias["cm"], GB["cm"]["bias"]))
        sclose()

    if n_sub >= 2:
        pump_all()
        sclose()
        sub_l0attn()

    h_sel8b = din("sel8b", [8, 8 * 128], BF16)
    GM1, GA1, GM2, GA2, GW1, GW2, GTM = 144, 160, 176, 192, 208, 224, 240

    def sub_l1ffn(pref):
        bar = barrier()
        st_tok = stats(X, state["x"])
        CFM, SEL8B = sopen((pref + "CFM", [8, 2048], BF16),
                           (pref + "SEL8B", [8, 8 * 128], BF16))
        dsel = Dma(p, "sync", pref + "dsel")
        selt = dsel.load(SEL8B[:], h_sel8b, waits=bar)
        AUX, SCO, IT8, FT8, FT8b = sopen(
            (pref + "AUX", [128, NC * T], F32), (pref + "SCO", [128, 128], F32),
            (pref + "IT8", [128, 8], I32), (pref + "FT8", [128, 8], F32),
            (pref + "FT8b", [128, 8], F32))
        lnx_tok = materialize_lnx(AUX, st_tok, bar=bar)
        xn_tok = None
        for c in range(NC):
            xn_tok = V_(lambda e, c=c: e.tensor_copy(
                XN[:, T * c:T * c + T], AUX[:, T * c:T * c + T]), waits=[lnx_tok])
        glast = None
        for tt in range(16):
            mms = [(AUX[:, T * kc + 128 * tt:T * kc + 128 * tt + 128],
                    SM["gw"][:, kc * 8:kc * 8 + 8],
                    PS[4][0:128, 8 * tt:8 * tt + 8]) for kc in range(NC)]
            glast = mm_group(4, mms, waits=[lnx_tok, CW])
        sg_tok = S_(lambda e: e.activation(out=SCO[:], in_=PS[4][0:128, 0:128],
                                           func=AF.Sigmoid), waits=[glast])
        consume(4, sg_tok)
        a = sg_tok
        for tt in range(16):
            sl = SCO[:, 8 * tt:8 * tt + 8]
            a = V_(lambda e, sl=sl, tt=tt: e.reduce_max(
                STC[:, GM1 + tt:GM1 + tt + 1], sl, axis=AX.X), waits=[a])
            a = V_(lambda e, sl=sl, tt=tt: e.tensor_scalar(
                IT8[:], sl, STC[:, GM1 + tt:GM1 + tt + 1], None, OP.is_equal),
                waits=[a])
            a = V_(lambda e: e.select(FT8[:], IT8[:], C["iota8"][:], C["c999"][:],
                                      add_drain=True), waits=[a])
            a = V_(lambda e, tt=tt: e.tensor_reduce(
                STC[:, GA1 + tt:GA1 + tt + 1], FT8[:], AX.X, OP.min), waits=[a])
            a = V_(lambda e, tt=tt: e.tensor_scalar(
                IT8[:], C["iota8"][:], STC[:, GA1 + tt:GA1 + tt + 1], None,
                OP.is_equal), waits=[a])
            a = V_(lambda e, sl=sl: e.select(FT8[:], IT8[:], C["cneg"][:], sl,
                                             add_drain=True), waits=[a])
            a = V_(lambda e, tt=tt: e.reduce_max(
                STC[:, GM2 + tt:GM2 + tt + 1], FT8[:], axis=AX.X), waits=[a])
            a = V_(lambda e, tt=tt: e.tensor_scalar(
                IT8[:], FT8[:], STC[:, GM2 + tt:GM2 + tt + 1], None, OP.is_equal),
                waits=[a])
            a = V_(lambda e: e.select(FT8[:], IT8[:], C["iota8"][:], C["c999"][:],
                                      add_drain=True), waits=[a])
            a = V_(lambda e, tt=tt: e.tensor_reduce(
                STC[:, GA2 + tt:GA2 + tt + 1], FT8[:], AX.X, OP.min), waits=[a])
        a = V_(lambda e: e.tensor_add(STC[:, GTM:GTM + 16], STC[:, GM1:GM1 + 16],
                                      STC[:, GM2:GM2 + 16]), waits=[a])
        a = V_(lambda e: e.tensor_scalar(STC[:, GTM:GTM + 16], STC[:, GTM:GTM + 16],
                                         1e-20, None, OP.add), waits=[a])
        a = V_(lambda e: e.reciprocal(STC[:, GTM:GTM + 16], STC[:, GTM:GTM + 16]),
               waits=[a])
        a = V_(lambda e: e.scalar_tensor_tensor(
            STC[:, GW1:GW1 + 16], STC[:, GM1:GM1 + 16], 2.5, STC[:, GTM:GTM + 16],
            OP.mult, OP.mult), waits=[a])
        a = V_(lambda e: e.scalar_tensor_tensor(
            STC[:, GW2:GW2 + 16], STC[:, GM2:GM2 + 16], 2.5, STC[:, GTM:GTM + 16],
            OP.mult, OP.mult), waits=[a])
        for tt in range(16):
            a = V_(lambda e, tt=tt: e.tensor_scalar(
                FT8[:], C["iota8"][:], STC[:, GA1 + tt:GA1 + tt + 1], None,
                OP.is_equal), waits=[a])
            a = V_(lambda e, tt=tt: e.tensor_scalar(
                FT8[:], FT8[:], STC[:, GW1 + tt:GW1 + tt + 1], None, OP.mult),
                waits=[a])
            a = V_(lambda e, tt=tt: e.tensor_scalar(
                FT8b[:], C["iota8"][:], STC[:, GA2 + tt:GA2 + tt + 1], None,
                OP.is_equal), waits=[a])
            a = V_(lambda e, tt=tt: e.tensor_scalar(
                FT8b[:], FT8b[:], STC[:, GW2 + tt:GW2 + tt + 1], None, OP.mult),
                waits=[a])
            a = V_(lambda e: e.tensor_add(FT8[:], FT8[:], FT8b[:]), waits=[a])
            tokt = TT_(lambda e: e.transpose(PS[5][0:8, 0:128], FT8[:],
                                             C["ident_f"][:]),
                       waits=[a, bank_free[5]])
            a = V_(lambda e, tt=tt: e.tensor_copy(CFM[0:8, 128 * tt:128 * tt + 128],
                                                  PS[5][0:8, 0:128]), waits=[tokt])
            consume(5, a)
        cfm_tok = a
        sclose()
        bar2 = barrier()
        bufs = sopen(
            (pref + "HMS", [128, 8 * 512], BF16),
            (pref + "HM0", [128, 4 * 512], BF16), (pref + "HM1", [128, 4 * 512], BF16),
            (pref + "HM2", [128, 4 * 512], BF16), (pref + "HM3", [128, 4 * 512], BF16),
            (pref + "HM4", [128, 4 * 512], BF16), (pref + "HM5", [128, 4 * 512], BF16),
            (pref + "HM6", [128, 4 * 512], BF16), (pref + "HM7", [128, 4 * 512], BF16),
            (pref + "SGB", [128, 2 * 512], BF16),
            (pref + "GUW0", [128, 8 * 256], BF16), (pref + "GUW1", [128, 8 * 256], BF16),
            (pref + "GUW2", [128, 8 * 256], BF16), (pref + "GUW3", [128, 8 * 256], BF16),
            (pref + "WD0", [128, 1024], BF16), (pref + "WD1", [128, 1024], BF16))
        HMS = bufs[0]
        HM = bufs[1:9]
        SGB, GUW0, GUW1, GUW2, GUW3, WD0, WD1 = bufs[9:]
        GUW = [GUW0, GUW1, GUW2, GUW3]
        WD = [WD0, WD1]
        guw = Dma(p, "sync", pref + "guw", n_slots=4)
        wdd = Dma(p, "scalar", pref + "wdd", n_slots=2)
        lastx = state["x"]
        guw_free = [list(bar2) for _ in range(4)]
        wd_free = [list(bar2), list(bar2)]
        hm_free = {}
        sg_free = {}
        slot_ctr = [0]
        cast_tok = state.get("l1cast", T0)

        def gu_block(g, gscr, uscr, base_off, NI, HBUF, key, cbc):
            cbc_tok = None
            if cbc is not None:
                cbc_tok = mm_group(4, [(SEL8B[0:8, cbc * 128:cbc * 128 + 128],
                                        CFM[0:8, TG * g:TG * g + TG],
                                        PS[4][0:128, 0:TG])],
                                   waits=[cfm_tok, selt])
            for sl_ in range(NI // 256):
                gslot = slot_ctr[0] % 4
                uslot = (slot_ctr[0] + 1) % 4
                slot_ctr[0] += 2
                ldg = guw.load(
                    bass.AP(GUW[gslot], 0, [[8 * 256, 128], [256, NC], [1, 256]]),
                    bass.AP(gscr, base_off + sl_ * 256,
                            [[NI, 128], [128 * NI, NC], [1, 256]]),
                    waits=list(guw_free[gslot]) + [cast_tok], slot=gslot)
                ldu = guw.load(
                    bass.AP(GUW[uslot], 0, [[8 * 256, 128], [256, NC], [1, 256]]),
                    bass.AP(uscr, base_off + sl_ * 256,
                            [[NI, 128], [128 * NI, NC], [1, 256]]),
                    waits=list(guw_free[uslot]) + [cast_tok], slot=uslot)
                gtoks = []
                for nsub in range(2):
                    ic = sl_ * 2 + nsub
                    bi = ic % 2
                    mms = [(GUW[gslot][:, kc * 256 + nsub * 128:kc * 256 + nsub * 128 + 128],
                            XN[:, T * kc + TG * g:T * kc + TG * g + TG])
                           for kc in range(NC)]
                    gtok = mm_group(bi, mms, waits=[ldg, xn_tok])
                    sgt = S_(lambda e, bi=bi, nsub=nsub: e.activation(
                        out=SGB[:, nsub * 512:nsub * 512 + 512],
                        in_=PS[bi][0:128, 0:TG], func=AF.Silu),
                        waits=[gtok, sg_free.get(nsub, T0)])
                    consume(bi, sgt)
                    sg_free[nsub] = sgt
                    gtoks.append((gtok, sgt))
                for nsub in range(2):
                    ic = sl_ * 2 + nsub
                    bi = 2 + ic % 2
                    mms = [(GUW[uslot][:, kc * 256 + nsub * 128:kc * 256 + nsub * 128 + 128],
                            XN[:, T * kc + TG * g:T * kc + TG * g + TG])
                           for kc in range(NC)]
                    utok = mm_group(bi, mms, waits=[ldu, xn_tok])
                    hmt = V_(lambda e, bi=bi, ic=ic, nsub=nsub, HBUF=HBUF:
                             e.scalar_tensor_tensor(
                                 HBUF[:, ic * 512:ic * 512 + 512],
                                 PS[bi][0:128, 0:TG], 1.0,
                                 SGB[:, nsub * 512:nsub * 512 + 512],
                                 OP.mult, OP.mult),
                             waits=[utok, gtoks[nsub][1],
                                    hm_free.get((key, ic), T0)])
                    if cbc is not None:
                        hmt = V_(lambda e, ic=ic, HBUF=HBUF: e.tensor_mul(
                            HBUF[:, ic * 512:ic * 512 + 512],
                            HBUF[:, ic * 512:ic * 512 + 512],
                            PS[4][0:128, 0:TG]), waits=[hmt, cbc_tok])
                    consume(bi, hmt)
                    hm_free[(key, ic)] = hmt
                    sg_free[nsub] = hmt
                guw_free[gslot] = [gtoks[-1][0]]
                guw_free[uslot] = [utok]
            if cbc is not None:
                consume(4, hm_free[(key, NI // 128 - 1)])

        for g in range(NTG):
            gu_block(g, s_wsg.tensor, s_wsu.tensor, 0, 1024, HMS, "s", None)
            for e_ in range(8):
                gu_block(g, s_wg.tensor, s_wu.tensor, e_ * D * 512, 512,
                         HM[e_], "e%d" % e_, e_)
            dn = [("s", HMS, ic, s_wsd.tensor, ic * 128 * 1024) for ic in range(8)]
            for e_ in range(8):
                dn += [("e%d" % e_, HM[e_], ic, s_wd.tensor,
                        e_ * 512 * 1024 + ic * 128 * 1024) for ic in range(4)]
            z_toks = [None] * 8
            for di, (key, HBUF, ic, wt, woff) in enumerate(dn):
                slot = di % 2
                ldt = wdd.load(bass.AP(WD[slot], 0, [[1024, 128], [1, 1024]]),
                               bass.AP(wt, woff, [[1024, 128], [1, 1024]]),
                               waits=list(wd_free[slot]) + [cast_tok], slot=slot)
                lastmm = None
                for nc_ in range(8):
                    if nc_ == 0:
                        w0 = [ldt, hm_free[(key, ic)]]
                    else:
                        w0 = []
                    if di == 0:
                        w0 = w0 + [bank_free[nc_]]
                    lastmm = p.emit(
                        "tensor",
                        lambda e, nc_=nc_, slot=slot, HBUF=HBUF, ic=ic,
                        st=(di == 0), sp=(di == len(dn) - 1): e.matmul(
                            PS[nc_][0:128, 0:TG],
                            WD[slot][:, nc_ * 128:nc_ * 128 + 128],
                            HBUF[:, ic * 512:ic * 512 + 512], start=st, stop=sp),
                        waits=w0,
                        inc=("tE" if (nc_ == 7 or di == len(dn) - 1) else None))
                    if di == len(dn) - 1:
                        z_toks[nc_] = lastmm
                wd_free[slot] = [lastmm]
            for nc_ in range(8):
                xt = V_(lambda e, nc_=nc_, g=g: e.scalar_tensor_tensor(
                    X[:, T * nc_ + TG * g:T * nc_ + TG * g + TG],
                    PS[nc_][0:128, 0:TG], 0.5,
                    X[:, T * nc_ + TG * g:T * nc_ + TG * g + TG], OP.mult, OP.add),
                    waits=[z_toks[nc_], lastx])
                consume(nc_, xt)
                lastx = xt
            for key, HBUF, ic, wt, woff in dn:
                hm_free[(key, ic)] = z_toks[7]
            for ic in range(2):
                sg_free[ic] = z_toks[7]
        state["x"] = lastx
        sclose()
        sclose()

    def sub_l1attn():
        bar = barrier()
        st_tok = stats(X, state["x"], rms_eps=1e-6)
        AUX, = sopen(("l5AUX", [128, NC * T], F32))
        lnx_tok = materialize_lnx(AUX, st_tok, bar=bar)
        xn_tok = materialize_xn(AUX, lnx_tok, RRC, SM["l1w"], None)
        lastx = state["x"]
        for c in range(NC):
            lastx = V_(lambda e, c=c: e.tensor_add(
                X[:, T * c:T * c + T], X[:, T * c:T * c + T],
                AUX[:, T * c:T * c + T]), waits=[lnx_tok, lastx])
        state["x"] = lastx
        sclose()
        bar2 = barrier()
        FB, OB, GBF, SB_, WSL0, WSL1, WSMB, WSMF = sopen(
            ("l5FB", [128, T], F32), ("l5OB", [128, T], F32),
            ("l5GB", [128, T], F32), ("l5SB", [128, T], F32),
            ("l5W0", [128, 8 * 128], BF16), ("l5W1", [128, 8 * 128], BF16),
            ("l5WSM", [128, 8], BF16), ("l5WSF", [128, 8], F32))
        WSL = [WSL0, WSL1]
        wl = Dma(p, "gpsimd", "l5wl", n_slots=2)
        wl_free = [list(bar2), list(bar2)]
        fb_free = list(bar2)
        ob_free = list(bar2)
        gb_free = list(bar2)
        sb_free = list(bar2)
        wsm_free = T0
        sctr = [0]

        def wload(wh, c):
            slot = sctr[0] % 2
            sctr[0] += 1
            t = wl.load(bass.AP(WSL[slot], 0, [[8 * 128, 128], [128, NC], [1, 128]]),
                        bass.AP(wh.tensor, c * 128, [[D, 128], [128 * D, NC], [1, 128]]),
                        waits=list(wl_free[slot]), slot=slot)
            return slot, t

        for c in range(NC):
            slot, ldt = wload(h_wf, c)
            fts = []
            for g in range(NTG):
                bi = g % 2
                mms = [(WSL[slot][:, kc * 128:kc * 128 + 128],
                        XN[:, T * kc + TG * g:T * kc + TG * g + TG])
                       for kc in range(NC)]
                tok = mm_group(bi, mms, waits=[ldt, xn_tok])
                ft = S_(lambda e, bi=bi, g=g: e.activation(
                    out=FB[:, TG * g:TG * g + TG], in_=PS[bi][0:128, 0:TG],
                    func=AF.Sigmoid), waits=[tok] + (fb_free if g == 0 else []))
                consume(bi, ft)
                fts.append(ft)
            wl_free[slot] = [tok]
            fmax = V_(lambda e: e.tensor_scalar(FB[:], FB[:], 0.9, None, OP.max),
                      waits=[fts[-1]])
            slot, ldt = wload(h_wo, c)
            ots = []
            for g in range(NTG):
                bi = g % 2
                mms = [(WSL[slot][:, kc * 128:kc * 128 + 128],
                        XN[:, T * kc + TG * g:T * kc + TG * g + TG])
                       for kc in range(NC)]
                tok = mm_group(bi, mms, waits=[ldt, xn_tok])
                ot = S_(lambda e, bi=bi, g=g: e.activation(
                    out=OB[:, TG * g:TG * g + TG], in_=PS[bi][0:128, 0:TG],
                    func=AF.Sigmoid), waits=[tok] + (ob_free if g == 0 else []))
                consume(bi, ot)
                ots.append(ot)
            wl_free[slot] = [tok]
            slot, ldt = wload(h_wi, c)
            wst = None
            for kc in range(NC):
                wst = V_(lambda e, kc=kc, slot=slot: e.reduce_sum(
                    WSMF[:, kc:kc + 1], WSL[slot][:, kc * 128:kc * 128 + 128],
                    axis=AX.X), waits=[ldt, wsm_free])
            wst = V_(lambda e: e.tensor_copy(WSMB[:], WSMF[:]), waits=[wst])
            wl_free[slot] = [wst]
            ginit = V_(lambda e: e.tensor_scalar(GBF[:], FB[:], -1.0, 1.0,
                                                 OP.mult, OP.add),
                       waits=[fmax] + gb_free)
            glast = None
            for g in range(NTG):
                tokv = mm_group(6, [(WSMB[:, kc:kc + 1],
                                     XN[:, T * kc + TG * g:T * kc + TG * g + TG],
                                     PS[6][0:1, 0:TG]) for kc in range(NC)],
                                waits=[wst, xn_tok])
                cpv = V_(lambda e: e.tensor_copy(ROWS[0:1, 0:TG], PS[6][0:1, 0:TG]),
                         waits=[tokv, rows_free[0]])
                consume(6, cpv)
                tokb = mm_group(5, [(C["r064f"][0:1, 0:128], ROWS[0:1, 0:TG],
                                     PS[5][0:128, 0:TG])], waits=[cpv])
                rows_free[0] = tokb
                glast = V_(lambda e, g=g: e.tensor_mul(
                    GBF[:, TG * g:TG * g + TG], GBF[:, TG * g:TG * g + TG],
                    PS[5][0:128, 0:TG]), waits=[tokb, ginit])
                consume(5, glast)
            wsm_free = glast
            sc = V_(lambda e: e.tensor_tensor_scan(SB_[:], FB[:], GBF[:], 0.0,
                                                   OP.mult, OP.add),
                    waits=[glast, fmax] + sb_free)
            fb_free = [sc]
            gb_free = [sc]
            yv = V_(lambda e: e.tensor_mul(SB_[:], SB_[:], OB[:]), waits=[sc, ots[-1]])
            ob_free = [yv]
            lastx = V_(lambda e, c=c: e.tensor_add(
                X[:, T * c:T * c + T], X[:, T * c:T * c + T], SB_[:]),
                waits=[yv, lastx])
            sb_free = [lastx]
        state["x"] = lastx
        sclose()

    if n_sub >= 3:
        sub_l0ffn("f3", t1, t2)
    if n_sub >= 4:
        sub_l1ffn("m4")
    if n_sub >= 5:
        sub_l1attn()
    if n_sub >= 6:
        sub_l1ffn("m6")


    st = dX.load(bass.AP(out_h.tensor, 0, [[T, 128], [128 * T, NC], [1, T]]),
                 bass.AP(X, 0, [[NC * T, 128], [T, NC], [1, T]]), waits=[state["x"]],
                 slot=0)
    p.emit("sync", None, waits=[st])

    with nc.Block() as block:
        @block.sync
        def _(e):
            p.replay("sync", e)

        @block.tensor
        def _(e):
            p.replay("tensor", e)

        @block.vector
        def _(e):
            p.replay("vector", e)

        @block.scalar
        def _(e):
            p.replay("scalar", e)

        @block.gpsimd
        def _(e):
            p.replay("gpsimd", e)

    es.close()
    return nc


def kernel(**inputs):
    in_maps = _host_inputs(inputs)
    nc = build()
    if _RUN is not None:
        results = _RUN(nc, in_maps)
    else:
        res = run_bass_kernel_spmd(nc, in_maps, core_ids=list(range(8)))
        results = res.results
    out = np.stack([np.asarray(r["out"]).T for r in results])
    return np.ascontiguousarray(out.astype(np.float32))
